# version 14
# baseline (speedup 1.0000x reference)
import numpy as np
import ml_dtypes
import concourse.bass as bass
import concourse.mybir as mybir
from concourse.bass_utils import run_bass_kernel_spmd

F32 = mybir.dt.float32
BF16 = mybir.dt.bfloat16
AF = mybir.ActivationFunctionType
ALU = mybir.AluOpType

C, T, NT = 2048, 2048, 16
CH, NCH = 128, 16
LN8 = float(np.log(8.0))
EPS = 1e-5
BF = ml_dtypes.bfloat16


class Sem:
    __slots__ = ("h", "n")

    def __init__(self, h):
        self.h, self.n = h, 0


class Pool:
    def __init__(self, tensor, nelem):
        self.t, self.n = tensor, nelem
        self.free = [(0, nelem)]
        self.live = {}
        self.hwm = 0

    def alloc(self, name, nelem, dtype=BF16, parts=128):
        ne = (nelem * (2 if dtype != BF16 else 1) + 15) & ~15
        for i, (o, sz) in enumerate(self.free):
            if sz >= ne:
                if sz == ne:
                    del self.free[i]
                else:
                    self.free[i] = (o + ne, sz - ne)
                self.live[name] = (o, ne, dtype, parts)
                self.hwm = max(self.hwm, max(o2 + s2 for o2, s2, _, _ in self.live.values()))
                return self.view(name)
        raise MemoryError(f"pool overflow: {name} needs {ne}, free={self.free}, "
                          f"live={[(k, v[1]) for k, v in self.live.items()]}")

    def alloc_top(self, name, nelem, dtype=BF16, parts=128):
        ne = (nelem * (2 if dtype != BF16 else 1) + 15) & ~15
        for i in range(len(self.free) - 1, -1, -1):
            o, sz = self.free[i]
            no = ((o + sz - ne) // 16) * 16
            if no >= o:
                if no == o and o + sz == no + ne:
                    del self.free[i]
                else:
                    self.free[i] = (o, no - o)
                    if o + sz > no + ne:
                        self.free.insert(i + 1, (no + ne, o + sz - (no + ne)))
                self.live[name] = (no, ne, dtype, parts)
                self.hwm = max(self.hwm, max(o2 + s2 for o2, s2, _, _ in self.live.values()))
                return self.view(name)
        raise MemoryError(f"pool overflow(top): {name} needs {ne}, free={self.free}")

    def view(self, name):
        o, ne, dtype, parts = self.live[name]
        ap = self.t[0:parts, o:o + ne]
        if dtype != BF16:
            ap = ap.bitcast(dtype)
        return ap

    def freeb(self, *names):
        for name in names:
            o, ne, _, _ = self.live.pop(name)
            self.free.append((o, ne))
        self.free.sort()
        out = []
        for o, sz in self.free:
            if out and out[-1][0] + out[-1][1] == o:
                out[-1] = (out[-1][0], out[-1][1] + sz)
            else:
                out.append((o, sz))
        self.free = out



_WRAP_OPS = {
    "tensor_tensor", "tensor_scalar", "tensor_copy", "tensor_tensor_scan",
    "scalar_tensor_tensor", "reciprocal", "memset", "memzero", "select",
    "activation", "copy", "add", "mul", "square", "sqrt", "tensor_add",
    "tensor_sub", "tensor_mul", "tensor_max", "tensor_scalar_add",
    "tensor_scalar_mul", "tensor_scalar_max", "tensor_single_scalar",
}


class EngWrap:
    def __init__(self, eng, sem):
        self._e, self._sem = eng, sem

    def __getattr__(self, name):
        attr = getattr(self._e, name)
        if name in _WRAP_OPS:
            def call(*a, _attr=attr, **k):
                if self._sem.n:
                    self._e.wait_ge(self._sem.h, self._sem.n)
                ins = _attr(*a, **k)
                ins.then_inc(self._sem.h, 1)
                self._sem.n += 1
                return _SinkInstr()
            return call
        if name == "dma_start":
            def call(*a, _attr=attr, **k):
                if self._sem.n:
                    self._e.wait_ge(self._sem.h, self._sem.n)
                return _attr(*a, **k)
            return call
        return attr

    def milestone(self, sem, k=1):
        ins = self._e.wait_ge(self._sem.h, self._sem.n)
        ins.then_inc(sem.h, k)


class _SinkInstr:

    def then_inc(self, h, k):
        raise RuntimeError("use milestone() instead of then_inc on wrapped ops")


class BlkProxy:
    def __init__(self, blk, B):
        self._blk, self._B = blk, B

    def vector(self, f):
        return self._blk.vector(lambda v: f(EngWrap(v, self._B.sVq)))

    def scalar(self, f):
        return self._blk.scalar(lambda s: f(EngWrap(s, self._B.sSq)))

    def tensor(self, f):
        return self._blk.tensor(f)

    def gpsimd(self, f):
        return self._blk.gpsimd(lambda g: f(EngWrap(g, self._B.sGq)))

    def sync(self, f):
        return self._blk.sync(f)


class KB:
    def __init__(self, taps=(), phases=99):
        self.taps, self.phases = set(taps), phases
        self.nc = bass.Bass("TRN2", target_bir_lowering=False)
        self.ctx = []
        self.tapt = {}
        self.spar = 0

    def declare(self):
        d = self.nc.dram_tensor
        self.i_xm = d("xm", [C, T], F32, kind="ExternalInput").ap()
        self.i_xp = d("xp", [C, T], F32, kind="ExternalInput").ap()
        self.i_sflag = d("sflag", [128, 1], F32, kind="ExternalInput").ap()
        self.i_qw = d("qw_t", [NT, NT, 128, 128], F32, kind="ExternalInput").ap()
        self.i_kw = d("kw_t", [NT, NT, 128, 128], F32, kind="ExternalInput").ap()
        self.i_ow = d("ow_t", [NT, NT, 128, 128], F32, kind="ExternalInput").ap()
        self.i_vw = d("vw_t", [4, NT, 128, 512], F32, kind="ExternalInput").ap()
        self.i_w1 = d("w1", [C, 128], F32, kind="ExternalInput").ap()
        self.i_w2 = d("w2t", [4, 33, C], F32, kind="ExternalInput").ap()
        self.i_td1 = d("td1", [C, 64], F32, kind="ExternalInput").ap()
        self.i_td2 = d("td2t", [65, C], F32, kind="ExternalInput").ap()
        self.i_gut = d("gut", [64, NT, 128, 256], F32, kind="ExternalInput").ap()
        self.i_dwt = d("dwt", [NT, 64, 128, 128], F32, kind="ExternalInput").ap()
        self.i_vecs = d("vecs", [128, 6 * NT], F32, kind="ExternalInput").ap()
        self.i_vbrow = d("vbrow", [128, C], F32, kind="ExternalInput").ap()
        self.i_utm = d("utm", [128, 128], F32, kind="ExternalInput").ap()
        self.i_ident = d("ident", [128, 128], F32, kind="ExternalInput").ap()
        self.i_cmask = d("cmask", [128, T], F32, kind="ExternalInput").ap()
        self.o_out = d("out", [C, T], F32, kind="ExternalOutput").ap()
        self.h_xn = d("s_xn", [C, T + 1], BF16).ap()
        self.h_q = d("s_q", [C, T], BF16).ap()
        self.h_k = d("s_k", [C, T], BF16).ap()
        self.h_w = d("s_w", [C, T], F32).ap()
        self.h_v = d("s_v", [T, C], BF16).ap()
        self.h_s = d("s_s", [C, T], BF16).ap()
        self.h_y = d("s_y", [C, T], BF16).ap()
        self.h_k16 = d("s_k16", [C, T], BF16).ap()
        self.h_q16 = d("s_q16", [C, T], BF16).ap()
        self.h_kst = d("s_kst", [C, T], BF16).ap()
        self.h_btab = d("s_btab", [NT, 128, 128], F32).ap()
        for nm, sh, dt in [("xn", [C, T + 1], BF16), ("q", [C, T], BF16),
                           ("k", [C, T], BF16),
                           ("v", [T, C], BF16), ("s", [C, T], BF16),
                           ("y", [C, T], BF16), ("yln", [128, NT * T], BF16),
                           ("Sst", [128, 1024], F32)]:
            if nm in self.taps:
                self.tapt[nm] = d("tap_" + nm, sh, dt, kind="ExternalOutput").ap()

    def allocate(self):
        nc = self.nc

        def sb(name, shape, dtype):
            g = nc.sbuf_tensor(name, shape, dtype)
            t = g.__enter__()
            self.ctx.append(g)
            return t

        def mkpsum(name, shape, dtype=F32):
            g = nc.psum_tensor(name, shape, dtype)
            t = g.__enter__()
            self.ctx.append(g)
            return t

        def mksem(name):
            g = nc.semaphore(name)
            h = g.__enter__()
            self.ctx.append(g)
            return Sem(h)

        self.vecs = sb("g_vecs", [128, 6 * NT], F32)
        self.sflag = sb("g_sflag", [128, 1], F32)
        self.utm = sb("g_utm", [128, 128], BF16)
        self.ident = sb("g_ident", [128, 128], BF16)
        self.cmask = sb("g_cmask", [128, T], BF16)
        self.ones = sb("g_ones", [128, 1], BF16)
        self.onesrow = sb("g_onesrow", [1, 128], F32)
        self.cbln8 = sb("g_cbln8", [128, 1], F32)
        self.S = sb("g_S", [128, 3 * 1024], F32)
        self.Sbf = sb("g_Sbf", [128, 1024], BF16)
        self.bcol = sb("g_bcol", [128, NT], BF16)
        self.Btab = sb("g_btab", [128, NT * 9], F32)
        self.dtab = sb("g_dtab", [128, NT * 8], F32)
        self.esttab = sb("g_est", [128, NT * 8], BF16)
        self.eintab = sb("g_eint", [128, NT * 8], BF16)
        self.drow = sb("g_drow", [128, NT], F32)
        self.pool_t = sb("g_pool", [128, 95400], BF16)
        self.pool = Pool(self.pool_t, 95400)
        self.PA = mkpsum("p_A", [128, 2048], F32)
        self.PB = mkpsum("p_B", [128, 2048], F32)
        self.sT = mksem("sT")
        self.sV = mksem("sV")
        self.sS = mksem("sS")
        self.sG = mksem("sG")
        self.dsems = {}
        self.sVq = mksem("sVq")
        self.sSq = mksem("sSq")
        self.sGq = mksem("sGq")
        self._mksem = mksem

    def dsem(self, name):
        if name not in self.dsems:
            self.dsems[name] = self._mksem("d_" + name)
        return self.dsems[name]

    def close(self):
        for g in reversed(self.ctx):
            g.__exit__(None, None, None)

    def barrier(self, blk):
        B = self

        @blk.tensor
        def _(t):
            t.drain()
            t.nop(nofuse=True).then_inc(B.sT.h, 1)
        B.sT.n += 1

        @blk.gpsimd
        def _(g):
            g.nop(nofuse=True).then_inc(B.sG.h, 1)
        B.sG.n += 1

        tgt = [(B.sT.h, B.sT.n), (B.sVq.h, B.sVq.n), (B.sSq.h, B.sSq.n),
               (B.sGq.h, B.sGq.n),
               (B.sG.h, B.sG.n)] + [(d.h, d.n) for d in B.dsems.values()]
        for sec in ("tensor", "vector", "scalar", "gpsimd", "sync"):
            def f(e, tgt=tgt):
                for h, n in tgt:
                    e.wait_ge(h, n)
            getattr(blk, sec)(f)


def emit_p0(B, blk):
    cst = B.dsem("cst")

    @blk.gpsimd
    def _(g):
        def dma(out, in_):
            g.dma_start(out=out, in_=in_).then_inc(cst.h, 16)
            cst.n += 16
        dma(B.vecs[:, :], B.i_vecs)
        dma(B.sflag[:, :], B.i_sflag)
        dma(B.utm[:, :], B.i_utm)
        dma(B.ident[:, :], B.i_ident)
        dma(B.cmask[:, :], B.i_cmask)

    @blk.vector
    def _(v):
        v.memset(B.ones[:, :], 1.0)
        v.memset(B.onesrow[:, :], 1.0)
        v.memset(B.cbln8[:, :], -LN8)
        v.memset(B.Btab[:, :], 0.0)
        v.memset(B.S[:, :], 0.0)
        v.memset(B.Sbf[:, :], 0.0)

    B.barrier(blk)


def emit_w12_prefetch(B, blk):
    P = B.pool
    w1sb = P.alloc("w1sb", NT * 128, BF16)
    w2sb = P.alloc("w2sb", 4 * C, BF16, parts=33)
    wpre = B.dsem("wpre")

    @blk.gpsimd
    def _(g, wpre=wpre, pv=wpre.n):
        g.wait_ge(wpre.h, pv)
        g.dma_start(out=w2sb, in_=B.i_w2.rearrange("r p c -> p r c")).then_inc(wpre.h, 16)
        g.dma_start(out=w1sb, in_=B.i_w1.rearrange("(t p) m -> p t m", p=128)).then_inc(wpre.h, 16)
    wpre.n += 32
    return wpre.n


def emit_ln(B, blk, xin, prefix):
    xin_t = xin.rearrange("(t p) n -> p t n", p=128)
    P = B.pool
    xnres = P.alloc("xnres", NT * (T + 1), BF16)
    kvx = P.alloc("kvx", 2 * T, F32)
    sqb = P.alloc("sqb", 2 * T, BF16)
    srow = P.alloc("srow", 3 * T, F32)
    abrow = P.alloc("abrow", 2 * T, F32)
    bcast = P.alloc("bcast", 2 * T, F32)
    scrf = P.alloc("scrf", T, F32)

    def xo(k):
        return xnres[:, k * (T + 1):(k + 1) * (T + 1)]

    sS_done = [None, None]
    sT_done = [None, None]
    for k in range(NT):
        p = k % 2
        kvs = B.dsem(f"kvx{p}")

        @blk.sync
        def _(sy, k=k, p=p, kvs=kvs, pv=kvs.n, wt=sS_done[p]):
            sy.wait_ge(kvs.h, pv)
            if wt is not None:
                sy.wait_ge(B.sSq.h, wt)
            sy.dma_start(out=kvx[:, p * T:(p + 1) * T], in_=xin_t[:, k, :]) \
                .then_inc(kvs.h, 16)
        kvs.n += 16
        dx_t = kvs.n

        @blk.scalar
        def _(s, k=k, p=p, kvs=kvs, dx_t=dx_t, wt=sT_done[p]):
            s.wait_ge(kvs.h, dx_t)
            if wt is not None:
                s.wait_ge(B.sT.h, wt)
            s.copy(xo(k)[:, 1:T + 1], kvx[:, p * T:(p + 1) * T])
            s.activation(sqb[:, p * T:(p + 1) * T], xo(k)[:, 1:T + 1],
                         AF.Square)
        sS_done[p] = B.sSq.n
        sv = B.sSq.n

        @blk.tensor
        def _(t, k=k, p=p, sv=sv):
            t.wait_ge(B.sSq.h, sv)
            base = k * (T + 1) + 1
            for n in range(4):
                last = t.matmul(B.PA[0:1, n * 512:(n + 1) * 512], B.ones[:, :],
                         xnres[:, base + n * 512:base + (n + 1) * 512],
                         start=(k == 0), stop=(k == NT - 1))
                last = t.matmul(B.PB[0:1, n * 512:(n + 1) * 512], B.ones[:, :],
                         sqb[:, p * T + n * 512:p * T + (n + 1) * 512],
                         start=(k == 0), stop=(k == NT - 1))
            t.drain()
            t.nop(nofuse=True).then_inc(B.sT.h, 1)
        B.sT.n += 1
        sT_done[p] = B.sT.n

    emit_stats_tail(B, blk, srow, abrow, bcast)

    bc_v = B.sVq.n
    for k in range(NT):
        p = k % 2
        xsp = B.dsem(f"xnsp{p}")

        @blk.vector
        def _(v, k=k, bc_v=bc_v):
            v.wait_ge(B.sVq.h, bc_v)
            xk = xo(k)
            v.tensor_tensor(scrf[:, :], xk[:, 1:T + 1], bcast[:, 0:T], ALU.mult)
            v.tensor_tensor(xk[:, 1:T + 1], scrf[:, :], bcast[:, T:2 * T],
                            ALU.add)
            if prefix:
                v.memset(xk[:, 0:1], 0.0)
                v.tensor_copy(B.bcol[:, k:k + 1], xk[:, T:T + 1])
            else:
                v.tensor_scalar(xk[:, 0:1], B.bcol[:, k:k + 1], B.sflag[:, 0:1],
                                None, ALU.mult)
        vv2 = B.sVq.n

        if not prefix:
            @blk.scalar
            def _(s, k=k, p=p, xsp=xsp, vv2=vv2):
                s.wait_ge(B.sVq.h, vv2)
                s.dma_start(out=B.h_xn.rearrange("(t p) n -> p t n", p=128)[:, k, :],
                            in_=xo(k)).then_inc(xsp.h, 16)
            xsp.n += 16

    B.barrier(blk)
    P.freeb("kvx", "sqb", "srow", "abrow", "bcast", "scrf")


def emit_stats_tail(B, blk, srow, abrow, bcast):
    tv = B.sT.n

    @blk.vector
    def _(v, tv=tv):
        v.wait_ge(B.sT.h, tv)
        sx, e2, mu = srow[0:1, 0:T], srow[0:1, T:2 * T], srow[0:1, 2 * T:3 * T]
        v.tensor_copy(sx, B.PA[0:1, :])
        v.tensor_copy(e2, B.PB[0:1, :])
        v.tensor_scalar(mu, sx, 1.0 / C, None, ALU.mult)
        v.tensor_scalar(e2, e2, 1.0 / C, None, ALU.mult)
        v.tensor_tensor(sx, mu, mu, ALU.mult)
        v.tensor_tensor(e2, e2, sx, ALU.subtract)
        v.tensor_scalar(e2, e2, EPS, None, ALU.add)
    vv = B.sVq.n

    @blk.scalar
    def _(s, vv=vv):
        s.wait_ge(B.sVq.h, vv)
        s.activation(srow[0:1, 0:T], srow[0:1, T:2 * T], AF.Ln)
        s.activation(abrow[0:1, 0:T], srow[0:1, 0:T], AF.Exp, scale=-0.5)
    sv = B.sSq.n

    @blk.vector
    def _(v, sv=sv):
        v.wait_ge(B.sSq.h, sv)
        v.tensor_tensor(abrow[0:1, T:2 * T], srow[0:1, 2 * T:3 * T],
                        abrow[0:1, 0:T], ALU.mult)
        v.tensor_scalar(abrow[0:1, T:2 * T], abrow[0:1, T:2 * T], -1.0, None,
                        ALU.mult)
    vv = B.sVq.n

    @blk.tensor
    def _(t, vv=vv):
        t.wait_ge(B.sVq.h, vv)
        for n in range(4):
            last = t.matmul(B.PA[:, n * 512:(n + 1) * 512], B.onesrow[:, :],
                     abrow[0:1, n * 512:(n + 1) * 512], start=True, stop=True)
            last = t.matmul(B.PB[:, n * 512:(n + 1) * 512], B.onesrow[:, :],
                     abrow[0:1, T + n * 512:T + (n + 1) * 512], start=True, stop=True)
        t.drain()
        t.nop(nofuse=True).then_inc(B.sT.h, 1)
    B.sT.n += 1
    tv = B.sT.n

    @blk.vector
    def _(v, tv=tv):
        v.wait_ge(B.sT.h, tv)
        v.tensor_copy(bcast[:, 0:T], B.PA[:, :])
        v.tensor_copy(bcast[:, T:2 * T], B.PB[:, :])


def emit_mm(B, blk, w_dma):
    P = B.pool
    xnres = P.view("xnres")
    w1sb = P.view("w1sb")
    td1sb = P.alloc("td1sb", NT * 64, BF16)
    td2sb = P.alloc("td2sb", C, BF16, parts=65)
    mmb = P.alloc("mmb", T, BF16)
    scrA = P.alloc("scrA", T, BF16)
    scrB = P.alloc("scrB", 2 * T, BF16)
    for r in range(4):
        P.alloc(f"mmr{r}", T, BF16, parts=33)

    wpre = B.dsem("wpre")

    @blk.gpsimd
    def _(g, wpre=wpre, pv=wpre.n):
        g.wait_ge(wpre.h, pv)
        g.dma_start(out=td1sb, in_=B.i_td1.rearrange("(t p) m -> p t m", p=128)).then_inc(wpre.h, 16)
        g.dma_start(out=td2sb, in_=B.i_td2).then_inc(wpre.h, 16)
    wpre.n += 32
    td_dma = wpre.n

    sT_done = [None, None]
    for k in range(NT):
        p = k % 2
        xo = xnres[:, k * (T + 1):(k + 1) * (T + 1)]

        @blk.vector
        def _(v, k=k, p=p, xo=xo, wt=sT_done[p]):
            if wt is not None:
                v.wait_ge(B.sT.h, wt)
            v.tensor_tensor(scrA[:, :], xo[:, 0:T], xo[:, 1:T + 1], ALU.subtract)
            v.scalar_tensor_tensor(scrB[:, p * T:(p + 1) * T], scrA[:, :],
                                   B.vecs[:, 5 * NT + k:5 * NT + k + 1],
                                   xo[:, 1:T + 1], ALU.mult, ALU.add)
        vv = B.sVq.n

        @blk.tensor
        def _(t, k=k, p=p, vv=vv, wpre=wpre, w_dma=w_dma):
            t.wait_ge(B.sVq.h, vv)
            if k == 0:
                t.wait_ge(wpre.h, w_dma)
            for n in range(4):
                last = t.matmul(B.PA[:, n * 512:(n + 1) * 512],
                         w1sb[:, k * 128:(k + 1) * 128],
                         scrB[:, p * T + n * 512:p * T + (n + 1) * 512],
                         start=(k == 0), stop=(k == NT - 1))
            t.drain()
            t.nop(nofuse=True).then_inc(B.sT.h, 1)
        B.sT.n += 1
        sT_done[p] = B.sT.n

    tv = B.sT.n

    @blk.scalar
    def _(s, tv=tv):
        s.wait_ge(B.sT.h, tv)
        s.activation(mmb[:, :], B.PA[:, :], AF.Tanh)
    sv = B.sSq.n

    mms = B.dsem("mmr")

    @blk.gpsimd
    def _(g, sv=sv, mms=mms, pv=mms.n):
        g.wait_ge(mms.h, pv)
        g.wait_ge(B.sSq.h, sv)
        for r in range(4):
            g.dma_start(out=B.pool.view(f"mmr{r}")[0:32, :],
                        in_=mmb[32 * r:32 * (r + 1), :]).then_inc(mms.h, 16)
    mms.n += 4 * 16
    gv = mms.n

    @blk.vector
    def _(v, gv=gv, mms=mms):
        v.wait_ge(mms.h, gv)
        for r in range(4):
            v.memset(B.pool.view(f"mmr{r}")[32:33, :], 1.0)

    B.barrier(blk)
    P.freeb("mmb", "w1sb", "scrA", "scrB")
    return td_dma


def emit_mixer_r(B, blk, r):
    P = B.pool
    xnres = P.view("xnres")
    xrb = P.view("xrb")
    mv = P.view(f"mmr{r}")
    w2v = P.view("w2sb")
    scrA = P.alloc("mxA", T, BF16)
    mscr = P.alloc("mxB", 2 * T, BF16)
    scrC = P.alloc("mxC", T, BF16)
    sS_par = [None, None]
    sV_par = [None, None]
    for c in range(NT):
        p = c % 2
        PP = B.PA if p == 0 else B.PB
        xo = xnres[:, c * (T + 1):(c + 1) * (T + 1)]

        @blk.tensor
        def _(t, c=c, PP=PP, wts=sS_par[p]):
            if wts is not None:
                t.wait_ge(B.sSq.h, wts)
            for n in range(4):
                last = t.matmul(PP[:, n * 512:(n + 1) * 512],
                         w2v[:, r * C + c * 128:r * C + (c + 1) * 128],
                         mv[:, n * 512:(n + 1) * 512], start=True, stop=True)
            last.then_inc(B.sT.h, 1)
        B.sT.n += 1
        tv = B.sT.n

        @blk.scalar
        def _(s, p=p, PP=PP, tv=tv, wv=sV_par[p]):
            s.wait_ge(B.sT.h, tv)
            if wv is not None:
                s.wait_ge(B.sVq.h, wv)
            s.copy(mscr[:, p * T:(p + 1) * T], PP[:, :])
        sS_par[p] = B.sSq.n
        sv = B.sSq.n

        @blk.vector
        def _(v, c=c, p=p, xo=xo, sv=sv):
            v.wait_ge(B.sSq.h, sv)
            v.tensor_tensor(scrA[:, :], xo[:, 0:T], xo[:, 1:T + 1], ALU.subtract)
            v.tensor_tensor(scrC[:, :], scrA[:, :], mscr[:, p * T:(p + 1) * T],
                            ALU.mult)
            v.tensor_tensor(xrb[:, c * T:(c + 1) * T], scrC[:, :], xo[:, 1:T + 1],
                            ALU.add)
        sV_par[p] = B.sVq.n

    B.barrier(blk)
    P.freeb("mxA", "mxB", "mxC")


def bigmm_prefetch(B, blk, w_in):
    P = B.pool
    wslab = P.alloc("wslab", 2 * NT * 128, BF16)
    pre_t = {}
    for m in (0, 1):
        wss = B.dsem(f"wsl{m % 2}")

        @blk.gpsimd
        def _(g, m=m, wss=wss, pv=wss.n):
            g.wait_ge(wss.h, pv)
            g.dma_start(out=wslab[:, (m % 2) * NT * 128:((m % 2) + 1) * NT * 128],
                        in_=w_in[m].rearrange("k p c -> p k c")).then_inc(wss.h, 16)
        wss.n += 16
        pre_t[m] = wss.n
    return (wslab, 2, pre_t)


def emit_bigmm_cm(B, blk, w_in, src_sb, spill_to, bias_col, pre=None,
                  tail_cb=None):
    P = B.pool
    if pre is None:
        wslab = P.alloc("wslab", 2 * NT * 128, BF16)
        npre, pre_t = 0, {}
    else:
        wslab, npre, pre_t = pre
    evb = P.alloc("evb", T, BF16)
    sT_done = [None, None]
    sS_done = [None, None]
    spill_last = None
    for m in range(NT):
        p = m % 2
        PP = B.PA if p == 0 else B.PB

        wss = B.dsem(f"wsl{p}")
        if m >= npre:
            @blk.gpsimd
            def _(g, m=m, p=p, wss=wss, pv=wss.n, wt=sT_done[p]):
                g.wait_ge(wss.h, pv)
                if wt is not None:
                    g.wait_ge(B.sT.h, wt)
                g.dma_start(out=wslab[:, p * NT * 128:(p + 1) * NT * 128],
                            in_=w_in[m].rearrange("k p c -> p k c")).then_inc(wss.h, 16)
            wss.n += 16
            dw_t = wss.n
        else:
            dw_t = pre_t[m]

        @blk.tensor
        def _(t, m=m, p=p, PP=PP, wss=wss, dw_t=dw_t, wt=sS_done[p]):
            t.wait_ge(wss.h, dw_t)
            if wt is not None:
                t.wait_ge(B.sSq.h, wt)
            for kk in range(NT):
                for n in range(4):
                    last = t.matmul(PP[:, n * 512:(n + 1) * 512],
                             wslab[:, p * NT * 128 + kk * 128:p * NT * 128 + (kk + 1) * 128],
                             src_sb[:, kk * T + n * 512:kk * T + (n + 1) * 512],
                             start=(kk == 0), stop=(kk == NT - 1))
            last.then_inc(B.sT.h, 1)
        B.sT.n += 1
        sT_done[p] = B.sT.n
        tv = B.sT.n

        sps = B.dsem(f"spl{p}")

        @blk.scalar
        def _(s, m=m, PP=PP, sps=sps, tv=tv, wsp=spill_last):
            s.wait_ge(B.sT.h, tv)
            if wsp is not None:
                s.wait_ge(wsp[0].h, wsp[1])
            if bias_col is not None:
                s.activation(evb[:, :], PP[:, :], AF.Identity,
                             bias=B.vecs[:, bias_col * NT + m:bias_col * NT + m + 1])
            else:
                s.copy(evb[:, :], PP[:, :])
            s.dma_start(out=spill_to.rearrange("(t p) n -> p t n", p=128)[:, m, :],
                        in_=evb[:, :]).then_inc(sps.h, 16)
        sps.n += 16
        sS_done[p] = B.sSq.n
        spill_last = (sps, sps.n)

    if tail_cb is not None:
        tail_cb()
    B.barrier(blk)
    P.freeb("wslab", "evb")


def emit_vmm_tm(B, blk, xrb):
    P = B.pool
    vwslab = P.alloc("vwslab", 2 * NT * 512, BF16)
    vbr = P.alloc("vbr", C, F32)
    evb = P.alloc("evb", 2 * 512, BF16)

    wpre = B.dsem("wpre")

    @blk.gpsimd
    def _(g, wpre=wpre):
        g.dma_start(out=vbr, in_=B.i_vbrow).then_inc(wpre.h, 16)
    wpre.n += 16
    vbr_t = wpre.n

    oc_done = [None, None]
    sV_done = [None, None]
    sp_done = [None, None]
    for oc in range(4):
        po = oc % 2

        wss = B.dsem(f"wsl{po}")

        @blk.gpsimd
        def _(g, oc=oc, po=po, wss=wss, pv=wss.n, wt=oc_done[po]):
            g.wait_ge(wss.h, pv)
            if wt is not None:
                g.wait_ge(B.sT.h, wt)
            g.dma_start(out=vwslab[:, po * NT * 512:(po + 1) * NT * 512],
                        in_=B.i_vw[oc].rearrange("k p c -> p k c")).then_inc(wss.h, 16)
        wss.n += 16
        dw_t = wss.n

        for tt in range(NT):
            pt = tt % 2
            PP = B.PA if pt == 0 else B.PB

            @blk.tensor
            def _(t, po=po, tt=tt, PP=PP, wss=B.dsem(f"wsl{po}"), dw_t=dw_t,
                  vbr_t=vbr_t, wpre=wpre, wt=sV_done[pt]):
                t.wait_ge(wss.h, dw_t)
                if wt is not None:
                    t.wait_ge(B.sVq.h, wt)
                for kk in range(NT):
                    last = t.matmul(PP[:, 0:512],
                             xrb[:, kk * T + tt * 128:kk * T + (tt + 1) * 128],
                             vwslab[:, po * NT * 512 + kk * 512:po * NT * 512 + (kk + 1) * 512],
                             start=(kk == 0), stop=(kk == NT - 1))
                last.then_inc(B.sT.h, 1)
            B.sT.n += 1
            tv = B.sT.n

            @blk.vector
            def _(v, oc=oc, pt=pt, PP=PP, tv=tv, sps=B.dsem(f"spl{pt}"),
                  wpre=wpre, vbr_t=vbr_t, wt=sp_done[pt]):
                v.wait_ge(B.sT.h, tv)
                v.wait_ge(wpre.h, vbr_t)
                if wt is not None:
                    v.wait_ge(sps.h, wt)
                v.tensor_tensor(evb[:, pt * 512:(pt + 1) * 512], PP[:, 0:512],
                                vbr[:, oc * 512:(oc + 1) * 512], ALU.add)
            sV_done[pt] = B.sVq.n
            vv = B.sVq.n

            sps = B.dsem(f"spl{pt}")

            @blk.scalar
            def _(s, oc=oc, tt=tt, pt=pt, sps=sps, vv=vv):
                s.wait_ge(B.sVq.h, vv)
                s.dma_start(out=B.h_v[tt * 128:(tt + 1) * 128, oc * 512:(oc + 1) * 512],
                            in_=evb[:, pt * 512:(pt + 1) * 512]).then_inc(sps.h, 16)
            sps.n += 16
            sp_done[pt] = sps.n
        oc_done[po] = B.sT.n

    B.barrier(blk)
    P.freeb("vwslab", "vbr", "evb")


def emit_wpath_head(B, blk, td_dma):
    P = B.pool
    xrb = P.view("xrb")
    td1sb = P.view("td1sb")
    ubuf = P.alloc("ubuf", T, BF16, parts=65)
    wpre = B.dsem("wpre")

    @blk.vector
    def _(v):
        v.memset(ubuf[64:65, :], 1.0)
    u1 = B.sVq.n

    @blk.tensor
    def _(t, u1=u1, wpre=wpre, td_dma=td_dma):
        t.wait_ge(B.sVq.h, u1)
        t.wait_ge(wpre.h, td_dma)
        for kk in range(NT):
            for n in range(4):
                last = t.matmul(B.PA[0:64, n * 512:(n + 1) * 512],
                         td1sb[:, kk * 64:(kk + 1) * 64],
                         xrb[:, kk * T + n * 512:kk * T + (n + 1) * 512],
                         start=(kk == 0), stop=(kk == NT - 1))
        last.then_inc(B.sT.h, 1)
    B.sT.n += 1
    tv_td1 = B.sT.n

    @blk.scalar
    def _(s, tv=tv_td1):
        s.wait_ge(B.sT.h, tv)
        s.activation(ubuf[0:64, :], B.PA[0:64, :], AF.Tanh)

    B.barrier(blk)
    P.freeb("td1sb")


def emit_wpath_tail(B, blk, prefix):
    P = B.pool
    ubuf = P.view("ubuf")
    td2sb = P.view("td2sb")
    scrF = P.alloc("scrF", 2 * T, F32)
    evf = P.alloc("evf", 2 * T, F32)
    bfr = P.alloc("bfr", 2 * T, F32)
    fek = P.alloc("fek", 2 * T, BF16)
    fonem = P.alloc("fonem", T, BF16)
    btr = P.alloc("btr", 4 * 128, F32)
    krow = P.alloc("krow", 4 * T, BF16)
    few = P.alloc("few", 2 * T, F32)
    if not prefix:
        fqs = P.alloc("fqs", 2 * T, F32)
        feq = P.alloc("feq", 2 * T, BF16)
        btrp = P.alloc("btrp", 128, F32)
        qrow = P.alloc("qrow", 4 * T, BF16)

        @blk.vector
        def _(v):
            v.memset(btrp[:, :], 0.0)

    k_t = B.h_k.rearrange("(t p) n -> p t n", p=128)
    q_t = B.h_q.rearrange("(t p) n -> p t n", p=128)
    ko_t = (B.h_kst if prefix else B.h_k16).rearrange("(t p) n -> p t n", p=128)
    q16_t = B.h_q16.rearrange("(t p) n -> p t n", p=128)

    def sl(buf, p):
        return buf[:, p * T:(p + 1) * T]

    sXd = [None, None]
    sS_done = [None, None]
    vAd = [None, None]
    fsp_done = [None, None, None, None]
    ld_tgt = {}
    va_c = {}

    def sl4(buf, q):
        return buf[:, q * T:(q + 1) * T]

    def emit_tail(c):
        p = c % 2
        q = c % 4
        krs = B.dsem(f"krw{q}")
        fsps = B.dsem(f"fsp{q}")

        @blk.scalar
        def _(s, p=p, va=va_c[c]):
            s.wait_ge(B.sVq.h, va)
            s.activation(sl(few, p), sl(evf, p), AF.Exp)
            s.activation(sl(fek, p), sl(bfr, p), AF.Exp)
            if not prefix:
                s.activation(sl(feq, p), sl(fqs, p), AF.Exp, bias=B.cbln8[:, 0:1])
        sS_done[p] = B.sSq.n
        sz = B.sSq.n

        @blk.vector
        def _(v, p=p, q=q, sz=sz, krs=krs, lt=ld_tgt[c]):
            v.wait_ge(B.sSq.h, sz)
            v.wait_ge(krs.h, lt)
            v.tensor_scalar(fonem[:, :], sl(few, p), -1.0, 1.0, ALU.mult, ALU.add)
            v.tensor_tensor(fonem[:, :], fonem[:, :], sl(fek, p), ALU.mult)
            v.tensor_tensor(sl4(krow, q), sl4(krow, q), fonem[:, :], ALU.mult)
            if not prefix:
                v.tensor_tensor(sl4(qrow, q), sl4(qrow, q), sl(feq, p), ALU.mult)
        vc_ = B.sVq.n

        @blk.gpsimd
        def _(g, c=c, q=q, fsps=fsps, vc_=vc_, pv=fsps.n):
            g.wait_ge(fsps.h, pv)
            g.wait_ge(B.sVq.h, vc_)
            g.dma_start(out=ko_t[:, c, :], in_=sl4(krow, q)).then_inc(fsps.h, 16)
            g.dma_start(out=B.h_btab[c],
                        in_=btr[:, q * 128:(q + 1) * 128]).then_inc(fsps.h, 16)
            if not prefix:
                g.dma_start(out=q16_t[:, c, :], in_=sl4(qrow, q)).then_inc(fsps.h, 16)
        fsps.n += 16 * (2 if prefix else 3)
        fsp_done[q] = fsps.n

    for c in range(NT):
        p = c % 2
        q = c % 4
        PP = B.PA if p == 0 else B.PB
        krs = B.dsem(f"krw{q}")
        fsps = B.dsem(f"fsp{q}")

        @blk.sync
        def _(sy, c=c, q=q, krs=krs, pv=krs.n, fsps=fsps, wt=fsp_done[q]):
            sy.wait_ge(krs.h, pv)
            if wt is not None:
                sy.wait_ge(fsps.h, wt)
            sy.dma_start(out=sl4(krow, q), in_=k_t[:, c, :]).then_inc(krs.h, 16)
            if not prefix:
                sy.dma_start(out=sl4(qrow, q), in_=q_t[:, c, :]).then_inc(krs.h, 16)
        krs.n += 16 * (1 if prefix else 2)
        ld_tgt[c] = krs.n

        @blk.tensor
        def _(t, c=c, PP=PP, wt=sXd[p]):
            if wt is not None:
                t.wait_ge(B.sSq.h, wt)
            for n in range(4):
                last = t.matmul(PP[:, n * 512:(n + 1) * 512],
                         td2sb[:, c * 128:(c + 1) * 128],
                         ubuf[:, n * 512:(n + 1) * 512], start=True, stop=True)
            last.then_inc(B.sT.h, 1)
        B.sT.n += 1
        tv = B.sT.n

        @blk.scalar
        def _(s, p=p, PP=PP, tv=tv, wt=vAd[p]):
            s.wait_ge(B.sT.h, tv)
            if wt is not None:
                s.wait_ge(B.sVq.h, wt)
            s.activation(scrF[:, p * T:(p + 1) * T], PP[:, :], AF.Exp)
        sXd[p] = B.sSq.n
        sx = B.sSq.n

        @blk.vector
        def _(v, c=c, p=p, q=q, sx=sx, ws=sS_done[p], fsps=fsps, wg=fsp_done[q]):
            v.wait_ge(B.sSq.h, sx)
            if ws is not None:
                v.wait_ge(B.sSq.h, ws)
            if wg is not None:
                v.wait_ge(fsps.h, wg)
            v.tensor_scalar(sl(evf, p), scrF[:, p * T:(p + 1) * T], -1.0, -5.0,
                            ALU.mult, ALU.max)
            v.tensor_tensor_scan(sl(bfr, p), B.cmask[:, :], sl(evf, p), 0.0,
                                 ALU.mult, ALU.add)
            v.tensor_copy(btr[:, q * 128:(q + 1) * 128].rearrange(
                              "p (a b) -> p a b", b=1),
                          sl(bfr, p).rearrange("p (a b) -> p a b", b=16)[:, :, 15:16])
            b4 = sl(bfr, p).rearrange("p (k a b) -> p k a b", a=8, b=16)
            if prefix:
                bt8 = btr[:, q * 128:(q + 1) * 128] \
                    .rearrange("p (k a) -> p k a", a=8)[:, :, 7:8] \
                    .unsqueeze(-1).broadcast_to((128, NT, 8, 16))
                v.tensor_tensor(b4, bt8, b4, ALU.subtract)
            else:
                bt3 = btr[:, q * 128:(q + 1) * 128].rearrange("p (k a) -> p k a", a=8)
                bp3 = btrp[:, :].rearrange("p (k a) -> p k a", a=8)
                v.tensor_copy(bp3[:, :, 1:8], bt3[:, :, 0:7])
                bp4 = btrp[:, :] \
                    .rearrange("p (k a) -> p k a", a=8) \
                    .unsqueeze(-1).broadcast_to((128, NT, 8, 16))
                v.tensor_tensor(sl(fqs, p).rearrange("p (k a b) -> p k a b",
                                                     a=8, b=16),
                                b4, bp4, ALU.subtract)
                bt4 = btr[:, q * 128:(q + 1) * 128] \
                    .rearrange("p (k a) -> p k a", a=8) \
                    .unsqueeze(-1).broadcast_to((128, NT, 8, 16))
                v.tensor_tensor(b4, bt4, b4, ALU.subtract)
        vAd[p] = B.sVq.n
        va_c[c] = B.sVq.n

        if c > 0:
            emit_tail(c - 1)
    emit_tail(NT - 1)

    B.barrier(blk)
    names = ["ubuf", "td2sb", "scrF", "evf", "bfr", "few", "fek", "fonem",
             "btr", "krow"]
    if not prefix:
        names += ["fqs", "feq", "btrp", "qrow"]
    P.freeb(*names)


def emit_scan(B, blk, prefix):
    P = B.pool
    ko_t = (B.h_kst if prefix else B.h_k16).rearrange("(t p) n -> p t n", p=128)
    q16_t = B.h_q16.rearrange("(t p) n -> p t n", p=128)
    y_t = B.h_y.rearrange("(t p) n -> p t n", p=128)

    strk = P.alloc("strk", 2 * T, BF16)
    strv = P.alloc("strv", 2 * T, BF16)
    btres = P.alloc("btres", NT * 128, F32)
    khT = P.alloc("khT", 2 * 256, BF16)
    drowp = [P.alloc(f"drow{p}", NT, F32) for p in range(2)]
    strkv = [strk[:, 0:T], strk[:, T:2 * T]]
    if not prefix:
        strq = P.alloc("strq", 2 * T, BF16)
        bt9 = P.alloc("bt9", 2 * NT * 9, F32)
        tabin = P.alloc("tabin", 2 * 640, F32)
        tabex = P.alloc("tabex", 2 * 640, BF16)
        fkst = [P.alloc(f"fkst{p}", T, BF16) for p in range(2)]
        fqh = [P.alloc(f"fqh{p}", T, BF16) for p in range(2)]
        fkd = [P.alloc(f"fkd{p}", T, BF16) for p in range(2)]
        fq32 = [P.alloc(f"fq32{p}", NT * 64, BF16) for p in range(2)]
        fk32 = [P.alloc(f"fk32{p}", NT * 64, BF16) for p in range(2)]
        fq64 = [P.alloc(f"fq64{p}", NT * 64, BF16) for p in range(2)]
        fk64 = [P.alloc(f"fk64{p}", NT * 64, BF16) for p in range(2)]
        Asb = P.alloc("Asb", 2 * 512, BF16)
        ybc = P.alloc("ybc", 2 * T, BF16)
        strqv = [strq[:, 0:T], strq[:, T:2 * T]]
        bt9r = bt9.rearrange("p (q t c) -> p q t c", q=2, c=9)
        tabinv = [tabin[:, 0:640], tabin[:, 640:1280]]
        tabexv = [tabex[:, 0:640], tabex[:, 640:1280]]
    btres_r = btres.rearrange("p (m c) -> p m c", c=128)

    pA = None
    pO = [B.PA[:, 1024:1280], B.PA[:, 1536:1792]]
    pST = B.PB[:, 0:1024]
    pTRb = B.PB[:, 1024:2048].bitcast(BF16)
    pTR = [pTRb[:, 0:256], pTRb[:, 1024:1280]]

    war_t = [None, None]
    sup_v = [None]
    exps_done = [None, None]
    fact_v = {}
    fact_k = {}
    fact_s = {}
    ld_tgt = {}
    ysp = B.dsem("ysp")
    bts = B.dsem("bts")

    @blk.gpsimd
    def _(g, bts=bts, pv=bts.n):
        g.wait_ge(bts.h, pv)
        g.dma_start(out=btres[:, :],
                    in_=B.h_btab.rearrange("m p c -> p m c")).then_inc(bts.h, 16)
    bts.n += 16
    bt_t = bts.n

    if not prefix:
        @blk.vector
        def _(v):
            v.memset(bt9r[:, :, :, 0:1], 0.0)

    def loads(g, ci):
        p = ci % 2
        scs = B.dsem(f"scn{p}")
        g.wait_ge(scs.h, scs.n)
        if war_t[p] is not None:
            g.wait_ge(B.sT.h, war_t[p])
        if not prefix:
            g.dma_start(out=strqv[p], in_=q16_t[:, :, ci * CH:(ci + 1) * CH]) \
                .then_inc(scs.h, 16)
            scs.n += 16
        g.dma_start(out=strkv[p], in_=ko_t[:, :, ci * CH:(ci + 1) * CH]) \
            .then_inc(scs.h, 16)
        g.dma_start(out=strv[:, p * T:(p + 1) * T],
                    in_=B.h_v[ci * CH:(ci + 1) * CH, :]).then_inc(scs.h, 16)
        scs.n += 32

    def factors_p1(ci):
        p = ci % 2
        tabv = None
        if prefix:
            fact_k[ci] = None
            fact_v[ci] = None
        else:
            @blk.vector
            def _(v, ci=ci, p=p, bt_t=bt_t, bts=bts, ws=exps_done[p]):
                v.wait_ge(bts.h, bt_t)
                if ws is not None:
                    v.wait_ge(B.sSq.h, ws)
                bt = bt9r[:, p]
                v.tensor_copy(bt[:, :, 1:9], btres_r[:, :, ci * 8:(ci + 1) * 8])
                tbi = tabinv[p]
                v.tensor_tensor(tbi[:, 0:128].rearrange("p (t c) -> p t c", c=8),
                                bt[:, :, 8:9].broadcast_to((128, NT, 8)),
                                bt[:, :, 1:9], ALU.subtract)
                v.tensor_tensor(tbi[:, 128:192].rearrange("p (t c) -> p t c", c=4),
                                bt[:, :, 4:8],
                                bt[:, :, 4:5].broadcast_to((128, NT, 4)),
                                ALU.subtract)
                v.tensor_tensor(tbi[:, 192:256].rearrange("p (t c) -> p t c", c=4),
                                bt[:, :, 4:5].broadcast_to((128, NT, 4)),
                                bt[:, :, 1:5], ALU.subtract)
                v.tensor_tensor(tbi[:, 256:384].rearrange("p (t c) -> p t c", c=8),
                                bt[:, :, 0:8], bt[:, :, 1:9], ALU.subtract)
                v.tensor_tensor(tbi[:, 384:512].rearrange("p (t c) -> p t c", c=8),
                                bt[:, :, 1:9], bt[:, :, 0:8], ALU.subtract)
                v.tensor_copy(tbi[:, 512:640].rearrange("p (t c) -> p t c", c=8),
                              bt[:, :, 0:8])
            tabv = B.sVq.n

        @blk.scalar
        def _(s, ci=ci, p=p, tabv=tabv, wv=sup_v[0], bts=bts, bt_t=bt_t):
            s.wait_ge(bts.h, bt_t)
            if tabv is not None:
                s.wait_ge(B.sVq.h, tabv)
            if wv is not None:
                s.wait_ge(B.sVq.h, wv)
            if not prefix:
                s.activation(tabexv[p], tabinv[p], AF.Exp)
            s.activation(drowp[p][:, :],
                         btres_r[:, :, ci * 8 + 7:ci * 8 + 8]
                         .rearrange("p m c -> p (m c)"), AF.Exp)
        exps_done[p] = B.sSq.n
        fact_s[ci] = B.sSq.n

    def factors_p2a(ci):
        p = ci % 2
        scs = B.dsem(f"scn{p}")
        c0 = B.sVq.n

        @blk.vector
        def _(v, ci=ci, p=p, scs=scs, dx_t=ld_tgt[ci], ws=exps_done[p],
              wt=war_t[p]):
            v.wait_ge(scs.h, dx_t)
            v.wait_ge(B.sSq.h, ws)
            if wt is not None:
                v.wait_ge(B.sT.h, wt)
            te = tabexv[p]
            k4 = strkv[p].rearrange("p (t a b) -> p t a b", a=8, b=16)
            q4 = strqv[p].rearrange("p (t a b) -> p t a b", a=8, b=16)
            est_b = te[:, 0:128].rearrange("p (t c) -> p t c", c=8) \
                .unsqueeze(-1).broadcast_to((128, NT, 8, 16))
            v.tensor_tensor(fkst[p][:, :].rearrange("p (t a b) -> p t a b",
                                                    a=8, b=16),
                            k4, est_b, ALU.mult)
            ein_b = te[:, 512:640].rearrange("p (t c) -> p t c", c=8) \
                .unsqueeze(-1).broadcast_to((128, NT, 8, 16))
            v.tensor_tensor(fqh[p][:, :].rearrange("p (t a b) -> p t a b",
                                                   a=8, b=16),
                            q4, ein_b, ALU.mult)
            rd_b = te[:, 256:384].rearrange("p (t c) -> p t c", c=8) \
                .unsqueeze(-1).broadcast_to((128, NT, 8, 16))
            v.tensor_tensor(fkd[p][:, :].rearrange("p (t a b) -> p t a b",
                                                   a=8, b=16),
                            k4, rd_b, ALU.mult)
        fact_k[ci] = c0 + 1

    def factors_p2b(ci):
        p = ci % 2
        scs = B.dsem(f"scn{p}")

        @blk.vector
        def _(v, ci=ci, p=p, scs=scs, dx_t=ld_tgt[ci], ws=exps_done[p],
              wt=war_t[p]):
            v.wait_ge(scs.h, dx_t)
            v.wait_ge(B.sSq.h, ws)
            if wt is not None:
                v.wait_ge(B.sT.h, wt)
            te = tabexv[p]
            k4 = strkv[p].rearrange("p (t a b) -> p t a b", a=8, b=16)
            q4 = strqv[p].rearrange("p (t a b) -> p t a b", a=8, b=16)
            e64q_b = te[:, 128:192].rearrange("p (t c) -> p t c", c=4) \
                .unsqueeze(-1).broadcast_to((128, NT, 4, 16))
            v.tensor_tensor(fq64[p][:, :].rearrange("p (t a b) -> p t a b",
                                                    a=4, b=16),
                            strqv[p].rearrange("p (t s) -> p t s", s=128)
                            [:, :, 64:128].rearrange("p t (a b) -> p t a b", b=16),
                            e64q_b, ALU.mult)
            e64k_b = te[:, 192:256].rearrange("p (t c) -> p t c", c=4) \
                .unsqueeze(-1).broadcast_to((128, NT, 4, 16))
            v.tensor_tensor(fk64[p][:, :].rearrange("p (t a b) -> p t a b",
                                                    a=4, b=16),
                            strkv[p].rearrange("p (t s) -> p t s", s=128)
                            [:, :, 0:64].rearrange("p t (a b) -> p t a b", b=16),
                            e64k_b, ALU.mult)
            q32_4 = fq32[p][:, :].rearrange("p (t a b) -> p t a b", a=4, b=16)
            k32_4 = fk32[p][:, :].rearrange("p (t a b) -> p t a b", a=4, b=16)
            dd = te[:, 384:512].rearrange("p (t c) -> p t c", c=8)
            v.tensor_copy(q32_4[:, :, 0::2, :], q4[:, :, 2::4, :])
            v.tensor_tensor(q32_4[:, :, 1::2, :], q4[:, :, 3::4, :],
                            dd[:, :, 2::4].unsqueeze(-1)
                            .broadcast_to((128, NT, 2, 16)), ALU.mult)
            v.tensor_copy(k32_4[:, :, 1::2, :], k4[:, :, 1::4, :])
            v.tensor_tensor(k32_4[:, :, 0::2, :], k4[:, :, 0::4, :],
                            dd[:, :, 1::4].unsqueeze(-1)
                            .broadcast_to((128, NT, 2, 16)), ALU.mult)
        fact_v[ci] = B.sVq.n

    def supd_emit_a(ci, tS):
        p = ci % 2

        @blk.vector
        def _(v, p=p, tS=tS, ws=fact_s[ci]):
            v.wait_ge(B.sT.h, tS)
            v.wait_ge(B.sSq.h, ws)
            sp = B.spar
            drow_b = drowp[p][:, :].unsqueeze(-1).broadcast_to((128, NT, 64))
            v.tensor_tensor(B.S[:, 2048:3072], B.S[:, sp * 1024:(sp + 1) * 1024],
                            drow_b, ALU.mult)
            v.tensor_tensor(B.S[:, (1 - sp) * 1024:(2 - sp) * 1024],
                            B.S[:, 2048:3072], pST, ALU.add)
        B.spar = 1 - B.spar
        if prefix:
            sup_v[0] = B.sVq.n

    def supd_emit_b(ci, tOmm):
        @blk.vector
        def _(v, tOmm=tOmm):
            v.wait_ge(B.sT.h, tOmm)
            sp = B.spar
            v.tensor_copy(B.Sbf[:, :], B.S[:, sp * 1024:(sp + 1) * 1024])
        sup_v[0] = B.sVq.n

    @blk.gpsimd
    def _(g):
        loads(g, 0)
    ld_tgt[0] = B.dsem("scn0").n
    factors_p1(0)
    if not prefix:
        factors_p2a(0)
        factors_p2b(0)

    pre_done = False
    for ci in range(NCH):
        p = ci % 2
        if ci + 1 < NCH:
            @blk.gpsimd
            def _(g, ci=ci):
                loads(g, ci + 1)
            ld_tgt[ci + 1] = B.dsem(f"scn{(ci + 1) % 2}").n
        hooks = {}
        if ci + 1 < NCH:
            hooks[1] = [lambda ci=ci: factors_p1(ci + 1)]
            if not prefix:
                hooks[3] = [lambda ci=ci: factors_p2a(ci + 1)]
                hooks[4] = [lambda ci=ci: factors_p2b(ci + 1)]
        nxt = None
        if ci + 1 < NCH:
            nxt = dict(p=1 - p, scs=B.dsem(f"scn{(ci + 1) % 2}"),
                       gK=(lambda ci=ci: fact_k[ci + 1]),
                       ld=(lambda ci=ci: ld_tgt[ci + 1]))
        tiles_emit(B, blk, prefix, ci, p, pA, pO, pST, pTR, strv,
                   None if prefix else fkst, strkv, khT,
                   None if prefix else Asb,
                   None if prefix else (strqv, strkv, fkd, fq32, fk32,
                                        fq64, fk64, fqh),
                   None if prefix else ybc,
                   fact_k[ci], fact_v[ci], B.dsem(f"scn{p}"), ld_tgt[ci],
                   sup_v, ysp, hooks,
                   (lambda tS, ci=ci: supd_emit_a(ci, tS),
                    lambda tOmm, ci=ci: supd_emit_b(ci, tOmm)),
                   pre_done, nxt)
        pre_done = ci + 1 < NCH
        war_t[p] = B.sT.n
        if not prefix:
            oev = B.sVq.n

            @blk.gpsimd
            def _(g, ci=ci, p=p, oev=oev):
                g.wait_ge(B.sVq.h, oev)
                g.dma_start(out=y_t[:, :, ci * CH:(ci + 1) * CH],
                            in_=ybc[:, p * T:(p + 1) * T]
                            .rearrange("p (t c) -> p t c", c=128)) \
                    .then_inc(ysp.h, 16)
            ysp.n += 16

    B.barrier(blk)
    names = ["strk", "strv", "btres", "khT", "drow0", "drow1"]
    if not prefix:
        names += ["strq", "bt9", "tabin", "tabex",
                  "fkst0", "fkst1", "fqh0", "fqh1", "fkd0", "fkd1",
                  "fq320", "fq321", "fk320", "fk321",
                  "fq640", "fq641", "fk640", "fk641", "Asb", "ybc"]
    P.freeb(*names)


def tiles_emit(B, blk, prefix, ci, p, pA, pO, pST, pTR, strv, fkst, strkv, khT,
               Asb, fmain, ybc, gK, gF, scs, ld_t, sup_v, ysp, hooks, supd_cb,
               pre_done, nxt):
    st = getattr(B, "_scn", None)
    if st is None or ci == 0:
        B._scn = {"khv": [None, None], "blk1": [None, None],
                  "aev": [None, None], "omm": [None, None],
                  "oev": [None, None], "pre_khv": None}
        st = B._scn
    if fmain is not None:
        fq16, fk16, fkd, fq32, fk32, fq64, fk64, fqh = fmain
    ksrc = strkv if prefix else fkst
    NP = NT // 2

    def transp(g, pp, kk, sc, ld):
        s = g % 2

        @blk.tensor
        def _(t, g=g, s=s, pp=pp, wk=st["khv"][s], kk=kk, sc=sc, ld=ld):
            if g == 0:
                if prefix:
                    t.wait_ge(sc.h, ld)
                else:
                    t.wait_ge(B.sVq.h, kk)
            if wk is not None:
                t.wait_ge(B.sVq.h, wk)
            for j in (0, 1):
                tile = 2 * g + j
                last = t.transpose(pTR[s][:, j * 128:(j + 1) * 128],
                                   ksrc[pp][:, tile * 128:(tile + 1) * 128],
                                   B.ident[:, :])
            last.then_inc(B.sT.h, 1)
        B.sT.n += 1
        return B.sT.n

    def khcopy(g, trv):
        s = g % 2

        @blk.vector
        def _(v, s=s, trv=trv, wb=st["blk1"][s]):
            v.wait_ge(B.sT.h, trv)
            if wb is not None:
                v.wait_ge(B.sT.h, wb)
            v.tensor_copy(khT[:, s * 256:(s + 1) * 256], pTR[s])
        st["khv"][s] = B.sVq.n
        return B.sVq.n

    if pre_done:
        khv_g = st["pre_khv"]
    else:
        tr0 = transp(0, p, gK, scs, ld_t)
        khv_g = khcopy(0, tr0)

    for g in range(NP):
        s = g % 2

        @blk.tensor
        def _(t, g=g, s=s, khv=khv_g, wa=st["aev"][s], sv=sup_v[0], gF=gF):
            t.wait_ge(B.sVq.h, khv)
            if g == 0:
                if sv is not None:
                    t.wait_ge(B.sVq.h, sv)
                if not prefix:
                    t.wait_ge(B.sVq.h, gF)
            if wa is not None:
                t.wait_ge(B.sVq.h, wa)
            for j in (0, 1):
                tile = 2 * g + j
                last = t.matmul(pST[0:64, tile * 64:(tile + 1) * 64],
                                khT[:, s * 256 + j * 128:s * 256 + j * 128 + 64],
                                strv[:, p * T + tile * 128:p * T + tile * 128 + 64],
                                start=True, stop=True, tile_position=(0, 0))
                last = t.matmul(pST[64:128, tile * 64:(tile + 1) * 64],
                                khT[:, s * 256 + j * 128 + 64:s * 256 + (j + 1) * 128],
                                strv[:, p * T + tile * 128 + 64:p * T + tile * 128 + 128],
                                start=True, stop=True, tile_position=(0, 64))
            if not prefix:
                for j in (0, 1):
                    tile = 2 * g + j
                    ts_ = tile * 128
                    tc_ = tile * 64
                    for hh in (0, 1):
                        hr = 64 * hh
                        base = s * 512 + j * 256 + 128 * hh
                        last = t.matmul(B.PA[0:128, base:base + 128],
                                        fkd[p][hr:hr + 64, ts_:ts_ + 128],
                                        fq16[p][hr:hr + 64, ts_:ts_ + 128],
                                        start=True, stop=True, tile_position=(hr, 0))
                        for J in (0, 2, 4, 6):
                            last = t.matmul(B.PA[J * 16:(J + 1) * 16,
                                                 base + (J + 1) * 16:base + (J + 2) * 16],
                                            fk16[p][hr:hr + 64, ts_ + J * 16:ts_ + (J + 1) * 16],
                                            fq16[p][hr:hr + 64, ts_ + (J + 1) * 16:ts_ + (J + 2) * 16],
                                            start=True, stop=True, tile_position=(hr, J * 16))
                        last = t.matmul(B.PA[0:32, base + 32:base + 64],
                                        fk32[p][hr:hr + 64, tc_:tc_ + 32],
                                        fq32[p][hr:hr + 64, tc_:tc_ + 32],
                                        start=True, stop=True, tile_position=(hr, 0))
                        last = t.matmul(B.PA[64:96, base + 96:base + 128],
                                        fk32[p][hr:hr + 64, tc_ + 32:tc_ + 64],
                                        fq32[p][hr:hr + 64, tc_ + 32:tc_ + 64],
                                        start=True, stop=True, tile_position=(hr, 64))
                        last = t.matmul(B.PA[0:64, base + 64:base + 128],
                                        fk64[p][hr:hr + 64, tc_:tc_ + 64],
                                        fq64[p][hr:hr + 64, tc_:tc_ + 64],
                                        start=True, stop=True, tile_position=(hr, 0))
            last.then_inc(B.sT.h, 1)
        B.sT.n += 1
        st["blk1"][s] = B.sT.n
        tb1 = B.sT.n

        if not prefix:
            @blk.vector
            def _(v, s=s, tb1=tb1, wo=st["omm"][s]):
                v.wait_ge(B.sT.h, tb1)
                if wo is not None:
                    v.wait_ge(B.sT.h, wo)
                utm_b = B.utm[:, :].unsqueeze(1).broadcast_to((128, 4, 128))
                v.tensor_tensor(Asb[:, s * 512:(s + 1) * 512],
                                B.PA[:, s * 512:(s + 1) * 512], utm_b, ALU.mult)
            st["aev"][s] = B.sVq.n
        aev_g = B.sVq.n

        if g == NP - 1:
            supd_cb[0](st["blk1"][s])

        if g + 1 < NP:
            trn = transp(g + 1, p, gK, scs, ld_t)
            khv_g = khcopy(g + 1, trn)

        for cb in hooks.get(g, []):
            cb()

        if prefix:
            continue

        @blk.tensor
        def _(t, g=g, s=s, aev=aev_g, wo=st["oev"][s]):
            t.wait_ge(B.sVq.h, aev)
            if wo is not None:
                t.wait_ge(B.sVq.h, wo)
            for j in (0, 1):
                tile = 2 * g + j
                ts_ = tile * 128
                tc_ = tile * 64
                oc = j * 128
                last = t.matmul(pO[s][0:64, oc:oc + 128], B.Sbf[0:64, tc_:tc_ + 64],
                                fqh[p][0:64, ts_:ts_ + 128], start=True, stop=False,
                                tile_position=(0, 0))
                last = t.matmul(pO[s][0:64, oc:oc + 128],
                                strv[:, p * T + ts_:p * T + ts_ + 64],
                                Asb[:, s * 512 + j * 256:s * 512 + j * 256 + 128],
                                start=False, stop=True, tile_position=(0, 0))
                last = t.matmul(pO[s][64:128, oc:oc + 128], B.Sbf[64:128, tc_:tc_ + 64],
                                fqh[p][64:128, ts_:ts_ + 128], start=True, stop=False,
                                tile_position=(64, 64))
                last = t.matmul(pO[s][64:128, oc:oc + 128],
                                strv[:, p * T + ts_ + 64:p * T + ts_ + 128],
                                Asb[:, s * 512 + j * 256 + 128:s * 512 + (j + 1) * 256],
                                start=False, stop=True, tile_position=(0, 64))
            last.then_inc(B.sT.h, 1)
        B.sT.n += 1
        st["omm"][s] = B.sT.n
        tO = B.sT.n

        @blk.vector
        def _(v, g=g, s=s, p=p, tO=tO, wy=ysp.n if (ci >= 2 and g == 0) else None):
            v.wait_ge(B.sT.h, tO)
            if wy is not None:
                v.wait_ge(ysp.h, wy)
            v.tensor_copy(ybc[:, p * T + g * 256:p * T + (g + 1) * 256], pO[s])
        st["oev"][s] = B.sVq.n

        if g == NP - 1:
            supd_cb[1](tO)

    if nxt is not None:
        trn = transp(0, nxt["p"], nxt["gK"](), nxt["scs"], nxt["ld"]())
        st["pre_khv"] = khcopy(0, trn)


def emit_sinit(B, blk):
    @blk.vector
    def _(v):
        sp = B.spar
        v.tensor_scalar(B.S[:, (1 - sp) * 1024:(2 - sp) * 1024],
                        B.S[:, sp * 1024:(sp + 1) * 1024],
                        B.sflag[:, 0:1], None, ALU.mult)
        v.tensor_copy(B.Sbf[:, :], B.S[:, (1 - sp) * 1024:(2 - sp) * 1024])
    B.spar = 1 - B.spar
    B.barrier(blk)


def emit_yln(B, blk, pre_cb=None):
    P = B.pool
    y_t = B.h_y.rearrange("(t p) n -> p t n", p=128)
    ytl = P.alloc("ytl", NT * T, BF16)
    sqb = P.alloc("sqb", 2 * T, BF16)
    ylnb = P.alloc("ylnb", NT * T, BF16)
    srow = ylnb[0:1, 0:24576].bitcast(F32)
    abrow = P.alloc("abrow", 2 * T, F32)
    bcast = P.alloc("bcast", 2 * T, F32)
    scrf = P.alloc("scrf", T, F32)
    yls = B.dsem("yls")
    if pre_cb is not None:
        pre_cb()

    sS_done = [None, None]
    sT_done = [None, None]
    for k in range(NT):
        p = k % 2

        @blk.sync
        def _(sy, k=k, p=p, pv=yls.n, wt=sS_done[p], wtt=sT_done[p]):
            sy.wait_ge(yls.h, pv)
            if wtt is not None:
                sy.wait_ge(B.sT.h, wtt)
            sy.dma_start(out=ytl[:, k * T:(k + 1) * T], in_=y_t[:, k, :]) \
                .then_inc(yls.h, 16)
        yls.n += 16
        dx_t = yls.n

        @blk.scalar
        def _(s, k=k, p=p, dx_t=dx_t, wt=sT_done[p]):
            s.wait_ge(yls.h, dx_t)
            if wt is not None:
                s.wait_ge(B.sT.h, wt)
            s.activation(sqb[:, p * T:(p + 1) * T], ytl[:, k * T:(k + 1) * T],
                         AF.Square)
        sS_done[p] = B.sSq.n
        sv = B.sSq.n

        @blk.tensor
        def _(t, k=k, p=p, sv=sv):
            t.wait_ge(B.sSq.h, sv)
            for n in range(4):
                last = t.matmul(B.PA[0:1, n * 512:(n + 1) * 512], B.ones[:, :],
                         ytl[:, k * T + n * 512:k * T + (n + 1) * 512],
                         start=(k == 0), stop=(k == NT - 1))
                last = t.matmul(B.PB[0:1, n * 512:(n + 1) * 512], B.ones[:, :],
                         sqb[:, p * T + n * 512:p * T + (n + 1) * 512],
                         start=(k == 0), stop=(k == NT - 1))
            t.drain()
            t.nop(nofuse=True).then_inc(B.sT.h, 1)
        B.sT.n += 1
        sT_done[p] = B.sT.n

    emit_stats_tail(B, blk, srow, abrow, bcast)
    bc_v = B.sVq.n

    for k in range(NT):
        @blk.vector
        def _(v, k=k, bc_v=bc_v):
            v.wait_ge(B.sVq.h, bc_v)
            v.tensor_tensor(scrf[:, :], ytl[:, k * T:(k + 1) * T], bcast[:, 0:T],
                            ALU.mult)
            v.tensor_tensor(scrf[:, :], scrf[:, :], bcast[:, T:2 * T], ALU.add)
            v.tensor_scalar(ylnb[:, k * T:(k + 1) * T], scrf[:, :],
                            B.vecs[:, 3 * NT + k:3 * NT + k + 1],
                            B.vecs[:, 4 * NT + k:4 * NT + k + 1],
                            ALU.mult, ALU.add)

    B.barrier(blk)
    P.freeb("ytl", "sqb", "abrow", "bcast", "scrf")
    return ylnb


def emit_ffn_prefetch(B, blk):
    P = B.pool
    xnc = P.alloc_top("xnc", NT * 1024, BF16)
    guslab = P.alloc_top("guslab", 2 * NT * 256, BF16)
    xn_t = B.h_xn.rearrange("(t p) n -> p t n", p=128)
    ffx = B.dsem("ffx")

    @blk.sync
    def _(sy, ffx=ffx, pv=ffx.n):
        sy.wait_ge(ffx.h, pv)
        for k in range(NT):
            sy.dma_start(out=xnc[:, k * 1024:(k + 1) * 1024],
                         in_=xn_t[:, k, 1:1025]).then_inc(ffx.h, 16)
    ffx.n += NT * 16
    xq = ffx.n
    gsem = []
    for g in (0, 1):
        ffg = B.dsem(f"ffg{g}")

        @blk.gpsimd
        def _(gg, g=g, ffg=ffg, pv=ffg.n):
            gg.wait_ge(ffg.h, pv)
            gg.dma_start(out=guslab[:, g * NT * 256:(g + 1) * NT * 256],
                         in_=B.i_gut[g].rearrange("k p c -> p k c")).then_inc(ffg.h, 16)
        ffg.n += 16
        gsem.append((ffg, ffg.n))
    return dict(xq=xq, gsem=gsem)


def emit_ffn(B, blk, pre=None):
    P = B.pool
    hbuf = P.alloc("hbuf", 64 * 1024, BF16)
    xn_t = B.h_xn.rearrange("(t p) n -> p t n", p=128)
    s_t = B.h_s.rearrange("(t p) n -> p t n", p=128)
    x_t = B.i_xm.rearrange("(t p) n -> p t n", p=128)
    o_t = B.o_out.rearrange("(t p) n -> p t n", p=128)

    t_pv, v_pv, s_pv = B.sT.n, B.sVq.n, B.sSq.n
    spl_pv = [B.dsem("spl0").n, B.dsem("spl1").n]

    for tc in range(2):
        pre0 = pre is not None and tc == 0
        if pre0:
            xnc = P.view("xnc")
            guslab = P.view("guslab")
        else:
            xnc = P.alloc("xnc", NT * 1024, BF16)
            guslab = P.alloc("guslab", 2 * NT * 256, BF16)
        sgev = P.alloc("sgev", 2 * 1024, BF16)

        ffx = B.dsem("ffx")

        if pre0:
            xq = pre["xq"]
        else:
            @blk.sync
            def _(sy, tc=tc, ffx=ffx, pv=ffx.n, tw=t_pv, vw=v_pv, sw=s_pv,
                  spw=tuple(spl_pv)):
                sy.wait_ge(ffx.h, pv)
                sy.wait_ge(B.sT.h, tw)
                sy.wait_ge(B.sVq.h, vw)
                sy.wait_ge(B.sSq.h, sw)
                sy.wait_ge(B.dsem("spl0").h, spw[0])
                sy.wait_ge(B.dsem("spl1").h, spw[1])
                for k in range(NT):
                    sy.dma_start(out=xnc[:, k * 1024:(k + 1) * 1024],
                                 in_=xn_t[:, k, 1 + tc * 1024:1 + (tc + 1) * 1024]) \
                        .then_inc(ffx.h, 16)
            ffx.n += NT * 16
            xq = ffx.n

        sV_done = [None, None]
        sS_done = [None, None]
        for g in range(64):
            pg = g % 2
            PP = B.PA if pg == 0 else B.PB

            wss = B.dsem(f"wsl{pg}")

            if pre0 and g < 2:
                dws, dw_t = pre["gsem"][g]
            else:
                @blk.gpsimd
                def _(gg, g=g, pg=pg, wss=wss, pv=wss.n, wt=sV_done[pg],
                      tw=t_pv, vw=v_pv):
                    gg.wait_ge(wss.h, pv)
                    if g < 2:
                        gg.wait_ge(B.sT.h, tw)
                        gg.wait_ge(B.sVq.h, vw)
                    if wt is not None:
                        gg.wait_ge(B.sVq.h, wt)
                    gg.dma_start(out=guslab[:, pg * NT * 256:(pg + 1) * NT * 256],
                                 in_=B.i_gut[g].rearrange("k p c -> p k c")).then_inc(wss.h, 16)
                wss.n += 16
                dws, dw_t = wss, wss.n

            @blk.tensor
            def _(t, g=g, pg=pg, PP=PP, dws=dws, ffx=ffx, dw_t=dw_t, xq=xq,
                  wt=sV_done[pg], vw=v_pv, sw=s_pv):
                t.wait_ge(dws.h, dw_t)
                if g < 2:
                    t.wait_ge(ffx.h, xq)
                    t.wait_ge(B.sVq.h, vw)
                    t.wait_ge(B.sSq.h, sw)
                if wt is not None:
                    t.wait_ge(B.sVq.h, wt)
                for kk in range(NT):
                    for n in range(2):
                        last = t.matmul(PP[:, n * 512:(n + 1) * 512],
                                 guslab[:, pg * NT * 256 + kk * 256:pg * NT * 256 + kk * 256 + 128],
                                 xnc[:, kk * 1024 + n * 512:kk * 1024 + (n + 1) * 512],
                                 start=(kk == 0), stop=(kk == NT - 1))
                        last = t.matmul(PP[:, 1024 + n * 512:1024 + (n + 1) * 512],
                                 guslab[:, pg * NT * 256 + kk * 256 + 128:pg * NT * 256 + (kk + 1) * 256],
                                 xnc[:, kk * 1024 + n * 512:kk * 1024 + (n + 1) * 512],
                                 start=(kk == 0), stop=(kk == NT - 1))
                last.then_inc(B.sT.h, 1)
            B.sT.n += 1
            tv = B.sT.n

            @blk.scalar
            def _(s, pg=pg, PP=PP, tv=tv, wt=sS_done[pg]):
                s.wait_ge(B.sT.h, tv)
                if wt is not None:
                    s.wait_ge(B.sVq.h, wt)
                s.activation(sgev[:, pg * 1024:(pg + 1) * 1024], PP[:, 0:1024],
                             AF.Silu)
            sv = B.sSq.n

            @blk.vector
            def _(v, g=g, pg=pg, PP=PP, sv=sv):
                v.wait_ge(B.sSq.h, sv)
                v.tensor_tensor(hbuf[:, g * 1024:(g + 1) * 1024],
                                sgev[:, pg * 1024:(pg + 1) * 1024],
                                PP[:, 1024:2048], ALU.mult)
            sV_done[pg] = B.sVq.n
            sS_done[pg] = B.sVq.n

        t_pv, v_pv, s_pv = B.sT.n, B.sVq.n, B.sSq.n
        P.freeb("xnc", "guslab", "sgev")
        dwslab = P.alloc("dwslab", 2 * 64 * 128, BF16)
        grow = P.alloc("grow", 2 * 1024, BF16)
        srow_b = P.alloc("srow_b", 2 * 1024, BF16)
        xrow_b = P.alloc("xrow_b", 2 * 1024, F32)
        zrB = P.alloc("zrB", 2 * 1024, F32)

        sT_done = [None, None]
        sV_done2 = [None, None]
        out_done = [None, None]
        for m in range(NT):
            pm = m % 2
            PP = B.PA if pm == 0 else B.PB

            wss = B.dsem(f"wsl{pm}")
            ffs = B.dsem("ffs")

            @blk.gpsimd
            def _(gg, m=m, pm=pm, wss=wss, pv=wss.n, wt=sT_done[pm],
                  tw=t_pv, vw=v_pv):
                gg.wait_ge(wss.h, pv)
                if m < 2:
                    gg.wait_ge(B.sT.h, tw)
                    gg.wait_ge(B.sVq.h, vw)
                if wt is not None:
                    gg.wait_ge(B.sT.h, wt)
                gg.dma_start(out=dwslab[:, pm * 64 * 128:(pm + 1) * 64 * 128],
                             in_=B.i_dwt[m].rearrange("k p c -> p k c")).then_inc(wss.h, 16)
            wss.n += 16
            dw_t = wss.n

            @blk.sync
            def _(sy, m=m, tc=tc, pm=pm, ffs=ffs, pv=ffs.n, wt=sV_done2[pm],
                  tw=t_pv, vw=v_pv):
                sy.wait_ge(ffs.h, pv)
                if m < 2:
                    sy.wait_ge(B.sT.h, tw)
                    sy.wait_ge(B.sVq.h, vw)
                if wt is not None:
                    sy.wait_ge(B.sVq.h, wt)
                sy.dma_start(out=srow_b[:, pm * 1024:(pm + 1) * 1024],
                             in_=s_t[:, m, tc * 1024:(tc + 1) * 1024]) \
                    .then_inc(ffs.h, 16)
                sy.dma_start(out=xrow_b[:, pm * 1024:(pm + 1) * 1024],
                             in_=x_t[:, m, tc * 1024:(tc + 1) * 1024]) \
                    .then_inc(ffs.h, 16)
            ffs.n += 32
            ldq = ffs.n

            @blk.tensor
            def _(t, m=m, pm=pm, PP=PP, wss=wss, dw_t=dw_t, wt=sV_done2[pm],
                  vw=v_pv, sw=s_pv):
                t.wait_ge(wss.h, dw_t)
                if m < 2:
                    t.wait_ge(B.sVq.h, vw)
                    t.wait_ge(B.sSq.h, sw)
                if wt is not None:
                    t.wait_ge(B.sVq.h, wt)
                for kk in range(64):
                    for n in range(2):
                        last = t.matmul(PP[:, n * 512:(n + 1) * 512],
                                 dwslab[:, pm * 64 * 128 + kk * 128:pm * 64 * 128 + (kk + 1) * 128],
                                 hbuf[:, kk * 1024 + n * 512:kk * 1024 + (n + 1) * 512],
                                 start=(kk == 0), stop=(kk == 63))
                last.then_inc(B.sT.h, 1)
            B.sT.n += 1
            sT_done[pm] = B.sT.n
            tv = B.sT.n

            @blk.scalar
            def _(s, m=m, pm=pm, PP=PP, tv=tv, wt=sV_done2[pm], vw=v_pv):
                s.wait_ge(B.sT.h, tv)
                if m < 2:
                    s.wait_ge(B.sVq.h, vw)
                if wt is not None:
                    s.wait_ge(B.sVq.h, wt)
                s.activation(grow[:, pm * 1024:(pm + 1) * 1024], PP[:, 0:1024],
                             AF.Silu)
            sv = B.sSq.n

            @blk.vector
            def _(v, m=m, pm=pm, sv=sv, ffs=B.dsem("ffs"), ldq=ldq,
                  sps=B.dsem(f"spl{pm}"), wt=out_done[pm]):
                v.wait_ge(B.sSq.h, sv)
                v.wait_ge(ffs.h, ldq)
                if wt is not None:
                    v.wait_ge(sps.h, wt)
                v.tensor_tensor(zrB[:, pm * 1024:(pm + 1) * 1024],
                                grow[:, pm * 1024:(pm + 1) * 1024],
                                srow_b[:, pm * 1024:(pm + 1) * 1024], ALU.mult)
                v.tensor_tensor(zrB[:, pm * 1024:(pm + 1) * 1024],
                                zrB[:, pm * 1024:(pm + 1) * 1024],
                                xrow_b[:, pm * 1024:(pm + 1) * 1024], ALU.add)
            sV_done2[pm] = B.sVq.n
            vv = B.sVq.n

            sps = B.dsem(f"spl{pm}")

            @blk.scalar
            def _(s, m=m, tc=tc, pm=pm, sps=sps, vv=vv):
                s.wait_ge(B.sVq.h, vv)
                s.dma_start(out=o_t[:, m, tc * 1024:(tc + 1) * 1024],
                            in_=zrB[:, pm * 1024:(pm + 1) * 1024]).then_inc(sps.h, 16)
            sps.n += 16
            out_done[pm] = sps.n

        t_pv, v_pv, s_pv = B.sT.n, B.sVq.n, B.sSq.n
        spl_pv = [B.dsem("spl0").n, B.dsem("spl1").n]
        P.freeb("dwslab", "grow", "srow_b", "xrow_b", "zrB")

    P.freeb("hbuf")



def emit_tap(B, blk, name, src):
    if name not in B.tapt:
        return
    B.barrier(blk)

    tps = B.dsem("tap")

    @blk.sync
    def _(sy, tps=tps):
        sy.dma_start(out=B.tapt[name], in_=src).then_inc(tps.h, 16)
    tps.n += 16
    B.barrier(blk)


def build_kernel(taps=(), phases=99):
    B = KB(taps, phases)
    B.declare()
    B.allocate()
    with B.nc.Block() as raw_blk:
        blk = BlkProxy(raw_blk, B)
        emit_p0(B, blk)
        for prefix in (True, False):
            w_dma = emit_w12_prefetch(B, blk)
            emit_ln(B, blk, B.i_xp if prefix else B.i_xm, prefix)
            td_dma = emit_mm(B, blk, w_dma)
            B.pool.alloc("xrb", NT * T, BF16)
            emit_mixer_r(B, blk, 3)
            B.pool.freeb("mmr3")
            emit_wpath_head(B, blk, td_dma)
            if not prefix:
                emit_mixer_r(B, blk, 0)
                B.pool.freeb("mmr0")
                emit_bigmm_cm(B, blk, B.i_qw, B.pool.view("xrb"), B.h_q,
                              bias_col=0)
            emit_mixer_r(B, blk, 1)
            B.pool.freeb("mmr1")
            emit_bigmm_cm(B, blk, B.i_kw, B.pool.view("xrb"), B.h_k, bias_col=1)
            emit_mixer_r(B, blk, 2)
            fr = ["mmr2", "w2sb", "xnres"]
            if prefix:
                fr.append("mmr0")
            B.pool.freeb(*fr)
            emit_vmm_tm(B, blk, B.pool.view("xrb"))
            B.pool.freeb("xrb")
            emit_wpath_tail(B, blk, prefix)
            emit_scan(B, blk, prefix)
            if prefix:
                emit_sinit(B, blk)
        box = {}
        ylnb = emit_yln(B, blk)
        emit_bigmm_cm(B, blk, B.i_ow, ylnb, B.h_s, bias_col=None,
                      tail_cb=lambda: box.__setitem__(
                          "ffn", emit_ffn_prefetch(B, blk)))
        B.pool.freeb("ylnb")
        emit_ffn(B, blk, pre=box["ffn"])
        B.barrier(blk)
    B.close()
    return B


_BUILD_CACHE = {}


def stage_weights(i):
    f32 = np.float32

    def cc(a):
        return np.ascontiguousarray(a, dtype=f32)

    qw = cc(i["L0_attn_q_w"]).reshape(16, 128, 16, 128).transpose(2, 0, 1, 3)
    kw = cc(i["L0_attn_k_w"]).reshape(16, 128, 16, 128).transpose(2, 0, 1, 3)
    ow = cc(i["L0_attn_o_w"]).reshape(16, 128, 16, 128).transpose(2, 0, 1, 3)
    vw = cc(i["L0_attn_v_w"]).reshape(16, 128, 4, 512).transpose(2, 0, 1, 3)
    w1 = cc(i["L0_attn_maa_w1"])
    w2 = np.stack([np.concatenate([cc(i["L0_attn_maa_w2"])[r],
                                   cc(i["L0_attn_" + n])[None, :]], axis=0)
                   for r, n in enumerate(["maa_r", "maa_k", "maa_v", "maa_w"])])
    td1 = cc(i["L0_attn_td_w1"])
    td2 = np.concatenate([cc(i["L0_attn_td_w2"]),
                          cc(i["L0_attn_time_decay"])[None, :]], axis=0)
    wgT = cc(i["L0_ffn_w_gate_up"]).T
    gate = np.ascontiguousarray(wgT[:, :8192]).reshape(16, 128, 64, 128)
    up = np.ascontiguousarray(wgT[:, 8192:]).reshape(16, 128, 64, 128)
    gut = np.ascontiguousarray(np.concatenate([gate, up], axis=3).transpose(2, 0, 1, 3))
    wdT = cc(i["L0_ffn_w_down"]).T
    dwt = np.ascontiguousarray(wdT.reshape(64, 128, 16, 128).transpose(2, 0, 1, 3))

    def col(name):
        return cc(i[name]).reshape(16, 128).T

    vecs = np.ascontiguousarray(np.concatenate(
        [col("L0_attn_q_b"), col("L0_attn_k_b"), col("L0_attn_v_b"),
         col("L0_attn_ln_g"), col("L0_attn_ln_b"), col("L0_attn_maa_x")], axis=1))
    vbrow = np.ascontiguousarray(np.broadcast_to(cc(i["L0_attn_v_b"])[None, :],
                                                 (128, 2048)))
    utm = (np.arange(128)[:, None] <= np.arange(128)[None, :]).astype(f32)
    ident = np.eye(128, dtype=f32)
    cmask = np.ones((128, T), f32)
    cmask[:, 0::CH] = 0.0
    return dict(qw_t=np.ascontiguousarray(qw), kw_t=np.ascontiguousarray(kw),
                ow_t=np.ascontiguousarray(ow), vw_t=np.ascontiguousarray(vw),
                w1=w1, w2t=w2, td1=td1, td2t=td2, gut=gut, dwt=dwt,
                vecs=vecs, vbrow=vbrow, utm=utm, ident=ident, cmask=cmask)


def run_cores(inputs, taps=(), phases=99, cores=8):
    key = (tuple(sorted(taps)), phases)
    if key not in _BUILD_CACHE:
        _BUILD_CACHE[key] = build_kernel(taps, phases)
    B = _BUILD_CACHE[key]
    shared = stage_weights(inputs)
    x = np.asarray(inputs["L0_attn_x"], dtype=np.float32)
    in_maps = []
    for core in range(cores):
        b, half = core // 2, core % 2
        xT = np.ascontiguousarray(x[b].T)
        m = dict(shared)
        m["xm"] = np.ascontiguousarray(xT[:, half * T:(half + 1) * T])
        m["xp"] = np.ascontiguousarray(xT[:, 0:T])
        m["sflag"] = np.full((128, 1), float(half), np.float32)
        in_maps.append(m)
    res = run_bass_kernel_spmd(B.nc, in_maps, core_ids=list(range(cores)))
    return B, res


def kernel(**inputs):
    _, res = run_cores(inputs, (), 99, 8)
    x = np.asarray(inputs["L0_attn_x"])
    out = np.empty((4, 4096, 2048), np.float32)
    for core in range(8):
        b, half = core // 2, core % 2
        out[b, half * T:(half + 1) * T, :] = res.results[core]["out"].T
    return out


# version 15 (final)
# speedup vs baseline: 1.0246x; 1.0246x over previous
import numpy as np
import ml_dtypes
import concourse.bass as bass
import concourse.mybir as mybir
from concourse.bass_utils import run_bass_kernel_spmd

F32 = mybir.dt.float32
BF16 = mybir.dt.bfloat16
AF = mybir.ActivationFunctionType
ALU = mybir.AluOpType

C, T, NT = 2048, 2048, 16
CH, NCH = 128, 16
LN8 = float(np.log(8.0))
EPS = 1e-5
BF = ml_dtypes.bfloat16


class Sem:
    __slots__ = ("h", "n")

    def __init__(self, h):
        self.h, self.n = h, 0


class Pool:
    def __init__(self, tensor, nelem):
        self.t, self.n = tensor, nelem
        self.free = [(0, nelem)]
        self.live = {}
        self.hwm = 0

    def alloc(self, name, nelem, dtype=BF16, parts=128):
        ne = (nelem * (2 if dtype != BF16 else 1) + 15) & ~15
        for i, (o, sz) in enumerate(self.free):
            if sz >= ne:
                if sz == ne:
                    del self.free[i]
                else:
                    self.free[i] = (o + ne, sz - ne)
                self.live[name] = (o, ne, dtype, parts)
                self.hwm = max(self.hwm, max(o2 + s2 for o2, s2, _, _ in self.live.values()))
                return self.view(name)
        raise MemoryError(f"pool overflow: {name} needs {ne}, free={self.free}, "
                          f"live={[(k, v[1]) for k, v in self.live.items()]}")

    def alloc_top(self, name, nelem, dtype=BF16, parts=128):
        ne = (nelem * (2 if dtype != BF16 else 1) + 15) & ~15
        for i in range(len(self.free) - 1, -1, -1):
            o, sz = self.free[i]
            no = ((o + sz - ne) // 16) * 16
            if no >= o:
                if no == o and o + sz == no + ne:
                    del self.free[i]
                else:
                    self.free[i] = (o, no - o)
                    if o + sz > no + ne:
                        self.free.insert(i + 1, (no + ne, o + sz - (no + ne)))
                self.live[name] = (no, ne, dtype, parts)
                self.hwm = max(self.hwm, max(o2 + s2 for o2, s2, _, _ in self.live.values()))
                return self.view(name)
        raise MemoryError(f"pool overflow(top): {name} needs {ne}, free={self.free}")

    def view(self, name):
        o, ne, dtype, parts = self.live[name]
        ap = self.t[0:parts, o:o + ne]
        if dtype != BF16:
            ap = ap.bitcast(dtype)
        return ap

    def freeb(self, *names):
        for name in names:
            o, ne, _, _ = self.live.pop(name)
            self.free.append((o, ne))
        self.free.sort()
        out = []
        for o, sz in self.free:
            if out and out[-1][0] + out[-1][1] == o:
                out[-1] = (out[-1][0], out[-1][1] + sz)
            else:
                out.append((o, sz))
        self.free = out



_WRAP_OPS = {
    "tensor_tensor", "tensor_scalar", "tensor_copy", "tensor_tensor_scan",
    "scalar_tensor_tensor", "reciprocal", "memset", "memzero", "select",
    "activation", "copy", "add", "mul", "square", "sqrt", "tensor_add",
    "tensor_sub", "tensor_mul", "tensor_max", "tensor_scalar_add",
    "tensor_scalar_mul", "tensor_scalar_max", "tensor_single_scalar",
}


class EngWrap:
    def __init__(self, eng, sem):
        self._e, self._sem = eng, sem

    def __getattr__(self, name):
        attr = getattr(self._e, name)
        if name in _WRAP_OPS:
            def call(*a, _attr=attr, **k):
                if self._sem.n:
                    self._e.wait_ge(self._sem.h, self._sem.n)
                ins = _attr(*a, **k)
                ins.then_inc(self._sem.h, 1)
                self._sem.n += 1
                return _SinkInstr()
            return call
        if name == "dma_start":
            def call(*a, _attr=attr, **k):
                if self._sem.n:
                    self._e.wait_ge(self._sem.h, self._sem.n)
                return _attr(*a, **k)
            return call
        return attr

    def milestone(self, sem, k=1):
        ins = self._e.wait_ge(self._sem.h, self._sem.n)
        ins.then_inc(sem.h, k)


class _SinkInstr:

    def then_inc(self, h, k):
        raise RuntimeError("use milestone() instead of then_inc on wrapped ops")


class BlkProxy:
    def __init__(self, blk, B):
        self._blk, self._B = blk, B

    def vector(self, f):
        return self._blk.vector(lambda v: f(EngWrap(v, self._B.sVq)))

    def scalar(self, f):
        return self._blk.scalar(lambda s: f(EngWrap(s, self._B.sSq)))

    def tensor(self, f):
        return self._blk.tensor(f)

    def gpsimd(self, f):
        return self._blk.gpsimd(lambda g: f(EngWrap(g, self._B.sGq)))

    def sync(self, f):
        return self._blk.sync(f)


class KB:
    def __init__(self, taps=(), phases=99):
        self.taps, self.phases = set(taps), phases
        self.nc = bass.Bass("TRN2", target_bir_lowering=False)
        self.ctx = []
        self.tapt = {}
        self.spar = 0

    def declare(self):
        d = self.nc.dram_tensor
        self.i_xm = d("xm", [C, T], F32, kind="ExternalInput").ap()
        self.i_xp = d("xp", [C, T], F32, kind="ExternalInput").ap()
        self.i_sflag = d("sflag", [128, 1], F32, kind="ExternalInput").ap()
        self.i_qw = d("qw_t", [NT, NT, 128, 128], F32, kind="ExternalInput").ap()
        self.i_kw = d("kw_t", [NT, NT, 128, 128], F32, kind="ExternalInput").ap()
        self.i_ow = d("ow_t", [NT, NT, 128, 128], F32, kind="ExternalInput").ap()
        self.i_vw = d("vw_t", [4, NT, 128, 512], F32, kind="ExternalInput").ap()
        self.i_w1 = d("w1", [C, 128], F32, kind="ExternalInput").ap()
        self.i_w2 = d("w2t", [4, 33, C], F32, kind="ExternalInput").ap()
        self.i_td1 = d("td1", [C, 64], F32, kind="ExternalInput").ap()
        self.i_td2 = d("td2t", [65, C], F32, kind="ExternalInput").ap()
        self.i_gut = d("gut", [64, NT, 128, 256], F32, kind="ExternalInput").ap()
        self.i_dwt = d("dwt", [NT, 64, 128, 128], F32, kind="ExternalInput").ap()
        self.i_vecs = d("vecs", [128, 6 * NT], F32, kind="ExternalInput").ap()
        self.i_vbrow = d("vbrow", [128, C], F32, kind="ExternalInput").ap()
        self.i_utm = d("utm", [128, 128], F32, kind="ExternalInput").ap()
        self.i_ident = d("ident", [128, 128], F32, kind="ExternalInput").ap()
        self.i_cmask = d("cmask", [128, T], F32, kind="ExternalInput").ap()
        self.o_out = d("out", [C, T], F32, kind="ExternalOutput").ap()
        self.h_xn = d("s_xn", [C, T + 1], BF16).ap()
        self.h_q = d("s_q", [C, T], BF16).ap()
        self.h_k = d("s_k", [C, T], BF16).ap()
        self.h_w = d("s_w", [C, T], F32).ap()
        self.h_v = d("s_v", [T, C], BF16).ap()
        self.h_s = d("s_s", [C, T], BF16).ap()
        self.h_y = d("s_y", [C, T], BF16).ap()
        self.h_k16 = d("s_k16", [C, T], BF16).ap()
        self.h_q16 = d("s_q16", [C, T], BF16).ap()
        self.h_kst = d("s_kst", [C, T], BF16).ap()
        self.h_btab = d("s_btab", [NT, 128, 128], F32).ap()
        for nm, sh, dt in [("xn", [C, T + 1], BF16), ("q", [C, T], BF16),
                           ("k", [C, T], BF16),
                           ("v", [T, C], BF16), ("s", [C, T], BF16),
                           ("y", [C, T], BF16), ("yln", [128, NT * T], BF16),
                           ("Sst", [128, 1024], F32)]:
            if nm in self.taps:
                self.tapt[nm] = d("tap_" + nm, sh, dt, kind="ExternalOutput").ap()

    def allocate(self):
        nc = self.nc

        def sb(name, shape, dtype):
            g = nc.sbuf_tensor(name, shape, dtype)
            t = g.__enter__()
            self.ctx.append(g)
            return t

        def mkpsum(name, shape, dtype=F32):
            g = nc.psum_tensor(name, shape, dtype)
            t = g.__enter__()
            self.ctx.append(g)
            return t

        def mksem(name):
            g = nc.semaphore(name)
            h = g.__enter__()
            self.ctx.append(g)
            return Sem(h)

        self.vecs = sb("g_vecs", [128, 6 * NT], F32)
        self.sflag = sb("g_sflag", [128, 1], F32)
        self.utm = sb("g_utm", [128, 128], BF16)
        self.ident = sb("g_ident", [128, 128], BF16)
        self.cmask = sb("g_cmask", [128, T], BF16)
        self.ones = sb("g_ones", [128, 1], BF16)
        self.onesrow = sb("g_onesrow", [1, 128], F32)
        self.cbln8 = sb("g_cbln8", [128, 1], F32)
        self.S = sb("g_S", [128, 3 * 1024], F32)
        self.Sbf = sb("g_Sbf", [128, 1024], BF16)
        self.bcol = sb("g_bcol", [128, NT], BF16)
        self.Btab = sb("g_btab", [128, NT * 9], F32)
        self.dtab = sb("g_dtab", [128, NT * 8], F32)
        self.esttab = sb("g_est", [128, NT * 8], BF16)
        self.eintab = sb("g_eint", [128, NT * 8], BF16)
        self.drow = sb("g_drow", [128, NT], F32)
        self.pool_t = sb("g_pool", [128, 95400], BF16)
        self.pool = Pool(self.pool_t, 95400)
        self.PA = mkpsum("p_A", [128, 2048], F32)
        self.PB = mkpsum("p_B", [128, 2048], F32)
        self.sT = mksem("sT")
        self.sV = mksem("sV")
        self.sS = mksem("sS")
        self.sG = mksem("sG")
        self.dsems = {}
        self.sVq = mksem("sVq")
        self.sSq = mksem("sSq")
        self.sGq = mksem("sGq")
        self._mksem = mksem

    def dsem(self, name):
        if name not in self.dsems:
            self.dsems[name] = self._mksem("d_" + name)
        return self.dsems[name]

    def close(self):
        for g in reversed(self.ctx):
            g.__exit__(None, None, None)

    def barrier(self, blk):
        B = self

        @blk.tensor
        def _(t):
            t.drain()
            t.nop(nofuse=True).then_inc(B.sT.h, 1)
        B.sT.n += 1

        @blk.gpsimd
        def _(g):
            g.nop(nofuse=True).then_inc(B.sG.h, 1)
        B.sG.n += 1

        tgt = [(B.sT.h, B.sT.n), (B.sVq.h, B.sVq.n), (B.sSq.h, B.sSq.n),
               (B.sGq.h, B.sGq.n),
               (B.sG.h, B.sG.n)] + [(d.h, d.n) for d in B.dsems.values()]
        for sec in ("tensor", "vector", "scalar", "gpsimd", "sync"):
            def f(e, tgt=tgt):
                for h, n in tgt:
                    e.wait_ge(h, n)
            getattr(blk, sec)(f)


def emit_p0(B, blk):
    cst = B.dsem("cst")

    @blk.gpsimd
    def _(g):
        def dma(out, in_):
            g.dma_start(out=out, in_=in_).then_inc(cst.h, 16)
            cst.n += 16
        dma(B.vecs[:, :], B.i_vecs)
        dma(B.sflag[:, :], B.i_sflag)
        dma(B.utm[:, :], B.i_utm)
        dma(B.ident[:, :], B.i_ident)
        dma(B.cmask[:, :], B.i_cmask)

    @blk.vector
    def _(v):
        v.memset(B.ones[:, :], 1.0)
        v.memset(B.onesrow[:, :], 1.0)
        v.memset(B.cbln8[:, :], -LN8)
        v.memset(B.Btab[:, :], 0.0)
        v.memset(B.S[:, :], 0.0)
        v.memset(B.Sbf[:, :], 0.0)

    B.barrier(blk)


def emit_w12_prefetch(B, blk):
    P = B.pool
    w1sb = P.alloc("w1sb", NT * 128, BF16)
    w2sb = P.alloc("w2sb", 4 * C, BF16, parts=33)
    wpre = B.dsem("wpre")

    @blk.gpsimd
    def _(g, wpre=wpre, pv=wpre.n):
        g.wait_ge(wpre.h, pv)
        g.dma_start(out=w2sb, in_=B.i_w2.rearrange("r p c -> p r c")).then_inc(wpre.h, 16)
        g.dma_start(out=w1sb, in_=B.i_w1.rearrange("(t p) m -> p t m", p=128)).then_inc(wpre.h, 16)
    wpre.n += 32
    return wpre.n


def emit_ln_p1(B, blk, xin, alloc=True):
    xin_t = xin.rearrange("(t p) n -> p t n", p=128)
    P = B.pool
    if alloc:
        P.alloc("xnres", NT * (T + 1), BF16)
        P.alloc("xrb", NT * T, BF16)
        P.alloc("kvx", 2 * T, F32)
        P.alloc("sqb", 2 * T, BF16)
    xnres, kvx, sqb = P.view("xnres"), P.view("kvx"), P.view("sqb")
    st = {"sS": [None, None], "sT": [None, None]}

    def tile(k):
        p = k % 2
        kvs = B.dsem(f"kvx{p}")

        @blk.sync
        def _(sy, k=k, p=p, kvs=kvs, pv=kvs.n, wt=st["sS"][p]):
            sy.wait_ge(kvs.h, pv)
            if wt is not None:
                sy.wait_ge(B.sSq.h, wt)
            sy.dma_start(out=kvx[:, p * T:(p + 1) * T], in_=xin_t[:, k, :]) \
                .then_inc(kvs.h, 16)
        kvs.n += 16
        dx_t = kvs.n
        base = k * (T + 1)

        @blk.scalar
        def _(s, p=p, kvs=kvs, dx_t=dx_t, wt=st["sT"][p], base=base):
            s.wait_ge(kvs.h, dx_t)
            if wt is not None:
                s.wait_ge(B.sT.h, wt)
            s.copy(xnres[:, base + 1:base + T + 1], kvx[:, p * T:(p + 1) * T])
            s.activation(sqb[:, p * T:(p + 1) * T],
                         xnres[:, base + 1:base + T + 1], AF.Square)
        st["sS"][p] = B.sSq.n
        sv = B.sSq.n

        @blk.tensor
        def _(t, k=k, p=p, sv=sv, base=base):
            t.wait_ge(B.sSq.h, sv)
            for n in range(4):
                last = t.matmul(B.PA[0:1, n * 512:(n + 1) * 512], B.ones[:, :],
                         xnres[:, base + 1 + n * 512:base + 1 + (n + 1) * 512],
                         start=(k == 0), stop=(k == NT - 1))
                last = t.matmul(B.PA[64:65, n * 512:(n + 1) * 512], B.ones[:, :],
                         sqb[:, p * T + n * 512:p * T + (n + 1) * 512],
                         start=(k == 0), stop=(k == NT - 1),
                         tile_position=(0, 64))
            t.drain()
            t.nop(nofuse=True).then_inc(B.sT.h, 1)
        B.sT.n += 1
        st["sT"][p] = B.sT.n
    return tile


def emit_ln(B, blk, xin, prefix, p1_done=False):
    P = B.pool
    if not p1_done:
        tile = emit_ln_p1(B, blk, xin, alloc=True)
        for k in range(NT):
            tile(k)
    xnres = P.view("xnres")
    xrbv = P.view("xrb")
    srow = xrbv[:, 0:3 * 2 * T].bitcast(F32)
    abrow = xrbv[:, 6 * T:10 * T].bitcast(F32)
    bcast = xrbv[:, 10 * T:14 * T].bitcast(F32)
    scrf = xrbv[:, 14 * T:16 * T].bitcast(F32)

    def xo(k):
        return xnres[:, k * (T + 1):(k + 1) * (T + 1)]

    emit_stats_tail(B, blk, srow, abrow, bcast,
                    B.PA[0:1, :], B.PA[64:65, :])

    bc_v = B.sVq.n
    for k in range(NT):
        p = k % 2
        xsp = B.dsem(f"xnsp{p}")

        @blk.vector
        def _(v, k=k, bc_v=bc_v):
            v.wait_ge(B.sVq.h, bc_v)
            xk = xo(k)
            v.tensor_tensor(scrf[:, :], xk[:, 1:T + 1], bcast[:, 0:T], ALU.mult)
            v.tensor_tensor(xk[:, 1:T + 1], scrf[:, :], bcast[:, T:2 * T],
                            ALU.add)
            if prefix:
                v.memset(xk[:, 0:1], 0.0)
                v.tensor_copy(B.bcol[:, k:k + 1], xk[:, T:T + 1])
            else:
                v.tensor_scalar(xk[:, 0:1], B.bcol[:, k:k + 1], B.sflag[:, 0:1],
                                None, ALU.mult)
        vv2 = B.sVq.n

        if not prefix:
            @blk.scalar
            def _(s, k=k, p=p, xsp=xsp, vv2=vv2):
                s.wait_ge(B.sVq.h, vv2)
                s.dma_start(out=B.h_xn.rearrange("(t p) n -> p t n", p=128)[:, k, :],
                            in_=xo(k)).then_inc(xsp.h, 16)
            xsp.n += 16

    B.barrier(blk)
    P.freeb("kvx", "sqb")


def emit_stats_tail(B, blk, srow, abrow, bcast, srcX=None, srcQ=None):
    tv = B.sT.n
    if srcX is None:
        srcX, srcQ = B.PA[0:1, :], B.PB[0:1, :]

    @blk.vector
    def _(v, tv=tv):
        v.wait_ge(B.sT.h, tv)
        sx, e2, mu = srow[0:1, 0:T], srow[0:1, T:2 * T], srow[0:1, 2 * T:3 * T]
        v.tensor_copy(sx, srcX)
        v.tensor_copy(e2, srcQ)
        v.tensor_scalar(mu, sx, 1.0 / C, None, ALU.mult)
        v.tensor_scalar(e2, e2, 1.0 / C, None, ALU.mult)
        v.tensor_tensor(sx, mu, mu, ALU.mult)
        v.tensor_tensor(e2, e2, sx, ALU.subtract)
        v.tensor_scalar(e2, e2, EPS, None, ALU.add)
    vv = B.sVq.n

    @blk.scalar
    def _(s, vv=vv):
        s.wait_ge(B.sVq.h, vv)
        s.activation(srow[0:1, 0:T], srow[0:1, T:2 * T], AF.Ln)
        s.activation(abrow[0:1, 0:T], srow[0:1, 0:T], AF.Exp, scale=-0.5)
    sv = B.sSq.n

    @blk.vector
    def _(v, sv=sv):
        v.wait_ge(B.sSq.h, sv)
        v.tensor_tensor(abrow[0:1, T:2 * T], srow[0:1, 2 * T:3 * T],
                        abrow[0:1, 0:T], ALU.mult)
        v.tensor_scalar(abrow[0:1, T:2 * T], abrow[0:1, T:2 * T], -1.0, None,
                        ALU.mult)
    vv = B.sVq.n

    @blk.tensor
    def _(t, vv=vv):
        t.wait_ge(B.sVq.h, vv)
        for n in range(4):
            last = t.matmul(B.PA[:, n * 512:(n + 1) * 512], B.onesrow[:, :],
                     abrow[0:1, n * 512:(n + 1) * 512], start=True, stop=True)
            last = t.matmul(B.PB[:, n * 512:(n + 1) * 512], B.onesrow[:, :],
                     abrow[0:1, T + n * 512:T + (n + 1) * 512], start=True, stop=True)
        t.drain()
        t.nop(nofuse=True).then_inc(B.sT.h, 1)
    B.sT.n += 1
    tv = B.sT.n

    @blk.vector
    def _(v, tv=tv):
        v.wait_ge(B.sT.h, tv)
        v.tensor_copy(bcast[:, 0:T], B.PA[:, :])
        v.tensor_copy(bcast[:, T:2 * T], B.PB[:, :])


def emit_mm(B, blk, w_dma):
    P = B.pool
    xnres = P.view("xnres")
    w1sb = P.view("w1sb")
    td1sb = P.alloc("td1sb", NT * 64, BF16)
    td2sb = P.alloc("td2sb", C, BF16, parts=65)
    scrA = P.alloc("scrA", T, BF16)
    scrB = P.alloc("scrB", 2 * T, BF16)
    mmb = scrB[:, 0:T]
    for r in range(4):
        P.alloc_top(f"mmr{r}", T, BF16, parts=33)

    wpre = B.dsem("wpre")

    @blk.gpsimd
    def _(g, wpre=wpre, pv=wpre.n):
        g.wait_ge(wpre.h, pv)
        g.dma_start(out=td1sb, in_=B.i_td1.rearrange("(t p) m -> p t m", p=128)).then_inc(wpre.h, 16)
        g.dma_start(out=td2sb, in_=B.i_td2).then_inc(wpre.h, 16)
    wpre.n += 32
    td_dma = wpre.n

    sT_done = [None, None]
    for k in range(NT):
        p = k % 2
        xo = xnres[:, k * (T + 1):(k + 1) * (T + 1)]

        @blk.vector
        def _(v, k=k, p=p, xo=xo, wt=sT_done[p]):
            if wt is not None:
                v.wait_ge(B.sT.h, wt)
            v.tensor_tensor(scrA[:, :], xo[:, 0:T], xo[:, 1:T + 1], ALU.subtract)
            v.scalar_tensor_tensor(scrB[:, p * T:(p + 1) * T], scrA[:, :],
                                   B.vecs[:, 5 * NT + k:5 * NT + k + 1],
                                   xo[:, 1:T + 1], ALU.mult, ALU.add)
        vv = B.sVq.n

        @blk.tensor
        def _(t, k=k, p=p, vv=vv, wpre=wpre, w_dma=w_dma):
            t.wait_ge(B.sVq.h, vv)
            if k == 0:
                t.wait_ge(wpre.h, w_dma)
            for n in range(4):
                last = t.matmul(B.PA[:, n * 512:(n + 1) * 512],
                         w1sb[:, k * 128:(k + 1) * 128],
                         scrB[:, p * T + n * 512:p * T + (n + 1) * 512],
                         start=(k == 0), stop=(k == NT - 1))
            t.drain()
            t.nop(nofuse=True).then_inc(B.sT.h, 1)
        B.sT.n += 1
        sT_done[p] = B.sT.n

    tv = B.sT.n

    @blk.scalar
    def _(s, tv=tv):
        s.wait_ge(B.sT.h, tv)
        s.activation(mmb[:, :], B.PA[:, :], AF.Tanh)
    sv = B.sSq.n

    mms = B.dsem("mmr")

    @blk.gpsimd
    def _(g, sv=sv, mms=mms, pv=mms.n):
        g.wait_ge(mms.h, pv)
        g.wait_ge(B.sSq.h, sv)
        for r in range(4):
            g.dma_start(out=B.pool.view(f"mmr{r}")[0:32, :],
                        in_=mmb[32 * r:32 * (r + 1), :]).then_inc(mms.h, 16)
    mms.n += 4 * 16
    gv = mms.n

    @blk.vector
    def _(v, gv=gv, mms=mms):
        v.wait_ge(mms.h, gv)
        for r in range(4):
            v.memset(B.pool.view(f"mmr{r}")[32:33, :], 1.0)

    B.barrier(blk)
    P.freeb("w1sb", "scrA", "scrB")
    return td_dma


def emit_mixer_r(B, blk, r):
    P = B.pool
    xnres = P.view("xnres")
    xrb = P.view("xrb")
    mv = P.view(f"mmr{r}")
    w2v = P.view("w2sb")
    scrA = P.alloc("mxA", T, BF16)
    mscr = P.alloc("mxB", 2 * T, BF16)
    scrC = P.alloc("mxC", T, BF16)
    sS_par = [None, None]
    sV_par = [None, None]
    for c in range(NT):
        p = c % 2
        PP = B.PA if p == 0 else B.PB
        xo = xnres[:, c * (T + 1):(c + 1) * (T + 1)]

        @blk.tensor
        def _(t, c=c, PP=PP, wts=sS_par[p]):
            if wts is not None:
                t.wait_ge(B.sSq.h, wts)
            for n in range(4):
                last = t.matmul(PP[:, n * 512:(n + 1) * 512],
                         w2v[:, r * C + c * 128:r * C + (c + 1) * 128],
                         mv[:, n * 512:(n + 1) * 512], start=True, stop=True)
            last.then_inc(B.sT.h, 1)
        B.sT.n += 1
        tv = B.sT.n

        @blk.scalar
        def _(s, p=p, PP=PP, tv=tv, wv=sV_par[p]):
            s.wait_ge(B.sT.h, tv)
            if wv is not None:
                s.wait_ge(B.sVq.h, wv)
            s.copy(mscr[:, p * T:(p + 1) * T], PP[:, :])
        sS_par[p] = B.sSq.n
        sv = B.sSq.n

        @blk.vector
        def _(v, c=c, p=p, xo=xo, sv=sv):
            v.wait_ge(B.sSq.h, sv)
            v.tensor_tensor(scrA[:, :], xo[:, 0:T], xo[:, 1:T + 1], ALU.subtract)
            v.tensor_tensor(scrC[:, :], scrA[:, :], mscr[:, p * T:(p + 1) * T],
                            ALU.mult)
            v.tensor_tensor(xrb[:, c * T:(c + 1) * T], scrC[:, :], xo[:, 1:T + 1],
                            ALU.add)
        sV_par[p] = B.sVq.n

    B.barrier(blk)
    P.freeb("mxA", "mxB", "mxC")


def bigmm_prefetch(B, blk, w_in):
    P = B.pool
    wslab = P.alloc("wslab", 2 * NT * 128, BF16)
    pre_t = {}
    for m in (0, 1):
        wss = B.dsem(f"wsl{m % 2}")

        @blk.gpsimd
        def _(g, m=m, wss=wss, pv=wss.n):
            g.wait_ge(wss.h, pv)
            g.dma_start(out=wslab[:, (m % 2) * NT * 128:((m % 2) + 1) * NT * 128],
                        in_=w_in[m].rearrange("k p c -> p k c")).then_inc(wss.h, 16)
        wss.n += 16
        pre_t[m] = wss.n
    return (wslab, 2, pre_t)


def emit_bigmm_cm(B, blk, w_in, src_sb, spill_to, bias_col, pre=None,
                  tail_cb=None):
    P = B.pool
    if pre is None:
        wslab = P.alloc("wslab", 2 * NT * 128, BF16)
        npre, pre_t = 0, {}
    else:
        wslab, npre, pre_t = pre
    evb = P.alloc("evb", T, BF16)
    sT_done = [None, None]
    sS_done = [None, None]
    spill_last = None
    for m in range(NT):
        p = m % 2
        PP = B.PA if p == 0 else B.PB

        wss = B.dsem(f"wsl{p}")
        if m >= npre:
            @blk.gpsimd
            def _(g, m=m, p=p, wss=wss, pv=wss.n, wt=sT_done[p]):
                g.wait_ge(wss.h, pv)
                if wt is not None:
                    g.wait_ge(B.sT.h, wt)
                g.dma_start(out=wslab[:, p * NT * 128:(p + 1) * NT * 128],
                            in_=w_in[m].rearrange("k p c -> p k c")).then_inc(wss.h, 16)
            wss.n += 16
            dw_t = wss.n
        else:
            dw_t = pre_t[m]

        @blk.tensor
        def _(t, m=m, p=p, PP=PP, wss=wss, dw_t=dw_t, wt=sS_done[p]):
            t.wait_ge(wss.h, dw_t)
            if wt is not None:
                t.wait_ge(B.sSq.h, wt)
            for kk in range(NT):
                for n in range(4):
                    last = t.matmul(PP[:, n * 512:(n + 1) * 512],
                             wslab[:, p * NT * 128 + kk * 128:p * NT * 128 + (kk + 1) * 128],
                             src_sb[:, kk * T + n * 512:kk * T + (n + 1) * 512],
                             start=(kk == 0), stop=(kk == NT - 1))
            last.then_inc(B.sT.h, 1)
        B.sT.n += 1
        sT_done[p] = B.sT.n
        tv = B.sT.n

        sps = B.dsem(f"spl{p}")

        @blk.scalar
        def _(s, m=m, PP=PP, sps=sps, tv=tv, wsp=spill_last):
            s.wait_ge(B.sT.h, tv)
            if wsp is not None:
                s.wait_ge(wsp[0].h, wsp[1])
            if bias_col is not None:
                s.activation(evb[:, :], PP[:, :], AF.Identity,
                             bias=B.vecs[:, bias_col * NT + m:bias_col * NT + m + 1])
            else:
                s.copy(evb[:, :], PP[:, :])
            s.dma_start(out=spill_to.rearrange("(t p) n -> p t n", p=128)[:, m, :],
                        in_=evb[:, :]).then_inc(sps.h, 16)
        sps.n += 16
        sS_done[p] = B.sSq.n
        spill_last = (sps, sps.n)

    if tail_cb is not None:
        tail_cb()
    B.barrier(blk)
    P.freeb("wslab", "evb")


def emit_vmm_tm(B, blk, xrb):
    P = B.pool
    vwslab = P.alloc("vwslab", 2 * NT * 512, BF16)
    vbr = P.alloc("vbr", C, F32)
    evb = P.alloc("evb", 2 * 512, BF16)

    wpre = B.dsem("wpre")

    @blk.gpsimd
    def _(g, wpre=wpre):
        g.dma_start(out=vbr, in_=B.i_vbrow).then_inc(wpre.h, 16)
    wpre.n += 16
    vbr_t = wpre.n

    oc_done = [None, None]
    sV_done = [None, None]
    sp_done = [None, None]
    for oc in range(4):
        po = oc % 2

        wss = B.dsem(f"wsl{po}")

        @blk.gpsimd
        def _(g, oc=oc, po=po, wss=wss, pv=wss.n, wt=oc_done[po]):
            g.wait_ge(wss.h, pv)
            if wt is not None:
                g.wait_ge(B.sT.h, wt)
            g.dma_start(out=vwslab[:, po * NT * 512:(po + 1) * NT * 512],
                        in_=B.i_vw[oc].rearrange("k p c -> p k c")).then_inc(wss.h, 16)
        wss.n += 16
        dw_t = wss.n

        for tt in range(NT):
            pt = tt % 2
            PP = B.PA if pt == 0 else B.PB

            @blk.tensor
            def _(t, po=po, tt=tt, PP=PP, wss=B.dsem(f"wsl{po}"), dw_t=dw_t,
                  vbr_t=vbr_t, wpre=wpre, wt=sV_done[pt]):
                t.wait_ge(wss.h, dw_t)
                if wt is not None:
                    t.wait_ge(B.sVq.h, wt)
                for kk in range(NT):
                    last = t.matmul(PP[:, 0:512],
                             xrb[:, kk * T + tt * 128:kk * T + (tt + 1) * 128],
                             vwslab[:, po * NT * 512 + kk * 512:po * NT * 512 + (kk + 1) * 512],
                             start=(kk == 0), stop=(kk == NT - 1))
                last.then_inc(B.sT.h, 1)
            B.sT.n += 1
            tv = B.sT.n

            @blk.vector
            def _(v, oc=oc, pt=pt, PP=PP, tv=tv, sps=B.dsem(f"spl{pt}"),
                  wpre=wpre, vbr_t=vbr_t, wt=sp_done[pt]):
                v.wait_ge(B.sT.h, tv)
                v.wait_ge(wpre.h, vbr_t)
                if wt is not None:
                    v.wait_ge(sps.h, wt)
                v.tensor_tensor(evb[:, pt * 512:(pt + 1) * 512], PP[:, 0:512],
                                vbr[:, oc * 512:(oc + 1) * 512], ALU.add)
            sV_done[pt] = B.sVq.n
            vv = B.sVq.n

            sps = B.dsem(f"spl{pt}")

            @blk.scalar
            def _(s, oc=oc, tt=tt, pt=pt, sps=sps, vv=vv):
                s.wait_ge(B.sVq.h, vv)
                s.dma_start(out=B.h_v[tt * 128:(tt + 1) * 128, oc * 512:(oc + 1) * 512],
                            in_=evb[:, pt * 512:(pt + 1) * 512]).then_inc(sps.h, 16)
            sps.n += 16
            sp_done[pt] = sps.n
        oc_done[po] = B.sT.n

    B.barrier(blk)
    P.freeb("vwslab", "vbr", "evb")


def emit_wpath_head(B, blk, td_dma):
    P = B.pool
    xrb = P.view("xrb")
    td1sb = P.view("td1sb")
    ubuf = P.alloc("ubuf", T, BF16, parts=65)
    wpre = B.dsem("wpre")

    @blk.vector
    def _(v):
        v.memset(ubuf[64:65, :], 1.0)
    u1 = B.sVq.n

    @blk.tensor
    def _(t, u1=u1, wpre=wpre, td_dma=td_dma):
        t.wait_ge(B.sVq.h, u1)
        t.wait_ge(wpre.h, td_dma)
        for kk in range(NT):
            for n in range(4):
                last = t.matmul(B.PA[0:64, n * 512:(n + 1) * 512],
                         td1sb[:, kk * 64:(kk + 1) * 64],
                         xrb[:, kk * T + n * 512:kk * T + (n + 1) * 512],
                         start=(kk == 0), stop=(kk == NT - 1))
        last.then_inc(B.sT.h, 1)
    B.sT.n += 1
    tv_td1 = B.sT.n

    @blk.scalar
    def _(s, tv=tv_td1):
        s.wait_ge(B.sT.h, tv)
        s.activation(ubuf[0:64, :], B.PA[0:64, :], AF.Tanh)

    B.barrier(blk)
    P.freeb("td1sb")


def emit_wpath_tail(B, blk, prefix):
    P = B.pool
    ubuf = P.view("ubuf")
    td2sb = P.view("td2sb")
    scrF = P.alloc("scrF", 2 * T, F32)
    evf = P.alloc("evf", 2 * T, F32)
    bfr = P.alloc("bfr", 2 * T, F32)
    fek = P.alloc("fek", 2 * T, BF16)
    fonem = P.alloc("fonem", T, BF16)
    btr = P.alloc("btr", 4 * 128, F32)
    krow = P.alloc("krow", 4 * T, BF16)
    few = P.alloc("few", 2 * T, F32)
    if not prefix:
        fqs = P.alloc("fqs", 2 * T, F32)
        feq = P.alloc("feq", 2 * T, BF16)
        btrp = P.alloc("btrp", 128, F32)
        qrow = P.alloc("qrow", 4 * T, BF16)

        @blk.vector
        def _(v):
            v.memset(btrp[:, :], 0.0)

    k_t = B.h_k.rearrange("(t p) n -> p t n", p=128)
    q_t = B.h_q.rearrange("(t p) n -> p t n", p=128)
    ko_t = (B.h_kst if prefix else B.h_k16).rearrange("(t p) n -> p t n", p=128)
    q16_t = B.h_q16.rearrange("(t p) n -> p t n", p=128)

    def sl(buf, p):
        return buf[:, p * T:(p + 1) * T]

    sXd = [None, None]
    sS_done = [None, None]
    vAd = [None, None]
    fsp_done = [None, None, None, None]
    ld_tgt = {}
    va_c = {}

    def sl4(buf, q):
        return buf[:, q * T:(q + 1) * T]

    def emit_tail(c):
        p = c % 2
        q = c % 4
        krs = B.dsem(f"krw{q}")
        fsps = B.dsem(f"fsp{q}")

        @blk.scalar
        def _(s, p=p, va=va_c[c]):
            s.wait_ge(B.sVq.h, va)
            s.activation(sl(few, p), sl(evf, p), AF.Exp)
            s.activation(sl(fek, p), sl(bfr, p), AF.Exp)
            if not prefix:
                s.activation(sl(feq, p), sl(fqs, p), AF.Exp, bias=B.cbln8[:, 0:1])
        sS_done[p] = B.sSq.n
        sz = B.sSq.n

        @blk.vector
        def _(v, p=p, q=q, sz=sz, krs=krs, lt=ld_tgt[c]):
            v.wait_ge(B.sSq.h, sz)
            v.wait_ge(krs.h, lt)
            v.tensor_scalar(fonem[:, :], sl(few, p), -1.0, 1.0, ALU.mult, ALU.add)
            v.tensor_tensor(fonem[:, :], fonem[:, :], sl(fek, p), ALU.mult)
            v.tensor_tensor(sl4(krow, q), sl4(krow, q), fonem[:, :], ALU.mult)
            if not prefix:
                v.tensor_tensor(sl4(qrow, q), sl4(qrow, q), sl(feq, p), ALU.mult)
        vc_ = B.sVq.n

        @blk.gpsimd
        def _(g, c=c, q=q, fsps=fsps, vc_=vc_, pv=fsps.n):
            g.wait_ge(fsps.h, pv)
            g.wait_ge(B.sVq.h, vc_)
            g.dma_start(out=ko_t[:, c, :], in_=sl4(krow, q)).then_inc(fsps.h, 16)
            g.dma_start(out=B.h_btab[c],
                        in_=btr[:, q * 128:(q + 1) * 128]).then_inc(fsps.h, 16)
            if not prefix:
                g.dma_start(out=q16_t[:, c, :], in_=sl4(qrow, q)).then_inc(fsps.h, 16)
        fsps.n += 16 * (2 if prefix else 3)
        fsp_done[q] = fsps.n

    for c in range(NT):
        p = c % 2
        q = c % 4
        PP = B.PA if p == 0 else B.PB
        krs = B.dsem(f"krw{q}")
        fsps = B.dsem(f"fsp{q}")

        @blk.sync
        def _(sy, c=c, q=q, krs=krs, pv=krs.n, fsps=fsps, wt=fsp_done[q]):
            sy.wait_ge(krs.h, pv)
            if wt is not None:
                sy.wait_ge(fsps.h, wt)
            sy.dma_start(out=sl4(krow, q), in_=k_t[:, c, :]).then_inc(krs.h, 16)
            if not prefix:
                sy.dma_start(out=sl4(qrow, q), in_=q_t[:, c, :]).then_inc(krs.h, 16)
        krs.n += 16 * (1 if prefix else 2)
        ld_tgt[c] = krs.n

        @blk.tensor
        def _(t, c=c, PP=PP, wt=sXd[p]):
            if wt is not None:
                t.wait_ge(B.sSq.h, wt)
            for n in range(4):
                last = t.matmul(PP[:, n * 512:(n + 1) * 512],
                         td2sb[:, c * 128:(c + 1) * 128],
                         ubuf[:, n * 512:(n + 1) * 512], start=True, stop=True)
            last.then_inc(B.sT.h, 1)
        B.sT.n += 1
        tv = B.sT.n

        @blk.scalar
        def _(s, p=p, PP=PP, tv=tv, wt=vAd[p]):
            s.wait_ge(B.sT.h, tv)
            if wt is not None:
                s.wait_ge(B.sVq.h, wt)
            s.activation(scrF[:, p * T:(p + 1) * T], PP[:, :], AF.Exp)
        sXd[p] = B.sSq.n
        sx = B.sSq.n

        @blk.vector
        def _(v, c=c, p=p, q=q, sx=sx, ws=sS_done[p], fsps=fsps, wg=fsp_done[q]):
            v.wait_ge(B.sSq.h, sx)
            if ws is not None:
                v.wait_ge(B.sSq.h, ws)
            if wg is not None:
                v.wait_ge(fsps.h, wg)
            v.tensor_scalar(sl(evf, p), scrF[:, p * T:(p + 1) * T], -1.0, -5.0,
                            ALU.mult, ALU.max)
            v.tensor_tensor_scan(sl(bfr, p), B.cmask[:, :], sl(evf, p), 0.0,
                                 ALU.mult, ALU.add)
            v.tensor_copy(btr[:, q * 128:(q + 1) * 128].rearrange(
                              "p (a b) -> p a b", b=1),
                          sl(bfr, p).rearrange("p (a b) -> p a b", b=16)[:, :, 15:16])
            b4 = sl(bfr, p).rearrange("p (k a b) -> p k a b", a=8, b=16)
            if prefix:
                bt8 = btr[:, q * 128:(q + 1) * 128] \
                    .rearrange("p (k a) -> p k a", a=8)[:, :, 7:8] \
                    .unsqueeze(-1).broadcast_to((128, NT, 8, 16))
                v.tensor_tensor(b4, bt8, b4, ALU.subtract)
            else:
                bt3 = btr[:, q * 128:(q + 1) * 128].rearrange("p (k a) -> p k a", a=8)
                bp3 = btrp[:, :].rearrange("p (k a) -> p k a", a=8)
                v.tensor_copy(bp3[:, :, 1:8], bt3[:, :, 0:7])
                bp4 = btrp[:, :] \
                    .rearrange("p (k a) -> p k a", a=8) \
                    .unsqueeze(-1).broadcast_to((128, NT, 8, 16))
                v.tensor_tensor(sl(fqs, p).rearrange("p (k a b) -> p k a b",
                                                     a=8, b=16),
                                b4, bp4, ALU.subtract)
                bt4 = btr[:, q * 128:(q + 1) * 128] \
                    .rearrange("p (k a) -> p k a", a=8) \
                    .unsqueeze(-1).broadcast_to((128, NT, 8, 16))
                v.tensor_tensor(b4, bt4, b4, ALU.subtract)
        vAd[p] = B.sVq.n
        va_c[c] = B.sVq.n

        if c > 0:
            emit_tail(c - 1)
    emit_tail(NT - 1)

    B.barrier(blk)
    names = ["ubuf", "td2sb", "scrF", "evf", "bfr", "few", "fek", "fonem",
             "btr", "krow"]
    if not prefix:
        names += ["fqs", "feq", "btrp", "qrow"]
    P.freeb(*names)


def emit_scan(B, blk, prefix, chunk_cb=None):
    P = B.pool
    ko_t = (B.h_kst if prefix else B.h_k16).rearrange("(t p) n -> p t n", p=128)
    q16_t = B.h_q16.rearrange("(t p) n -> p t n", p=128)
    y_t = B.h_y.rearrange("(t p) n -> p t n", p=128)

    strk = P.alloc("strk", 2 * T, BF16)
    strv = P.alloc("strv", 2 * T, BF16)
    btres = P.alloc("btres", NT * 128, F32)
    khT = P.alloc("khT", 2 * 256, BF16)
    drowp = [P.alloc(f"drow{p}", NT, F32) for p in range(2)]
    strkv = [strk[:, 0:T], strk[:, T:2 * T]]
    if not prefix:
        strq = P.alloc("strq", 2 * T, BF16)
        bt9 = P.alloc("bt9", 2 * NT * 9, F32)
        tabin = P.alloc("tabin", 2 * 640, F32)
        tabex = P.alloc("tabex", 2 * 640, BF16)
        fkst = [P.alloc(f"fkst{p}", T, BF16) for p in range(2)]
        fqh = [P.alloc(f"fqh{p}", T, BF16) for p in range(2)]
        fkd = [P.alloc(f"fkd{p}", T, BF16) for p in range(2)]
        fq32 = [P.alloc(f"fq32{p}", NT * 64, BF16) for p in range(2)]
        fk32 = [P.alloc(f"fk32{p}", NT * 64, BF16) for p in range(2)]
        fq64 = [P.alloc(f"fq64{p}", NT * 64, BF16) for p in range(2)]
        fk64 = [P.alloc(f"fk64{p}", NT * 64, BF16) for p in range(2)]
        Asb = P.alloc("Asb", 2 * 512, BF16)
        ybc = P.alloc("ybc", 2 * T, BF16)
        strqv = [strq[:, 0:T], strq[:, T:2 * T]]
        bt9r = bt9.rearrange("p (q t c) -> p q t c", q=2, c=9)
        tabinv = [tabin[:, 0:640], tabin[:, 640:1280]]
        tabexv = [tabex[:, 0:640], tabex[:, 640:1280]]
    btres_r = btres.rearrange("p (m c) -> p m c", c=128)

    pA = None
    pO = [B.PA[:, 1024:1280], B.PA[:, 1536:1792]]
    pST = B.PB[:, 0:1024]
    pTRb = B.PB[:, 1024:2048].bitcast(BF16)
    pTR = [pTRb[:, 0:256], pTRb[:, 1024:1280]]

    war_t = [None, None]
    sup_v = [None]
    exps_done = [None, None]
    fact_v = {}
    fact_k = {}
    fact_s = {}
    ld_tgt = {}
    ysp = B.dsem("ysp")
    bts = B.dsem("bts")

    @blk.gpsimd
    def _(g, bts=bts, pv=bts.n):
        g.wait_ge(bts.h, pv)
        g.dma_start(out=btres[:, :],
                    in_=B.h_btab.rearrange("m p c -> p m c")).then_inc(bts.h, 16)
    bts.n += 16
    bt_t = bts.n

    if not prefix:
        @blk.vector
        def _(v):
            v.memset(bt9r[:, :, :, 0:1], 0.0)

    def loads(g, ci):
        p = ci % 2
        scs = B.dsem(f"scn{p}")
        g.wait_ge(scs.h, scs.n)
        if war_t[p] is not None:
            g.wait_ge(B.sT.h, war_t[p])
        if not prefix:
            g.dma_start(out=strqv[p], in_=q16_t[:, :, ci * CH:(ci + 1) * CH]) \
                .then_inc(scs.h, 16)
            scs.n += 16
        g.dma_start(out=strkv[p], in_=ko_t[:, :, ci * CH:(ci + 1) * CH]) \
            .then_inc(scs.h, 16)
        g.dma_start(out=strv[:, p * T:(p + 1) * T],
                    in_=B.h_v[ci * CH:(ci + 1) * CH, :]).then_inc(scs.h, 16)
        scs.n += 32

    def factors_p1(ci):
        p = ci % 2
        tabv = None
        if prefix:
            fact_k[ci] = None
            fact_v[ci] = None
        else:
            @blk.vector
            def _(v, ci=ci, p=p, bt_t=bt_t, bts=bts, ws=exps_done[p]):
                v.wait_ge(bts.h, bt_t)
                if ws is not None:
                    v.wait_ge(B.sSq.h, ws)
                bt = bt9r[:, p]
                v.tensor_copy(bt[:, :, 1:9], btres_r[:, :, ci * 8:(ci + 1) * 8])
                tbi = tabinv[p]
                v.tensor_tensor(tbi[:, 0:128].rearrange("p (t c) -> p t c", c=8),
                                bt[:, :, 8:9].broadcast_to((128, NT, 8)),
                                bt[:, :, 1:9], ALU.subtract)
                v.tensor_tensor(tbi[:, 128:192].rearrange("p (t c) -> p t c", c=4),
                                bt[:, :, 4:8],
                                bt[:, :, 4:5].broadcast_to((128, NT, 4)),
                                ALU.subtract)
                v.tensor_tensor(tbi[:, 192:256].rearrange("p (t c) -> p t c", c=4),
                                bt[:, :, 4:5].broadcast_to((128, NT, 4)),
                                bt[:, :, 1:5], ALU.subtract)
                v.tensor_tensor(tbi[:, 256:384].rearrange("p (t c) -> p t c", c=8),
                                bt[:, :, 0:8], bt[:, :, 1:9], ALU.subtract)
                v.tensor_tensor(tbi[:, 384:512].rearrange("p (t c) -> p t c", c=8),
                                bt[:, :, 1:9], bt[:, :, 0:8], ALU.subtract)
                v.tensor_copy(tbi[:, 512:640].rearrange("p (t c) -> p t c", c=8),
                              bt[:, :, 0:8])
            tabv = B.sVq.n

        @blk.scalar
        def _(s, ci=ci, p=p, tabv=tabv, wv=sup_v[0], bts=bts, bt_t=bt_t):
            s.wait_ge(bts.h, bt_t)
            if tabv is not None:
                s.wait_ge(B.sVq.h, tabv)
            if wv is not None:
                s.wait_ge(B.sVq.h, wv)
            if not prefix:
                s.activation(tabexv[p], tabinv[p], AF.Exp)
            s.activation(drowp[p][:, :],
                         btres_r[:, :, ci * 8 + 7:ci * 8 + 8]
                         .rearrange("p m c -> p (m c)"), AF.Exp)
        exps_done[p] = B.sSq.n
        fact_s[ci] = B.sSq.n

    def factors_p2a(ci):
        p = ci % 2
        scs = B.dsem(f"scn{p}")
        c0 = B.sVq.n

        @blk.vector
        def _(v, ci=ci, p=p, scs=scs, dx_t=ld_tgt[ci], ws=exps_done[p],
              wt=war_t[p]):
            v.wait_ge(scs.h, dx_t)
            v.wait_ge(B.sSq.h, ws)
            if wt is not None:
                v.wait_ge(B.sT.h, wt)
            te = tabexv[p]
            k4 = strkv[p].rearrange("p (t a b) -> p t a b", a=8, b=16)
            q4 = strqv[p].rearrange("p (t a b) -> p t a b", a=8, b=16)
            est_b = te[:, 0:128].rearrange("p (t c) -> p t c", c=8) \
                .unsqueeze(-1).broadcast_to((128, NT, 8, 16))
            v.tensor_tensor(fkst[p][:, :].rearrange("p (t a b) -> p t a b",
                                                    a=8, b=16),
                            k4, est_b, ALU.mult)
            ein_b = te[:, 512:640].rearrange("p (t c) -> p t c", c=8) \
                .unsqueeze(-1).broadcast_to((128, NT, 8, 16))
            v.tensor_tensor(fqh[p][:, :].rearrange("p (t a b) -> p t a b",
                                                   a=8, b=16),
                            q4, ein_b, ALU.mult)
            rd_b = te[:, 256:384].rearrange("p (t c) -> p t c", c=8) \
                .unsqueeze(-1).broadcast_to((128, NT, 8, 16))
            v.tensor_tensor(fkd[p][:, :].rearrange("p (t a b) -> p t a b",
                                                   a=8, b=16),
                            k4, rd_b, ALU.mult)
        fact_k[ci] = c0 + 1

    def factors_p2b(ci):
        p = ci % 2
        scs = B.dsem(f"scn{p}")

        @blk.vector
        def _(v, ci=ci, p=p, scs=scs, dx_t=ld_tgt[ci], ws=exps_done[p],
              wt=war_t[p]):
            v.wait_ge(scs.h, dx_t)
            v.wait_ge(B.sSq.h, ws)
            if wt is not None:
                v.wait_ge(B.sT.h, wt)
            te = tabexv[p]
            k4 = strkv[p].rearrange("p (t a b) -> p t a b", a=8, b=16)
            q4 = strqv[p].rearrange("p (t a b) -> p t a b", a=8, b=16)
            e64q_b = te[:, 128:192].rearrange("p (t c) -> p t c", c=4) \
                .unsqueeze(-1).broadcast_to((128, NT, 4, 16))
            v.tensor_tensor(fq64[p][:, :].rearrange("p (t a b) -> p t a b",
                                                    a=4, b=16),
                            strqv[p].rearrange("p (t s) -> p t s", s=128)
                            [:, :, 64:128].rearrange("p t (a b) -> p t a b", b=16),
                            e64q_b, ALU.mult)
            e64k_b = te[:, 192:256].rearrange("p (t c) -> p t c", c=4) \
                .unsqueeze(-1).broadcast_to((128, NT, 4, 16))
            v.tensor_tensor(fk64[p][:, :].rearrange("p (t a b) -> p t a b",
                                                    a=4, b=16),
                            strkv[p].rearrange("p (t s) -> p t s", s=128)
                            [:, :, 0:64].rearrange("p t (a b) -> p t a b", b=16),
                            e64k_b, ALU.mult)
            q32_4 = fq32[p][:, :].rearrange("p (t a b) -> p t a b", a=4, b=16)
            k32_4 = fk32[p][:, :].rearrange("p (t a b) -> p t a b", a=4, b=16)
            dd = te[:, 384:512].rearrange("p (t c) -> p t c", c=8)
            v.tensor_copy(q32_4[:, :, 0::2, :], q4[:, :, 2::4, :])
            v.tensor_tensor(q32_4[:, :, 1::2, :], q4[:, :, 3::4, :],
                            dd[:, :, 2::4].unsqueeze(-1)
                            .broadcast_to((128, NT, 2, 16)), ALU.mult)
            v.tensor_copy(k32_4[:, :, 1::2, :], k4[:, :, 1::4, :])
            v.tensor_tensor(k32_4[:, :, 0::2, :], k4[:, :, 0::4, :],
                            dd[:, :, 1::4].unsqueeze(-1)
                            .broadcast_to((128, NT, 2, 16)), ALU.mult)
        fact_v[ci] = B.sVq.n

    def supd_emit_a(ci, tS):
        p = ci % 2

        @blk.vector
        def _(v, p=p, tS=tS, ws=fact_s[ci]):
            v.wait_ge(B.sT.h, tS)
            v.wait_ge(B.sSq.h, ws)
            sp = B.spar
            drow_b = drowp[p][:, :].unsqueeze(-1).broadcast_to((128, NT, 64))
            v.tensor_tensor(B.S[:, 2048:3072], B.S[:, sp * 1024:(sp + 1) * 1024],
                            drow_b, ALU.mult)
            v.tensor_tensor(B.S[:, (1 - sp) * 1024:(2 - sp) * 1024],
                            B.S[:, 2048:3072], pST, ALU.add)
        B.spar = 1 - B.spar
        if prefix:
            sup_v[0] = B.sVq.n

    def supd_emit_b(ci, tOmm):
        @blk.vector
        def _(v, tOmm=tOmm):
            v.wait_ge(B.sT.h, tOmm)
            sp = B.spar
            v.tensor_copy(B.Sbf[:, :], B.S[:, sp * 1024:(sp + 1) * 1024])
        sup_v[0] = B.sVq.n

    @blk.gpsimd
    def _(g):
        loads(g, 0)
    ld_tgt[0] = B.dsem("scn0").n
    factors_p1(0)
    if not prefix:
        factors_p2a(0)
        factors_p2b(0)

    pre_done = False
    for ci in range(NCH):
        p = ci % 2
        if ci + 1 < NCH:
            @blk.gpsimd
            def _(g, ci=ci):
                loads(g, ci + 1)
            ld_tgt[ci + 1] = B.dsem(f"scn{(ci + 1) % 2}").n
        hooks = {}
        if ci + 1 < NCH:
            hooks[1] = [lambda ci=ci: factors_p1(ci + 1)]
            if not prefix:
                hooks[3] = [lambda ci=ci: factors_p2a(ci + 1)]
                hooks[4] = [lambda ci=ci: factors_p2b(ci + 1)]
        nxt = None
        if ci + 1 < NCH:
            nxt = dict(p=1 - p, scs=B.dsem(f"scn{(ci + 1) % 2}"),
                       gK=(lambda ci=ci: fact_k[ci + 1]),
                       ld=(lambda ci=ci: ld_tgt[ci + 1]))
        tiles_emit(B, blk, prefix, ci, p, pA, pO, pST, pTR, strv,
                   None if prefix else fkst, strkv, khT,
                   None if prefix else Asb,
                   None if prefix else (strqv, strkv, fkd, fq32, fk32,
                                        fq64, fk64, fqh),
                   None if prefix else ybc,
                   fact_k[ci], fact_v[ci], B.dsem(f"scn{p}"), ld_tgt[ci],
                   sup_v, ysp, hooks,
                   (lambda tS, ci=ci: supd_emit_a(ci, tS),
                    lambda tOmm, ci=ci: supd_emit_b(ci, tOmm)),
                   pre_done, nxt)
        pre_done = ci + 1 < NCH
        war_t[p] = B.sT.n
        if not prefix:
            oev = B.sVq.n

            @blk.gpsimd
            def _(g, ci=ci, p=p, oev=oev):
                g.wait_ge(B.sVq.h, oev)
                g.dma_start(out=y_t[:, :, ci * CH:(ci + 1) * CH],
                            in_=ybc[:, p * T:(p + 1) * T]
                            .rearrange("p (t c) -> p t c", c=128)) \
                    .then_inc(ysp.h, 16)
            ysp.n += 16
        if chunk_cb is not None:
            chunk_cb(ci)

    B.barrier(blk)
    names = ["strk", "strv", "btres", "khT", "drow0", "drow1"]
    if not prefix:
        names += ["strq", "bt9", "tabin", "tabex",
                  "fkst0", "fkst1", "fqh0", "fqh1", "fkd0", "fkd1",
                  "fq320", "fq321", "fk320", "fk321",
                  "fq640", "fq641", "fk640", "fk641", "Asb", "ybc"]
    P.freeb(*names)


def tiles_emit(B, blk, prefix, ci, p, pA, pO, pST, pTR, strv, fkst, strkv, khT,
               Asb, fmain, ybc, gK, gF, scs, ld_t, sup_v, ysp, hooks, supd_cb,
               pre_done, nxt):
    st = getattr(B, "_scn", None)
    if st is None or ci == 0:
        B._scn = {"khv": [None, None], "blk1": [None, None],
                  "aev": [None, None], "omm": [None, None],
                  "oev": [None, None], "pre_khv": None}
        st = B._scn
    if fmain is not None:
        fq16, fk16, fkd, fq32, fk32, fq64, fk64, fqh = fmain
    ksrc = strkv if prefix else fkst
    NP = NT // 2

    def transp(g, pp, kk, sc, ld):
        s = g % 2

        @blk.tensor
        def _(t, g=g, s=s, pp=pp, wk=st["khv"][s], kk=kk, sc=sc, ld=ld):
            if g == 0:
                if prefix:
                    t.wait_ge(sc.h, ld)
                else:
                    t.wait_ge(B.sVq.h, kk)
            if wk is not None:
                t.wait_ge(B.sVq.h, wk)
            for j in (0, 1):
                tile = 2 * g + j
                last = t.transpose(pTR[s][:, j * 128:(j + 1) * 128],
                                   ksrc[pp][:, tile * 128:(tile + 1) * 128],
                                   B.ident[:, :])
            last.then_inc(B.sT.h, 1)
        B.sT.n += 1
        return B.sT.n

    def khcopy(g, trv):
        s = g % 2

        @blk.vector
        def _(v, s=s, trv=trv, wb=st["blk1"][s]):
            v.wait_ge(B.sT.h, trv)
            if wb is not None:
                v.wait_ge(B.sT.h, wb)
            v.tensor_copy(khT[:, s * 256:(s + 1) * 256], pTR[s])
        st["khv"][s] = B.sVq.n
        return B.sVq.n

    if pre_done:
        khv_g = st["pre_khv"]
    else:
        tr0 = transp(0, p, gK, scs, ld_t)
        khv_g = khcopy(0, tr0)

    for g in range(NP):
        s = g % 2

        @blk.tensor
        def _(t, g=g, s=s, khv=khv_g, wa=st["aev"][s], sv=sup_v[0], gF=gF):
            t.wait_ge(B.sVq.h, khv)
            if g == 0:
                if sv is not None:
                    t.wait_ge(B.sVq.h, sv)
                if not prefix:
                    t.wait_ge(B.sVq.h, gF)
            if wa is not None:
                t.wait_ge(B.sVq.h, wa)
            for j in (0, 1):
                tile = 2 * g + j
                last = t.matmul(pST[0:64, tile * 64:(tile + 1) * 64],
                                khT[:, s * 256 + j * 128:s * 256 + j * 128 + 64],
                                strv[:, p * T + tile * 128:p * T + tile * 128 + 64],
                                start=True, stop=True, tile_position=(0, 0))
                last = t.matmul(pST[64:128, tile * 64:(tile + 1) * 64],
                                khT[:, s * 256 + j * 128 + 64:s * 256 + (j + 1) * 128],
                                strv[:, p * T + tile * 128 + 64:p * T + tile * 128 + 128],
                                start=True, stop=True, tile_position=(0, 64))
            if not prefix:
                for j in (0, 1):
                    tile = 2 * g + j
                    ts_ = tile * 128
                    tc_ = tile * 64
                    for hh in (0, 1):
                        hr = 64 * hh
                        base = s * 512 + j * 256 + 128 * hh
                        last = t.matmul(B.PA[0:128, base:base + 128],
                                        fkd[p][hr:hr + 64, ts_:ts_ + 128],
                                        fq16[p][hr:hr + 64, ts_:ts_ + 128],
                                        start=True, stop=True, tile_position=(hr, 0))
                        for J in (0, 2, 4, 6):
                            last = t.matmul(B.PA[J * 16:(J + 1) * 16,
                                                 base + (J + 1) * 16:base + (J + 2) * 16],
                                            fk16[p][hr:hr + 64, ts_ + J * 16:ts_ + (J + 1) * 16],
                                            fq16[p][hr:hr + 64, ts_ + (J + 1) * 16:ts_ + (J + 2) * 16],
                                            start=True, stop=True, tile_position=(hr, J * 16))
                        last = t.matmul(B.PA[0:32, base + 32:base + 64],
                                        fk32[p][hr:hr + 64, tc_:tc_ + 32],
                                        fq32[p][hr:hr + 64, tc_:tc_ + 32],
                                        start=True, stop=True, tile_position=(hr, 0))
                        last = t.matmul(B.PA[64:96, base + 96:base + 128],
                                        fk32[p][hr:hr + 64, tc_ + 32:tc_ + 64],
                                        fq32[p][hr:hr + 64, tc_ + 32:tc_ + 64],
                                        start=True, stop=True, tile_position=(hr, 64))
                        last = t.matmul(B.PA[0:64, base + 64:base + 128],
                                        fk64[p][hr:hr + 64, tc_:tc_ + 64],
                                        fq64[p][hr:hr + 64, tc_:tc_ + 64],
                                        start=True, stop=True, tile_position=(hr, 0))
            last.then_inc(B.sT.h, 1)
        B.sT.n += 1
        st["blk1"][s] = B.sT.n
        tb1 = B.sT.n

        if not prefix:
            @blk.vector
            def _(v, s=s, tb1=tb1, wo=st["omm"][s]):
                v.wait_ge(B.sT.h, tb1)
                if wo is not None:
                    v.wait_ge(B.sT.h, wo)
                utm_b = B.utm[:, :].unsqueeze(1).broadcast_to((128, 4, 128))
                v.tensor_tensor(Asb[:, s * 512:(s + 1) * 512],
                                B.PA[:, s * 512:(s + 1) * 512], utm_b, ALU.mult)
            st["aev"][s] = B.sVq.n
        aev_g = B.sVq.n

        if g == NP - 1:
            supd_cb[0](st["blk1"][s])

        if g + 1 < NP:
            trn = transp(g + 1, p, gK, scs, ld_t)
            khv_g = khcopy(g + 1, trn)

        for cb in hooks.get(g, []):
            cb()

        if prefix:
            continue

        @blk.tensor
        def _(t, g=g, s=s, aev=aev_g, wo=st["oev"][s]):
            t.wait_ge(B.sVq.h, aev)
            if wo is not None:
                t.wait_ge(B.sVq.h, wo)
            for j in (0, 1):
                tile = 2 * g + j
                ts_ = tile * 128
                tc_ = tile * 64
                oc = j * 128
                last = t.matmul(pO[s][0:64, oc:oc + 128], B.Sbf[0:64, tc_:tc_ + 64],
                                fqh[p][0:64, ts_:ts_ + 128], start=True, stop=False,
                                tile_position=(0, 0))
                last = t.matmul(pO[s][0:64, oc:oc + 128],
                                strv[:, p * T + ts_:p * T + ts_ + 64],
                                Asb[:, s * 512 + j * 256:s * 512 + j * 256 + 128],
                                start=False, stop=True, tile_position=(0, 0))
                last = t.matmul(pO[s][64:128, oc:oc + 128], B.Sbf[64:128, tc_:tc_ + 64],
                                fqh[p][64:128, ts_:ts_ + 128], start=True, stop=False,
                                tile_position=(64, 64))
                last = t.matmul(pO[s][64:128, oc:oc + 128],
                                strv[:, p * T + ts_ + 64:p * T + ts_ + 128],
                                Asb[:, s * 512 + j * 256 + 128:s * 512 + (j + 1) * 256],
                                start=False, stop=True, tile_position=(0, 64))
            last.then_inc(B.sT.h, 1)
        B.sT.n += 1
        st["omm"][s] = B.sT.n
        tO = B.sT.n

        @blk.vector
        def _(v, g=g, s=s, p=p, tO=tO, wy=ysp.n if (ci >= 2 and g == 0) else None):
            v.wait_ge(B.sT.h, tO)
            if wy is not None:
                v.wait_ge(ysp.h, wy)
            v.tensor_copy(ybc[:, p * T + g * 256:p * T + (g + 1) * 256], pO[s])
        st["oev"][s] = B.sVq.n

        if g == NP - 1:
            supd_cb[1](tO)

    if nxt is not None:
        trn = transp(0, nxt["p"], nxt["gK"](), nxt["scs"], nxt["ld"]())
        st["pre_khv"] = khcopy(0, trn)


def emit_sinit(B, blk):
    @blk.vector
    def _(v):
        sp = B.spar
        v.tensor_scalar(B.S[:, (1 - sp) * 1024:(2 - sp) * 1024],
                        B.S[:, sp * 1024:(sp + 1) * 1024],
                        B.sflag[:, 0:1], None, ALU.mult)
        v.tensor_copy(B.Sbf[:, :], B.S[:, (1 - sp) * 1024:(2 - sp) * 1024])
    B.spar = 1 - B.spar
    B.barrier(blk)


def emit_yln(B, blk, pre_cb=None):
    P = B.pool
    y_t = B.h_y.rearrange("(t p) n -> p t n", p=128)
    ytl = P.alloc("ytl", NT * T, BF16)
    sqb = P.alloc("sqb", 2 * T, BF16)
    ylnb = P.alloc("ylnb", NT * T, BF16)
    srow = ylnb[0:1, 0:24576].bitcast(F32)
    abrow = P.alloc("abrow", 2 * T, F32)
    bcast = P.alloc("bcast", 2 * T, F32)
    scrf = P.alloc("scrf", T, F32)
    yls = B.dsem("yls")
    if pre_cb is not None:
        pre_cb()

    sS_done = [None, None]
    sT_done = [None, None]
    for k in range(NT):
        p = k % 2

        @blk.sync
        def _(sy, k=k, p=p, pv=yls.n, wt=sS_done[p], wtt=sT_done[p]):
            sy.wait_ge(yls.h, pv)
            if wtt is not None:
                sy.wait_ge(B.sT.h, wtt)
            sy.dma_start(out=ytl[:, k * T:(k + 1) * T], in_=y_t[:, k, :]) \
                .then_inc(yls.h, 16)
        yls.n += 16
        dx_t = yls.n

        @blk.scalar
        def _(s, k=k, p=p, dx_t=dx_t, wt=sT_done[p]):
            s.wait_ge(yls.h, dx_t)
            if wt is not None:
                s.wait_ge(B.sT.h, wt)
            s.activation(sqb[:, p * T:(p + 1) * T], ytl[:, k * T:(k + 1) * T],
                         AF.Square)
        sS_done[p] = B.sSq.n
        sv = B.sSq.n

        @blk.tensor
        def _(t, k=k, p=p, sv=sv):
            t.wait_ge(B.sSq.h, sv)
            for n in range(4):
                last = t.matmul(B.PA[0:1, n * 512:(n + 1) * 512], B.ones[:, :],
                         ytl[:, k * T + n * 512:k * T + (n + 1) * 512],
                         start=(k == 0), stop=(k == NT - 1))
                last = t.matmul(B.PB[0:1, n * 512:(n + 1) * 512], B.ones[:, :],
                         sqb[:, p * T + n * 512:p * T + (n + 1) * 512],
                         start=(k == 0), stop=(k == NT - 1))
            t.drain()
            t.nop(nofuse=True).then_inc(B.sT.h, 1)
        B.sT.n += 1
        sT_done[p] = B.sT.n

    emit_stats_tail(B, blk, srow, abrow, bcast)
    bc_v = B.sVq.n

    for k in range(NT):
        @blk.vector
        def _(v, k=k, bc_v=bc_v):
            v.wait_ge(B.sVq.h, bc_v)
            v.tensor_tensor(scrf[:, :], ytl[:, k * T:(k + 1) * T], bcast[:, 0:T],
                            ALU.mult)
            v.tensor_tensor(scrf[:, :], scrf[:, :], bcast[:, T:2 * T], ALU.add)
            v.tensor_scalar(ylnb[:, k * T:(k + 1) * T], scrf[:, :],
                            B.vecs[:, 3 * NT + k:3 * NT + k + 1],
                            B.vecs[:, 4 * NT + k:4 * NT + k + 1],
                            ALU.mult, ALU.add)

    B.barrier(blk)
    P.freeb("ytl", "sqb", "abrow", "bcast", "scrf")
    return ylnb


def emit_ffn_prefetch(B, blk):
    P = B.pool
    xnc = P.alloc_top("xnc", NT * 1024, BF16)
    guslab = P.alloc_top("guslab", 2 * NT * 256, BF16)
    xn_t = B.h_xn.rearrange("(t p) n -> p t n", p=128)
    ffx = B.dsem("ffx")

    @blk.sync
    def _(sy, ffx=ffx, pv=ffx.n):
        sy.wait_ge(ffx.h, pv)
        for k in range(NT):
            sy.dma_start(out=xnc[:, k * 1024:(k + 1) * 1024],
                         in_=xn_t[:, k, 1:1025]).then_inc(ffx.h, 16)
    ffx.n += NT * 16
    xq = ffx.n
    gsem = []
    for g in (0, 1):
        ffg = B.dsem(f"ffg{g}")

        @blk.gpsimd
        def _(gg, g=g, ffg=ffg, pv=ffg.n):
            gg.wait_ge(ffg.h, pv)
            gg.dma_start(out=guslab[:, g * NT * 256:(g + 1) * NT * 256],
                         in_=B.i_gut[g].rearrange("k p c -> p k c")).then_inc(ffg.h, 16)
        ffg.n += 16
        gsem.append((ffg, ffg.n))
    return dict(xq=xq, gsem=gsem)


def emit_ffn(B, blk, pre=None):
    P = B.pool
    hbuf = P.alloc("hbuf", 64 * 1024, BF16)
    xn_t = B.h_xn.rearrange("(t p) n -> p t n", p=128)
    s_t = B.h_s.rearrange("(t p) n -> p t n", p=128)
    x_t = B.i_xm.rearrange("(t p) n -> p t n", p=128)
    o_t = B.o_out.rearrange("(t p) n -> p t n", p=128)

    t_pv, v_pv, s_pv = B.sT.n, B.sVq.n, B.sSq.n
    spl_pv = [B.dsem("spl0").n, B.dsem("spl1").n]

    for tc in range(2):
        pre0 = pre is not None and tc == 0
        if pre0:
            xnc = P.view("xnc")
            guslab = P.view("guslab")
        else:
            xnc = P.alloc("xnc", NT * 1024, BF16)
            guslab = P.alloc("guslab", 2 * NT * 256, BF16)
        sgev = P.alloc("sgev", 2 * 1024, BF16)

        ffx = B.dsem("ffx")

        if pre0:
            xq = pre["xq"]
        else:
            @blk.sync
            def _(sy, tc=tc, ffx=ffx, pv=ffx.n, tw=t_pv, vw=v_pv, sw=s_pv,
                  spw=tuple(spl_pv)):
                sy.wait_ge(ffx.h, pv)
                sy.wait_ge(B.sT.h, tw)
                sy.wait_ge(B.sVq.h, vw)
                sy.wait_ge(B.sSq.h, sw)
                sy.wait_ge(B.dsem("spl0").h, spw[0])
                sy.wait_ge(B.dsem("spl1").h, spw[1])
                for k in range(NT):
                    sy.dma_start(out=xnc[:, k * 1024:(k + 1) * 1024],
                                 in_=xn_t[:, k, 1 + tc * 1024:1 + (tc + 1) * 1024]) \
                        .then_inc(ffx.h, 16)
            ffx.n += NT * 16
            xq = ffx.n

        sV_done = [None, None]
        sS_done = [None, None]
        for g in range(64):
            pg = g % 2
            PP = B.PA if pg == 0 else B.PB

            wss = B.dsem(f"wsl{pg}")

            if pre0 and g < 2:
                dws, dw_t = pre["gsem"][g]
            else:
                @blk.gpsimd
                def _(gg, g=g, pg=pg, wss=wss, pv=wss.n, wt=sV_done[pg],
                      tw=t_pv, vw=v_pv):
                    gg.wait_ge(wss.h, pv)
                    if g < 2:
                        gg.wait_ge(B.sT.h, tw)
                        gg.wait_ge(B.sVq.h, vw)
                    if wt is not None:
                        gg.wait_ge(B.sVq.h, wt)
                    gg.dma_start(out=guslab[:, pg * NT * 256:(pg + 1) * NT * 256],
                                 in_=B.i_gut[g].rearrange("k p c -> p k c")).then_inc(wss.h, 16)
                wss.n += 16
                dws, dw_t = wss, wss.n

            @blk.tensor
            def _(t, g=g, pg=pg, PP=PP, dws=dws, ffx=ffx, dw_t=dw_t, xq=xq,
                  wt=sV_done[pg], vw=v_pv, sw=s_pv):
                t.wait_ge(dws.h, dw_t)
                if g < 2:
                    t.wait_ge(ffx.h, xq)
                    t.wait_ge(B.sVq.h, vw)
                    t.wait_ge(B.sSq.h, sw)
                if wt is not None:
                    t.wait_ge(B.sVq.h, wt)
                for kk in range(NT):
                    for n in range(2):
                        last = t.matmul(PP[:, n * 512:(n + 1) * 512],
                                 guslab[:, pg * NT * 256 + kk * 256:pg * NT * 256 + kk * 256 + 128],
                                 xnc[:, kk * 1024 + n * 512:kk * 1024 + (n + 1) * 512],
                                 start=(kk == 0), stop=(kk == NT - 1))
                        last = t.matmul(PP[:, 1024 + n * 512:1024 + (n + 1) * 512],
                                 guslab[:, pg * NT * 256 + kk * 256 + 128:pg * NT * 256 + (kk + 1) * 256],
                                 xnc[:, kk * 1024 + n * 512:kk * 1024 + (n + 1) * 512],
                                 start=(kk == 0), stop=(kk == NT - 1))
                last.then_inc(B.sT.h, 1)
            B.sT.n += 1
            tv = B.sT.n

            @blk.scalar
            def _(s, pg=pg, PP=PP, tv=tv, wt=sS_done[pg]):
                s.wait_ge(B.sT.h, tv)
                if wt is not None:
                    s.wait_ge(B.sVq.h, wt)
                s.activation(sgev[:, pg * 1024:(pg + 1) * 1024], PP[:, 0:1024],
                             AF.Silu)
            sv = B.sSq.n

            @blk.vector
            def _(v, g=g, pg=pg, PP=PP, sv=sv):
                v.wait_ge(B.sSq.h, sv)
                v.tensor_tensor(hbuf[:, g * 1024:(g + 1) * 1024],
                                sgev[:, pg * 1024:(pg + 1) * 1024],
                                PP[:, 1024:2048], ALU.mult)
            sV_done[pg] = B.sVq.n
            sS_done[pg] = B.sVq.n

        t_pv, v_pv, s_pv = B.sT.n, B.sVq.n, B.sSq.n
        P.freeb("xnc", "guslab", "sgev")
        dwslab = P.alloc("dwslab", 2 * 64 * 128, BF16)
        grow = P.alloc("grow", 2 * 1024, BF16)
        srow_b = P.alloc("srow_b", 2 * 1024, BF16)
        xrow_b = P.alloc("xrow_b", 2 * 1024, F32)
        zrB = P.alloc("zrB", 2 * 1024, F32)

        sT_done = [None, None]
        sV_done2 = [None, None]
        out_done = [None, None]
        for m in range(NT):
            pm = m % 2
            PP = B.PA if pm == 0 else B.PB

            wss = B.dsem(f"wsl{pm}")
            ffs = B.dsem("ffs")

            @blk.gpsimd
            def _(gg, m=m, pm=pm, wss=wss, pv=wss.n, wt=sT_done[pm],
                  tw=t_pv, vw=v_pv):
                gg.wait_ge(wss.h, pv)
                if m < 2:
                    gg.wait_ge(B.sT.h, tw)
                    gg.wait_ge(B.sVq.h, vw)
                if wt is not None:
                    gg.wait_ge(B.sT.h, wt)
                gg.dma_start(out=dwslab[:, pm * 64 * 128:(pm + 1) * 64 * 128],
                             in_=B.i_dwt[m].rearrange("k p c -> p k c")).then_inc(wss.h, 16)
            wss.n += 16
            dw_t = wss.n

            @blk.sync
            def _(sy, m=m, tc=tc, pm=pm, ffs=ffs, pv=ffs.n, wt=sV_done2[pm],
                  tw=t_pv, vw=v_pv):
                sy.wait_ge(ffs.h, pv)
                if m < 2:
                    sy.wait_ge(B.sT.h, tw)
                    sy.wait_ge(B.sVq.h, vw)
                if wt is not None:
                    sy.wait_ge(B.sVq.h, wt)
                sy.dma_start(out=srow_b[:, pm * 1024:(pm + 1) * 1024],
                             in_=s_t[:, m, tc * 1024:(tc + 1) * 1024]) \
                    .then_inc(ffs.h, 16)
                sy.dma_start(out=xrow_b[:, pm * 1024:(pm + 1) * 1024],
                             in_=x_t[:, m, tc * 1024:(tc + 1) * 1024]) \
                    .then_inc(ffs.h, 16)
            ffs.n += 32
            ldq = ffs.n

            @blk.tensor
            def _(t, m=m, pm=pm, PP=PP, wss=wss, dw_t=dw_t, wt=sV_done2[pm],
                  vw=v_pv, sw=s_pv):
                t.wait_ge(wss.h, dw_t)
                if m < 2:
                    t.wait_ge(B.sVq.h, vw)
                    t.wait_ge(B.sSq.h, sw)
                if wt is not None:
                    t.wait_ge(B.sVq.h, wt)
                for kk in range(64):
                    for n in range(2):
                        last = t.matmul(PP[:, n * 512:(n + 1) * 512],
                                 dwslab[:, pm * 64 * 128 + kk * 128:pm * 64 * 128 + (kk + 1) * 128],
                                 hbuf[:, kk * 1024 + n * 512:kk * 1024 + (n + 1) * 512],
                                 start=(kk == 0), stop=(kk == 63))
                last.then_inc(B.sT.h, 1)
            B.sT.n += 1
            sT_done[pm] = B.sT.n
            tv = B.sT.n

            @blk.scalar
            def _(s, m=m, pm=pm, PP=PP, tv=tv, wt=sV_done2[pm], vw=v_pv):
                s.wait_ge(B.sT.h, tv)
                if m < 2:
                    s.wait_ge(B.sVq.h, vw)
                if wt is not None:
                    s.wait_ge(B.sVq.h, wt)
                s.activation(grow[:, pm * 1024:(pm + 1) * 1024], PP[:, 0:1024],
                             AF.Silu)
            sv = B.sSq.n

            @blk.vector
            def _(v, m=m, pm=pm, sv=sv, ffs=B.dsem("ffs"), ldq=ldq,
                  sps=B.dsem(f"spl{pm}"), wt=out_done[pm]):
                v.wait_ge(B.sSq.h, sv)
                v.wait_ge(ffs.h, ldq)
                if wt is not None:
                    v.wait_ge(sps.h, wt)
                v.tensor_tensor(zrB[:, pm * 1024:(pm + 1) * 1024],
                                grow[:, pm * 1024:(pm + 1) * 1024],
                                srow_b[:, pm * 1024:(pm + 1) * 1024], ALU.mult)
                v.tensor_tensor(zrB[:, pm * 1024:(pm + 1) * 1024],
                                zrB[:, pm * 1024:(pm + 1) * 1024],
                                xrow_b[:, pm * 1024:(pm + 1) * 1024], ALU.add)
            sV_done2[pm] = B.sVq.n
            vv = B.sVq.n

            sps = B.dsem(f"spl{pm}")

            @blk.scalar
            def _(s, m=m, tc=tc, pm=pm, sps=sps, vv=vv):
                s.wait_ge(B.sVq.h, vv)
                s.dma_start(out=o_t[:, m, tc * 1024:(tc + 1) * 1024],
                            in_=zrB[:, pm * 1024:(pm + 1) * 1024]).then_inc(sps.h, 16)
            sps.n += 16
            out_done[pm] = sps.n

        t_pv, v_pv, s_pv = B.sT.n, B.sVq.n, B.sSq.n
        spl_pv = [B.dsem("spl0").n, B.dsem("spl1").n]
        P.freeb("dwslab", "grow", "srow_b", "xrow_b", "zrB")

    P.freeb("hbuf")



def emit_tap(B, blk, name, src):
    if name not in B.tapt:
        return
    B.barrier(blk)

    tps = B.dsem("tap")

    @blk.sync
    def _(sy, tps=tps):
        sy.dma_start(out=B.tapt[name], in_=src).then_inc(tps.h, 16)
    tps.n += 16
    B.barrier(blk)


def build_kernel(taps=(), phases=99):
    B = KB(taps, phases)
    B.declare()
    B.allocate()
    with B.nc.Block() as raw_blk:
        blk = BlkProxy(raw_blk, B)
        emit_p0(B, blk)
        for prefix in (True, False):
            w_dma = emit_w12_prefetch(B, blk)
            emit_ln(B, blk, B.i_xp if prefix else B.i_xm, prefix,
                    p1_done=not prefix)
            td_dma = emit_mm(B, blk, w_dma)
            emit_mixer_r(B, blk, 3)
            B.pool.freeb("mmr3")
            emit_wpath_head(B, blk, td_dma)
            if not prefix:
                emit_mixer_r(B, blk, 0)
                B.pool.freeb("mmr0")
                emit_bigmm_cm(B, blk, B.i_qw, B.pool.view("xrb"), B.h_q,
                              bias_col=0)
            emit_mixer_r(B, blk, 1)
            B.pool.freeb("mmr1")
            emit_bigmm_cm(B, blk, B.i_kw, B.pool.view("xrb"), B.h_k, bias_col=1)
            emit_mixer_r(B, blk, 2)
            fr = ["mmr2", "w2sb", "xnres"]
            if prefix:
                fr.append("mmr0")
            B.pool.freeb(*fr)
            emit_vmm_tm(B, blk, B.pool.view("xrb"))
            B.pool.freeb("xrb")
            emit_wpath_tail(B, blk, prefix)
            if prefix:
                ln_tile = emit_ln_p1(B, blk, B.i_xm, alloc=True)
                emit_scan(B, blk, True, chunk_cb=ln_tile)
                emit_sinit(B, blk)
            else:
                emit_scan(B, blk, False)
        box = {}
        ylnb = emit_yln(B, blk)
        emit_bigmm_cm(B, blk, B.i_ow, ylnb, B.h_s, bias_col=None,
                      tail_cb=lambda: box.__setitem__(
                          "ffn", emit_ffn_prefetch(B, blk)))
        B.pool.freeb("ylnb")
        emit_ffn(B, blk, pre=box["ffn"])
        B.barrier(blk)
    B.close()
    return B


_BUILD_CACHE = {}


def stage_weights(i):
    f32 = np.float32

    def cc(a):
        return np.ascontiguousarray(a, dtype=f32)

    qw = cc(i["L0_attn_q_w"]).reshape(16, 128, 16, 128).transpose(2, 0, 1, 3)
    kw = cc(i["L0_attn_k_w"]).reshape(16, 128, 16, 128).transpose(2, 0, 1, 3)
    ow = cc(i["L0_attn_o_w"]).reshape(16, 128, 16, 128).transpose(2, 0, 1, 3)
    vw = cc(i["L0_attn_v_w"]).reshape(16, 128, 4, 512).transpose(2, 0, 1, 3)
    w1 = cc(i["L0_attn_maa_w1"])
    w2 = np.stack([np.concatenate([cc(i["L0_attn_maa_w2"])[r],
                                   cc(i["L0_attn_" + n])[None, :]], axis=0)
                   for r, n in enumerate(["maa_r", "maa_k", "maa_v", "maa_w"])])
    td1 = cc(i["L0_attn_td_w1"])
    td2 = np.concatenate([cc(i["L0_attn_td_w2"]),
                          cc(i["L0_attn_time_decay"])[None, :]], axis=0)
    wgT = cc(i["L0_ffn_w_gate_up"]).T
    gate = np.ascontiguousarray(wgT[:, :8192]).reshape(16, 128, 64, 128)
    up = np.ascontiguousarray(wgT[:, 8192:]).reshape(16, 128, 64, 128)
    gut = np.ascontiguousarray(np.concatenate([gate, up], axis=3).transpose(2, 0, 1, 3))
    wdT = cc(i["L0_ffn_w_down"]).T
    dwt = np.ascontiguousarray(wdT.reshape(64, 128, 16, 128).transpose(2, 0, 1, 3))

    def col(name):
        return cc(i[name]).reshape(16, 128).T

    vecs = np.ascontiguousarray(np.concatenate(
        [col("L0_attn_q_b"), col("L0_attn_k_b"), col("L0_attn_v_b"),
         col("L0_attn_ln_g"), col("L0_attn_ln_b"), col("L0_attn_maa_x")], axis=1))
    vbrow = np.ascontiguousarray(np.broadcast_to(cc(i["L0_attn_v_b"])[None, :],
                                                 (128, 2048)))
    utm = (np.arange(128)[:, None] <= np.arange(128)[None, :]).astype(f32)
    ident = np.eye(128, dtype=f32)
    cmask = np.ones((128, T), f32)
    cmask[:, 0::CH] = 0.0
    return dict(qw_t=np.ascontiguousarray(qw), kw_t=np.ascontiguousarray(kw),
                ow_t=np.ascontiguousarray(ow), vw_t=np.ascontiguousarray(vw),
                w1=w1, w2t=w2, td1=td1, td2t=td2, gut=gut, dwt=dwt,
                vecs=vecs, vbrow=vbrow, utm=utm, ident=ident, cmask=cmask)


def run_cores(inputs, taps=(), phases=99, cores=8):
    key = (tuple(sorted(taps)), phases)
    if key not in _BUILD_CACHE:
        _BUILD_CACHE[key] = build_kernel(taps, phases)
    B = _BUILD_CACHE[key]
    shared = stage_weights(inputs)
    x = np.asarray(inputs["L0_attn_x"], dtype=np.float32)
    in_maps = []
    for core in range(cores):
        b, half = core // 2, core % 2
        xT = np.ascontiguousarray(x[b].T)
        m = dict(shared)
        m["xm"] = np.ascontiguousarray(xT[:, half * T:(half + 1) * T])
        m["xp"] = np.ascontiguousarray(xT[:, 0:T])
        m["sflag"] = np.full((128, 1), float(half), np.float32)
        in_maps.append(m)
    res = run_bass_kernel_spmd(B.nc, in_maps, core_ids=list(range(cores)))
    return B, res


def kernel(**inputs):
    _, res = run_cores(inputs, (), 99, 8)
    x = np.asarray(inputs["L0_attn_x"])
    out = np.empty((4, 4096, 2048), np.float32)
    for core in range(8):
        b, half = core // 2, core % 2
        out[b, half * T:(half + 1) * T, :] = res.results[core]["out"].T
    return out


# version 16
# speedup vs baseline: 1.0338x; 1.0090x over previous
import numpy as np
import ml_dtypes
import concourse.bass as bass
import concourse.mybir as mybir
from concourse.bass_utils import run_bass_kernel_spmd

F32 = mybir.dt.float32
BF16 = mybir.dt.bfloat16
AF = mybir.ActivationFunctionType
ALU = mybir.AluOpType

C, T, NT = 2048, 2048, 16
CH, NCH = 128, 16
LN8 = float(np.log(8.0))
EPS = 1e-5
BF = ml_dtypes.bfloat16


class Sem:
    __slots__ = ("h", "n")

    def __init__(self, h):
        self.h, self.n = h, 0


class Pool:
    def __init__(self, tensor, nelem):
        self.t, self.n = tensor, nelem
        self.free = [(0, nelem)]
        self.live = {}
        self.hwm = 0

    def alloc(self, name, nelem, dtype=BF16, parts=128):
        ne = (nelem * (2 if dtype != BF16 else 1) + 15) & ~15
        for i, (o, sz) in enumerate(self.free):
            if sz >= ne:
                if sz == ne:
                    del self.free[i]
                else:
                    self.free[i] = (o + ne, sz - ne)
                self.live[name] = (o, ne, dtype, parts)
                self.hwm = max(self.hwm, max(o2 + s2 for o2, s2, _, _ in self.live.values()))
                return self.view(name)
        raise MemoryError(f"pool overflow: {name} needs {ne}, free={self.free}, "
                          f"live={[(k, v[1]) for k, v in self.live.items()]}")

    def alloc_top(self, name, nelem, dtype=BF16, parts=128):
        ne = (nelem * (2 if dtype != BF16 else 1) + 15) & ~15
        for i in range(len(self.free) - 1, -1, -1):
            o, sz = self.free[i]
            no = ((o + sz - ne) // 16) * 16
            if no >= o:
                if no == o and o + sz == no + ne:
                    del self.free[i]
                else:
                    self.free[i] = (o, no - o)
                    if o + sz > no + ne:
                        self.free.insert(i + 1, (no + ne, o + sz - (no + ne)))
                self.live[name] = (no, ne, dtype, parts)
                self.hwm = max(self.hwm, max(o2 + s2 for o2, s2, _, _ in self.live.values()))
                return self.view(name)
        raise MemoryError(f"pool overflow(top): {name} needs {ne}, free={self.free}")

    def view(self, name):
        o, ne, dtype, parts = self.live[name]
        ap = self.t[0:parts, o:o + ne]
        if dtype != BF16:
            ap = ap.bitcast(dtype)
        return ap

    def freeb(self, *names):
        for name in names:
            o, ne, _, _ = self.live.pop(name)
            self.free.append((o, ne))
        self.free.sort()
        out = []
        for o, sz in self.free:
            if out and out[-1][0] + out[-1][1] == o:
                out[-1] = (out[-1][0], out[-1][1] + sz)
            else:
                out.append((o, sz))
        self.free = out



_WRAP_OPS = {
    "tensor_tensor", "tensor_scalar", "tensor_copy", "tensor_tensor_scan",
    "scalar_tensor_tensor", "reciprocal", "memset", "memzero", "select",
    "activation", "copy", "add", "mul", "square", "sqrt", "tensor_add",
    "tensor_sub", "tensor_mul", "tensor_max", "tensor_scalar_add",
    "tensor_scalar_mul", "tensor_scalar_max", "tensor_single_scalar",
}


class EngWrap:
    def __init__(self, eng, sem):
        self._e, self._sem = eng, sem

    def __getattr__(self, name):
        attr = getattr(self._e, name)
        if name in _WRAP_OPS:
            def call(*a, _attr=attr, **k):
                if self._sem.n:
                    self._e.wait_ge(self._sem.h, self._sem.n)
                ins = _attr(*a, **k)
                ins.then_inc(self._sem.h, 1)
                self._sem.n += 1
                return _SinkInstr()
            return call
        if name == "dma_start":
            def call(*a, _attr=attr, **k):
                if self._sem.n:
                    self._e.wait_ge(self._sem.h, self._sem.n)
                return _attr(*a, **k)
            return call
        return attr

    def milestone(self, sem, k=1):
        ins = self._e.wait_ge(self._sem.h, self._sem.n)
        ins.then_inc(sem.h, k)


class _SinkInstr:

    def then_inc(self, h, k):
        raise RuntimeError("use milestone() instead of then_inc on wrapped ops")


class BlkProxy:
    def __init__(self, blk, B):
        self._blk, self._B = blk, B

    def vector(self, f):
        return self._blk.vector(lambda v: f(EngWrap(v, self._B.sVq)))

    def scalar(self, f):
        return self._blk.scalar(lambda s: f(EngWrap(s, self._B.sSq)))

    def tensor(self, f):
        return self._blk.tensor(f)

    def gpsimd(self, f):
        return self._blk.gpsimd(lambda g: f(EngWrap(g, self._B.sGq)))

    def sync(self, f):
        return self._blk.sync(f)


class KB:
    def __init__(self, taps=(), phases=99):
        self.taps, self.phases = set(taps), phases
        self.nc = bass.Bass("TRN2", target_bir_lowering=False)
        self.ctx = []
        self.tapt = {}
        self.spar = 0

    def declare(self):
        d = self.nc.dram_tensor
        self.i_xm = d("xm", [C, T], F32, kind="ExternalInput").ap()
        self.i_xp = d("xp", [C, T], F32, kind="ExternalInput").ap()
        self.i_sflag = d("sflag", [128, 1], F32, kind="ExternalInput").ap()
        self.i_qw = d("qw_t", [NT, NT, 128, 128], F32, kind="ExternalInput").ap()
        self.i_kw = d("kw_t", [NT, NT, 128, 128], F32, kind="ExternalInput").ap()
        self.i_ow = d("ow_t", [NT, NT, 128, 128], F32, kind="ExternalInput").ap()
        self.i_vw = d("vw_t", [4, NT, 128, 512], F32, kind="ExternalInput").ap()
        self.i_w1 = d("w1", [C, 128], F32, kind="ExternalInput").ap()
        self.i_w2 = d("w2t", [4, 33, C], F32, kind="ExternalInput").ap()
        self.i_td1 = d("td1", [C, 64], F32, kind="ExternalInput").ap()
        self.i_td2 = d("td2t", [65, C], F32, kind="ExternalInput").ap()
        self.i_gut = d("gut", [64, NT, 128, 256], F32, kind="ExternalInput").ap()
        self.i_dwt = d("dwt", [NT, 64, 128, 128], F32, kind="ExternalInput").ap()
        self.i_vecs = d("vecs", [128, 6 * NT], F32, kind="ExternalInput").ap()
        self.i_vbrow = d("vbrow", [128, C], F32, kind="ExternalInput").ap()
        self.i_utm = d("utm", [128, 128], F32, kind="ExternalInput").ap()
        self.i_ident = d("ident", [128, 128], F32, kind="ExternalInput").ap()
        self.i_cmask = d("cmask", [128, T], F32, kind="ExternalInput").ap()
        self.o_out = d("out", [C, T], F32, kind="ExternalOutput").ap()
        self.h_xn = d("s_xn", [C, T + 1], BF16).ap()
        self.h_q = d("s_q", [C, T], BF16).ap()
        self.h_k = d("s_k", [C, T], BF16).ap()
        self.h_w = d("s_w", [C, T], F32).ap()
        self.h_v = d("s_v", [T, C], BF16).ap()
        self.h_s = d("s_s", [C, T], BF16).ap()
        self.h_y = d("s_y", [C, T], BF16).ap()
        self.h_k16 = d("s_k16", [C, T], BF16).ap()
        self.h_q16 = d("s_q16", [C, T], BF16).ap()
        self.h_kst = d("s_kst", [C, T], BF16).ap()
        self.h_btab = d("s_btab", [NT, 128, 128], F32).ap()
        for nm, sh, dt in [("xn", [C, T + 1], BF16), ("q", [C, T], BF16),
                           ("k", [C, T], BF16),
                           ("v", [T, C], BF16), ("s", [C, T], BF16),
                           ("y", [C, T], BF16), ("yln", [128, NT * T], BF16),
                           ("Sst", [128, 1024], F32)]:
            if nm in self.taps:
                self.tapt[nm] = d("tap_" + nm, sh, dt, kind="ExternalOutput").ap()

    def allocate(self):
        nc = self.nc

        def sb(name, shape, dtype):
            g = nc.sbuf_tensor(name, shape, dtype)
            t = g.__enter__()
            self.ctx.append(g)
            return t

        def mkpsum(name, shape, dtype=F32):
            g = nc.psum_tensor(name, shape, dtype)
            t = g.__enter__()
            self.ctx.append(g)
            return t

        def mksem(name):
            g = nc.semaphore(name)
            h = g.__enter__()
            self.ctx.append(g)
            return Sem(h)

        self.vecs = sb("g_vecs", [128, 6 * NT], F32)
        self.sflag = sb("g_sflag", [128, 1], F32)
        self.utm = sb("g_utm", [128, 128], BF16)
        self.ident = sb("g_ident", [128, 128], BF16)
        self.cmask = sb("g_cmask", [128, T], BF16)
        self.ones = sb("g_ones", [128, 1], BF16)
        self.onesrow = sb("g_onesrow", [1, 128], F32)
        self.cbln8 = sb("g_cbln8", [128, 1], F32)
        self.S = sb("g_S", [128, 3 * 1024], F32)
        self.Sbf = sb("g_Sbf", [128, 1024], BF16)
        self.bcol = sb("g_bcol", [128, NT], BF16)
        self.Btab = sb("g_btab", [128, NT * 9], F32)
        self.dtab = sb("g_dtab", [128, NT * 8], F32)
        self.esttab = sb("g_est", [128, NT * 8], BF16)
        self.eintab = sb("g_eint", [128, NT * 8], BF16)
        self.drow = sb("g_drow", [128, NT], F32)
        self.pool_t = sb("g_pool", [128, 95400], BF16)
        self.pool = Pool(self.pool_t, 95400)
        self.PA = mkpsum("p_A", [128, 2048], F32)
        self.PB = mkpsum("p_B", [128, 2048], F32)
        self.sT = mksem("sT")
        self.sV = mksem("sV")
        self.sS = mksem("sS")
        self.sG = mksem("sG")
        self.dsems = {}
        self.sVq = mksem("sVq")
        self.sSq = mksem("sSq")
        self.sGq = mksem("sGq")
        self._mksem = mksem

    def dsem(self, name):
        if name not in self.dsems:
            self.dsems[name] = self._mksem("d_" + name)
        return self.dsems[name]

    def close(self):
        for g in reversed(self.ctx):
            g.__exit__(None, None, None)

    def barrier(self, blk):
        B = self

        @blk.tensor
        def _(t):
            t.drain()
            t.nop(nofuse=True).then_inc(B.sT.h, 1)
        B.sT.n += 1

        @blk.gpsimd
        def _(g):
            g.nop(nofuse=True).then_inc(B.sG.h, 1)
        B.sG.n += 1

        tgt = [(B.sT.h, B.sT.n), (B.sVq.h, B.sVq.n), (B.sSq.h, B.sSq.n),
               (B.sGq.h, B.sGq.n),
               (B.sG.h, B.sG.n)] + [(d.h, d.n) for d in B.dsems.values()]
        for sec in ("tensor", "vector", "scalar", "gpsimd", "sync"):
            def f(e, tgt=tgt):
                for h, n in tgt:
                    e.wait_ge(h, n)
            getattr(blk, sec)(f)


def emit_p0(B, blk):
    cst = B.dsem("cst")

    @blk.gpsimd
    def _(g):
        def dma(out, in_):
            g.dma_start(out=out, in_=in_).then_inc(cst.h, 16)
            cst.n += 16
        dma(B.vecs[:, :], B.i_vecs)
        dma(B.sflag[:, :], B.i_sflag)
        dma(B.utm[:, :], B.i_utm)
        dma(B.ident[:, :], B.i_ident)
        dma(B.cmask[:, :], B.i_cmask)

    @blk.vector
    def _(v):
        v.memset(B.ones[:, :], 1.0)
        v.memset(B.onesrow[:, :], 1.0)
        v.memset(B.cbln8[:, :], -LN8)
        v.memset(B.Btab[:, :], 0.0)
        v.memset(B.S[:, :], 0.0)
        v.memset(B.Sbf[:, :], 0.0)

    B.barrier(blk)


def emit_w12_prefetch(B, blk):
    P = B.pool
    w1sb = P.alloc("w1sb", NT * 128, BF16)
    w2sb = P.alloc("w2sb", 4 * C, BF16, parts=33)
    wpre = B.dsem("wpre")

    @blk.gpsimd
    def _(g, wpre=wpre, pv=wpre.n):
        g.wait_ge(wpre.h, pv)
        g.dma_start(out=w2sb, in_=B.i_w2.rearrange("r p c -> p r c")).then_inc(wpre.h, 16)
        g.dma_start(out=w1sb, in_=B.i_w1.rearrange("(t p) m -> p t m", p=128)).then_inc(wpre.h, 16)
    wpre.n += 32
    return wpre.n


def emit_ln_p1(B, blk, xin, alloc=True):
    xin_t = xin.rearrange("(t p) n -> p t n", p=128)
    P = B.pool
    if alloc:
        P.alloc("xnres", NT * (T + 1), BF16)
        P.alloc("xrb", NT * T, BF16)
        P.alloc("kvx", 2 * T, F32)
        P.alloc("sqb", 2 * T, BF16)
    xnres, kvx, sqb = P.view("xnres"), P.view("kvx"), P.view("sqb")
    st = {"sS": [None, None], "sT": [None, None]}

    def tile(k):
        p = k % 2
        kvs = B.dsem(f"kvx{p}")

        @blk.sync
        def _(sy, k=k, p=p, kvs=kvs, pv=kvs.n, wt=st["sS"][p]):
            sy.wait_ge(kvs.h, pv)
            if wt is not None:
                sy.wait_ge(B.sSq.h, wt)
            sy.dma_start(out=kvx[:, p * T:(p + 1) * T], in_=xin_t[:, k, :]) \
                .then_inc(kvs.h, 16)
        kvs.n += 16
        dx_t = kvs.n
        base = k * (T + 1)

        @blk.scalar
        def _(s, p=p, kvs=kvs, dx_t=dx_t, wt=st["sT"][p], base=base):
            s.wait_ge(kvs.h, dx_t)
            if wt is not None:
                s.wait_ge(B.sT.h, wt)
            s.copy(xnres[:, base + 1:base + T + 1], kvx[:, p * T:(p + 1) * T])
            s.activation(sqb[:, p * T:(p + 1) * T],
                         xnres[:, base + 1:base + T + 1], AF.Square)
        st["sS"][p] = B.sSq.n
        sv = B.sSq.n

        @blk.tensor
        def _(t, k=k, p=p, sv=sv, base=base):
            t.wait_ge(B.sSq.h, sv)
            for n in range(4):
                last = t.matmul(B.PA[0:1, n * 512:(n + 1) * 512], B.ones[:, :],
                         xnres[:, base + 1 + n * 512:base + 1 + (n + 1) * 512],
                         start=(k == 0), stop=(k == NT - 1))
                last = t.matmul(B.PA[64:65, n * 512:(n + 1) * 512], B.ones[:, :],
                         sqb[:, p * T + n * 512:p * T + (n + 1) * 512],
                         start=(k == 0), stop=(k == NT - 1),
                         tile_position=(0, 64))
            t.drain()
            t.nop(nofuse=True).then_inc(B.sT.h, 1)
        B.sT.n += 1
        st["sT"][p] = B.sT.n
    return tile


def emit_ln(B, blk, xin, prefix, p1_done=False):
    P = B.pool
    if not p1_done:
        tile = emit_ln_p1(B, blk, xin, alloc=True)
        for k in range(NT):
            tile(k)
    xnres = P.view("xnres")
    xrbv = P.view("xrb")
    srow = xrbv[:, 0:3 * 2 * T].bitcast(F32)
    abrow = xrbv[:, 6 * T:10 * T].bitcast(F32)
    bcast = xrbv[:, 10 * T:14 * T].bitcast(F32)
    scrf = xrbv[:, 14 * T:16 * T].bitcast(F32)

    def xo(k):
        return xnres[:, k * (T + 1):(k + 1) * (T + 1)]

    emit_stats_tail(B, blk, srow, abrow, bcast,
                    B.PA[0:1, :], B.PA[64:65, :])

    bc_v = B.sVq.n
    for k in range(NT):
        p = k % 2
        xsp = B.dsem(f"xnsp{p}")

        @blk.vector
        def _(v, k=k, bc_v=bc_v):
            v.wait_ge(B.sVq.h, bc_v)
            xk = xo(k)
            v.tensor_tensor(scrf[:, :], xk[:, 1:T + 1], bcast[:, 0:T], ALU.mult)
            v.tensor_tensor(xk[:, 1:T + 1], scrf[:, :], bcast[:, T:2 * T],
                            ALU.add)
            if prefix:
                v.memset(xk[:, 0:1], 0.0)
                v.tensor_copy(B.bcol[:, k:k + 1], xk[:, T:T + 1])
            else:
                v.tensor_scalar(xk[:, 0:1], B.bcol[:, k:k + 1], B.sflag[:, 0:1],
                                None, ALU.mult)
        vv2 = B.sVq.n

        if not prefix:
            @blk.scalar
            def _(s, k=k, p=p, xsp=xsp, vv2=vv2):
                s.wait_ge(B.sVq.h, vv2)
                s.dma_start(out=B.h_xn.rearrange("(t p) n -> p t n", p=128)[:, k, :],
                            in_=xo(k)).then_inc(xsp.h, 16)
            xsp.n += 16

    B.barrier(blk)
    P.freeb("kvx", "sqb")


def emit_stats_tail(B, blk, srow, abrow, bcast, srcX=None, srcQ=None):
    tv = B.sT.n
    if srcX is None:
        srcX, srcQ = B.PA[0:1, :], B.PB[0:1, :]

    @blk.vector
    def _(v, tv=tv):
        v.wait_ge(B.sT.h, tv)
        sx, e2, mu = srow[0:1, 0:T], srow[0:1, T:2 * T], srow[0:1, 2 * T:3 * T]
        v.tensor_copy(sx, srcX)
        v.tensor_copy(e2, srcQ)
        v.tensor_scalar(mu, sx, 1.0 / C, None, ALU.mult)
        v.tensor_scalar(e2, e2, 1.0 / C, None, ALU.mult)
        v.tensor_tensor(sx, mu, mu, ALU.mult)
        v.tensor_tensor(e2, e2, sx, ALU.subtract)
        v.tensor_scalar(e2, e2, EPS, None, ALU.add)
    vv = B.sVq.n

    @blk.scalar
    def _(s, vv=vv):
        s.wait_ge(B.sVq.h, vv)
        s.activation(srow[0:1, 0:T], srow[0:1, T:2 * T], AF.Ln)
        s.activation(abrow[0:1, 0:T], srow[0:1, 0:T], AF.Exp, scale=-0.5)
    sv = B.sSq.n

    @blk.vector
    def _(v, sv=sv):
        v.wait_ge(B.sSq.h, sv)
        v.tensor_tensor(abrow[0:1, T:2 * T], srow[0:1, 2 * T:3 * T],
                        abrow[0:1, 0:T], ALU.mult)
        v.tensor_scalar(abrow[0:1, T:2 * T], abrow[0:1, T:2 * T], -1.0, None,
                        ALU.mult)
    vv = B.sVq.n

    @blk.tensor
    def _(t, vv=vv):
        t.wait_ge(B.sVq.h, vv)
        for n in range(4):
            last = t.matmul(B.PA[:, n * 512:(n + 1) * 512], B.onesrow[:, :],
                     abrow[0:1, n * 512:(n + 1) * 512], start=True, stop=True)
            last = t.matmul(B.PB[:, n * 512:(n + 1) * 512], B.onesrow[:, :],
                     abrow[0:1, T + n * 512:T + (n + 1) * 512], start=True, stop=True)
        t.drain()
        t.nop(nofuse=True).then_inc(B.sT.h, 1)
    B.sT.n += 1
    tv = B.sT.n

    @blk.vector
    def _(v, tv=tv):
        v.wait_ge(B.sT.h, tv)
        v.tensor_copy(bcast[:, 0:T], B.PA[:, :])
        v.tensor_copy(bcast[:, T:2 * T], B.PB[:, :])


def emit_mm(B, blk, w_dma):
    P = B.pool
    xnres = P.view("xnres")
    w1sb = P.view("w1sb")
    td1sb = P.alloc("td1sb", NT * 64, BF16)
    td2sb = P.alloc("td2sb", C, BF16, parts=65)
    scrA = P.alloc("scrA", T, BF16)
    scrB = P.alloc("scrB", 2 * T, BF16)
    mmb = scrB[:, 0:T]
    for r in range(4):
        P.alloc_top(f"mmr{r}", T, BF16, parts=33)

    wpre = B.dsem("wpre")

    @blk.gpsimd
    def _(g, wpre=wpre, pv=wpre.n):
        g.wait_ge(wpre.h, pv)
        g.dma_start(out=td1sb, in_=B.i_td1.rearrange("(t p) m -> p t m", p=128)).then_inc(wpre.h, 16)
        g.dma_start(out=td2sb, in_=B.i_td2).then_inc(wpre.h, 16)
    wpre.n += 32
    td_dma = wpre.n

    sT_done = [None, None]
    for k in range(NT):
        p = k % 2
        xo = xnres[:, k * (T + 1):(k + 1) * (T + 1)]

        @blk.vector
        def _(v, k=k, p=p, xo=xo, wt=sT_done[p]):
            if wt is not None:
                v.wait_ge(B.sT.h, wt)
            v.tensor_tensor(scrA[:, :], xo[:, 0:T], xo[:, 1:T + 1], ALU.subtract)
            v.scalar_tensor_tensor(scrB[:, p * T:(p + 1) * T], scrA[:, :],
                                   B.vecs[:, 5 * NT + k:5 * NT + k + 1],
                                   xo[:, 1:T + 1], ALU.mult, ALU.add)
        vv = B.sVq.n

        @blk.tensor
        def _(t, k=k, p=p, vv=vv, wpre=wpre, w_dma=w_dma):
            t.wait_ge(B.sVq.h, vv)
            if k == 0:
                t.wait_ge(wpre.h, w_dma)
            for n in range(4):
                last = t.matmul(B.PA[:, n * 512:(n + 1) * 512],
                         w1sb[:, k * 128:(k + 1) * 128],
                         scrB[:, p * T + n * 512:p * T + (n + 1) * 512],
                         start=(k == 0), stop=(k == NT - 1))
            t.drain()
            t.nop(nofuse=True).then_inc(B.sT.h, 1)
        B.sT.n += 1
        sT_done[p] = B.sT.n

    tv = B.sT.n

    @blk.scalar
    def _(s, tv=tv):
        s.wait_ge(B.sT.h, tv)
        s.activation(mmb[:, :], B.PA[:, :], AF.Tanh)
    sv = B.sSq.n

    mms = B.dsem("mmr")

    @blk.gpsimd
    def _(g, sv=sv, mms=mms, pv=mms.n):
        g.wait_ge(mms.h, pv)
        g.wait_ge(B.sSq.h, sv)
        for r in range(4):
            g.dma_start(out=B.pool.view(f"mmr{r}")[0:32, :],
                        in_=mmb[32 * r:32 * (r + 1), :]).then_inc(mms.h, 16)
    mms.n += 4 * 16
    gv = mms.n

    @blk.vector
    def _(v, gv=gv, mms=mms):
        v.wait_ge(mms.h, gv)
        for r in range(4):
            v.memset(B.pool.view(f"mmr{r}")[32:33, :], 1.0)

    B.barrier(blk)
    P.freeb("w1sb", "scrA", "scrB")
    return td_dma


def emit_mixer_r(B, blk, r):
    P = B.pool
    xnres = P.view("xnres")
    xrb = P.view("xrb")
    mv = P.view(f"mmr{r}")
    w2v = P.view("w2sb")
    scrA = P.alloc("mxA", T, BF16)
    mscr = P.alloc("mxB", 2 * T, BF16)
    scrC = P.alloc("mxC", T, BF16)
    sS_par = [None, None]
    sV_par = [None, None]
    for c in range(NT):
        p = c % 2
        PP = B.PA if p == 0 else B.PB
        xo = xnres[:, c * (T + 1):(c + 1) * (T + 1)]

        @blk.tensor
        def _(t, c=c, PP=PP, wts=sS_par[p]):
            if wts is not None:
                t.wait_ge(B.sSq.h, wts)
            for n in range(4):
                last = t.matmul(PP[:, n * 512:(n + 1) * 512],
                         w2v[:, r * C + c * 128:r * C + (c + 1) * 128],
                         mv[:, n * 512:(n + 1) * 512], start=True, stop=True)
            last.then_inc(B.sT.h, 1)
        B.sT.n += 1
        tv = B.sT.n

        @blk.scalar
        def _(s, p=p, PP=PP, tv=tv, wv=sV_par[p]):
            s.wait_ge(B.sT.h, tv)
            if wv is not None:
                s.wait_ge(B.sVq.h, wv)
            s.copy(mscr[:, p * T:(p + 1) * T], PP[:, :])
        sS_par[p] = B.sSq.n
        sv = B.sSq.n

        @blk.vector
        def _(v, c=c, p=p, xo=xo, sv=sv):
            v.wait_ge(B.sSq.h, sv)
            v.tensor_tensor(scrA[:, :], xo[:, 0:T], xo[:, 1:T + 1], ALU.subtract)
            v.tensor_tensor(scrC[:, :], scrA[:, :], mscr[:, p * T:(p + 1) * T],
                            ALU.mult)
            v.tensor_tensor(xrb[:, c * T:(c + 1) * T], scrC[:, :], xo[:, 1:T + 1],
                            ALU.add)
        sV_par[p] = B.sVq.n

    B.barrier(blk)
    P.freeb("mxA", "mxB", "mxC")


def bigmm_prefetch(B, blk, w_in):
    P = B.pool
    wslab = P.alloc("wslab", 2 * NT * 128, BF16)
    pre_t = {}
    for m in (0, 1):
        wss = B.dsem(f"wsl{m % 2}")

        @blk.gpsimd
        def _(g, m=m, wss=wss, pv=wss.n):
            g.wait_ge(wss.h, pv)
            g.dma_start(out=wslab[:, (m % 2) * NT * 128:((m % 2) + 1) * NT * 128],
                        in_=w_in[m].rearrange("k p c -> p k c")).then_inc(wss.h, 16)
        wss.n += 16
        pre_t[m] = wss.n
    return (wslab, 2, pre_t)


def emit_bigmm_cm(B, blk, w_in, src_sb, spill_to, bias_col, pre=None,
                  tail_cb=None):
    P = B.pool
    if pre is None:
        wslab = P.alloc("wslab", 2 * NT * 128, BF16)
        npre, pre_t = 0, {}
    else:
        wslab, npre, pre_t = pre
    evb = P.alloc("evb", T, BF16)
    sT_done = [None, None]
    sS_done = [None, None]
    spill_last = None
    for m in range(NT):
        p = m % 2
        PP = B.PA if p == 0 else B.PB

        wss = B.dsem(f"wsl{p}")
        if m >= npre:
            @blk.gpsimd
            def _(g, m=m, p=p, wss=wss, pv=wss.n, wt=sT_done[p]):
                g.wait_ge(wss.h, pv)
                if wt is not None:
                    g.wait_ge(B.sT.h, wt)
                g.dma_start(out=wslab[:, p * NT * 128:(p + 1) * NT * 128],
                            in_=w_in[m].rearrange("k p c -> p k c")).then_inc(wss.h, 16)
            wss.n += 16
            dw_t = wss.n
        else:
            dw_t = pre_t[m]

        @blk.tensor
        def _(t, m=m, p=p, PP=PP, wss=wss, dw_t=dw_t, wt=sS_done[p]):
            t.wait_ge(wss.h, dw_t)
            if wt is not None:
                t.wait_ge(B.sSq.h, wt)
            for kk in range(NT):
                for n in range(4):
                    last = t.matmul(PP[:, n * 512:(n + 1) * 512],
                             wslab[:, p * NT * 128 + kk * 128:p * NT * 128 + (kk + 1) * 128],
                             src_sb[:, kk * T + n * 512:kk * T + (n + 1) * 512],
                             start=(kk == 0), stop=(kk == NT - 1))
            last.then_inc(B.sT.h, 1)
        B.sT.n += 1
        sT_done[p] = B.sT.n
        tv = B.sT.n

        sps = B.dsem(f"spl{p}")

        @blk.scalar
        def _(s, m=m, PP=PP, sps=sps, tv=tv, wsp=spill_last):
            s.wait_ge(B.sT.h, tv)
            if wsp is not None:
                s.wait_ge(wsp[0].h, wsp[1])
            if bias_col is not None:
                s.activation(evb[:, :], PP[:, :], AF.Identity,
                             bias=B.vecs[:, bias_col * NT + m:bias_col * NT + m + 1])
            else:
                s.copy(evb[:, :], PP[:, :])
            s.dma_start(out=spill_to.rearrange("(t p) n -> p t n", p=128)[:, m, :],
                        in_=evb[:, :]).then_inc(sps.h, 16)
        sps.n += 16
        sS_done[p] = B.sSq.n
        spill_last = (sps, sps.n)

    if tail_cb is not None:
        tail_cb()
    B.barrier(blk)
    P.freeb("wslab", "evb")


def emit_vmm_tm(B, blk, xrb):
    P = B.pool
    vwslab = P.alloc("vwslab", 2 * NT * 512, BF16)
    vbr = P.alloc("vbr", C, F32)
    evb = P.alloc("evb", 2 * 512, BF16)

    wpre = B.dsem("wpre")

    @blk.gpsimd
    def _(g, wpre=wpre):
        g.dma_start(out=vbr, in_=B.i_vbrow).then_inc(wpre.h, 16)
    wpre.n += 16
    vbr_t = wpre.n

    oc_done = [None, None]
    sV_done = [None, None]
    sp_done = [None, None]
    for oc in range(4):
        po = oc % 2

        wss = B.dsem(f"wsl{po}")

        @blk.gpsimd
        def _(g, oc=oc, po=po, wss=wss, pv=wss.n, wt=oc_done[po]):
            g.wait_ge(wss.h, pv)
            if wt is not None:
                g.wait_ge(B.sT.h, wt)
            g.dma_start(out=vwslab[:, po * NT * 512:(po + 1) * NT * 512],
                        in_=B.i_vw[oc].rearrange("k p c -> p k c")).then_inc(wss.h, 16)
        wss.n += 16
        dw_t = wss.n

        for tt in range(NT):
            pt = tt % 2
            PP = B.PA if pt == 0 else B.PB

            @blk.tensor
            def _(t, po=po, tt=tt, PP=PP, wss=B.dsem(f"wsl{po}"), dw_t=dw_t,
                  vbr_t=vbr_t, wpre=wpre, wt=sV_done[pt]):
                t.wait_ge(wss.h, dw_t)
                if wt is not None:
                    t.wait_ge(B.sVq.h, wt)
                for kk in range(NT):
                    last = t.matmul(PP[:, 0:512],
                             xrb[:, kk * T + tt * 128:kk * T + (tt + 1) * 128],
                             vwslab[:, po * NT * 512 + kk * 512:po * NT * 512 + (kk + 1) * 512],
                             start=(kk == 0), stop=(kk == NT - 1))
                last.then_inc(B.sT.h, 1)
            B.sT.n += 1
            tv = B.sT.n

            @blk.vector
            def _(v, oc=oc, pt=pt, PP=PP, tv=tv, sps=B.dsem(f"spl{pt}"),
                  wpre=wpre, vbr_t=vbr_t, wt=sp_done[pt]):
                v.wait_ge(B.sT.h, tv)
                v.wait_ge(wpre.h, vbr_t)
                if wt is not None:
                    v.wait_ge(sps.h, wt)
                v.tensor_tensor(evb[:, pt * 512:(pt + 1) * 512], PP[:, 0:512],
                                vbr[:, oc * 512:(oc + 1) * 512], ALU.add)
            sV_done[pt] = B.sVq.n
            vv = B.sVq.n

            sps = B.dsem(f"spl{pt}")

            @blk.scalar
            def _(s, oc=oc, tt=tt, pt=pt, sps=sps, vv=vv):
                s.wait_ge(B.sVq.h, vv)
                s.dma_start(out=B.h_v[tt * 128:(tt + 1) * 128, oc * 512:(oc + 1) * 512],
                            in_=evb[:, pt * 512:(pt + 1) * 512]).then_inc(sps.h, 16)
            sps.n += 16
            sp_done[pt] = sps.n
        oc_done[po] = B.sT.n

    B.barrier(blk)
    P.freeb("vwslab", "vbr", "evb")


def emit_wpath_head(B, blk, td_dma):
    P = B.pool
    xrb = P.view("xrb")
    td1sb = P.view("td1sb")
    ubuf = P.alloc("ubuf", T, BF16, parts=65)
    wpre = B.dsem("wpre")

    @blk.vector
    def _(v):
        v.memset(ubuf[64:65, :], 1.0)
    u1 = B.sVq.n

    @blk.tensor
    def _(t, u1=u1, wpre=wpre, td_dma=td_dma):
        t.wait_ge(B.sVq.h, u1)
        t.wait_ge(wpre.h, td_dma)
        for kk in range(NT):
            for n in range(4):
                last = t.matmul(B.PA[0:64, n * 512:(n + 1) * 512],
                         td1sb[:, kk * 64:(kk + 1) * 64],
                         xrb[:, kk * T + n * 512:kk * T + (n + 1) * 512],
                         start=(kk == 0), stop=(kk == NT - 1))
        last.then_inc(B.sT.h, 1)
    B.sT.n += 1
    tv_td1 = B.sT.n

    @blk.scalar
    def _(s, tv=tv_td1):
        s.wait_ge(B.sT.h, tv)
        s.activation(ubuf[0:64, :], B.PA[0:64, :], AF.Tanh)

    B.barrier(blk)
    P.freeb("td1sb")


def emit_wpath_tail(B, blk, prefix):
    P = B.pool
    ubuf = P.view("ubuf")
    td2sb = P.view("td2sb")
    scrF = P.alloc("scrF", 2 * T, F32)
    evf = P.alloc("evf", 2 * T, F32)
    bfr = P.alloc("bfr", 2 * T, F32)
    fek = P.alloc("fek", 2 * T, BF16)
    fonem = P.alloc("fonem", T, BF16)
    btr = P.alloc("btr", 4 * 128, F32)
    krow = P.alloc("krow", 4 * T, BF16)
    few = P.alloc("few", 2 * T, F32)
    if not prefix:
        fqs = P.alloc("fqs", 2 * T, F32)
        feq = P.alloc("feq", 2 * T, BF16)
        btrp = P.alloc("btrp", 128, F32)
        qrow = P.alloc("qrow", 4 * T, BF16)

        @blk.vector
        def _(v):
            v.memset(btrp[:, :], 0.0)

    k_t = B.h_k.rearrange("(t p) n -> p t n", p=128)
    q_t = B.h_q.rearrange("(t p) n -> p t n", p=128)
    ko_t = (B.h_kst if prefix else B.h_k16).rearrange("(t p) n -> p t n", p=128)
    q16_t = B.h_q16.rearrange("(t p) n -> p t n", p=128)

    def sl(buf, p):
        return buf[:, p * T:(p + 1) * T]

    sXd = [None, None]
    sS_done = [None, None]
    vAd = [None, None]
    fsp_done = [None, None, None, None]
    ld_tgt = {}
    va_c = {}

    def sl4(buf, q):
        return buf[:, q * T:(q + 1) * T]

    def emit_tail(c):
        p = c % 2
        q = c % 4
        krs = B.dsem(f"krw{q}")
        fsps = B.dsem(f"fsp{q}")

        @blk.scalar
        def _(s, p=p, va=va_c[c]):
            s.wait_ge(B.sVq.h, va)
            s.activation(sl(few, p), sl(evf, p), AF.Exp)
            s.activation(sl(fek, p), sl(bfr, p), AF.Exp)
            if not prefix:
                s.activation(sl(feq, p), sl(fqs, p), AF.Exp, bias=B.cbln8[:, 0:1])
        sS_done[p] = B.sSq.n
        sz = B.sSq.n

        @blk.vector
        def _(v, p=p, q=q, sz=sz, krs=krs, lt=ld_tgt[c]):
            v.wait_ge(B.sSq.h, sz)
            v.wait_ge(krs.h, lt)
            v.tensor_scalar(fonem[:, :], sl(few, p), -1.0, 1.0, ALU.mult, ALU.add)
            v.tensor_tensor(fonem[:, :], fonem[:, :], sl(fek, p), ALU.mult)
            v.tensor_tensor(sl4(krow, q), sl4(krow, q), fonem[:, :], ALU.mult)
            if not prefix:
                v.tensor_tensor(sl4(qrow, q), sl4(qrow, q), sl(feq, p), ALU.mult)
        vc_ = B.sVq.n

        @blk.gpsimd
        def _(g, c=c, q=q, fsps=fsps, vc_=vc_, pv=fsps.n):
            g.wait_ge(fsps.h, pv)
            g.wait_ge(B.sVq.h, vc_)
            g.dma_start(out=ko_t[:, c, :], in_=sl4(krow, q)).then_inc(fsps.h, 16)
            g.dma_start(out=B.h_btab[c],
                        in_=btr[:, q * 128:(q + 1) * 128]).then_inc(fsps.h, 16)
            if not prefix:
                g.dma_start(out=q16_t[:, c, :], in_=sl4(qrow, q)).then_inc(fsps.h, 16)
        fsps.n += 16 * (2 if prefix else 3)
        fsp_done[q] = fsps.n

    for c in range(NT):
        p = c % 2
        q = c % 4
        PP = B.PA if p == 0 else B.PB
        krs = B.dsem(f"krw{q}")
        fsps = B.dsem(f"fsp{q}")

        @blk.sync
        def _(sy, c=c, q=q, krs=krs, pv=krs.n, fsps=fsps, wt=fsp_done[q]):
            sy.wait_ge(krs.h, pv)
            if wt is not None:
                sy.wait_ge(fsps.h, wt)
            sy.dma_start(out=sl4(krow, q), in_=k_t[:, c, :]).then_inc(krs.h, 16)
            if not prefix:
                sy.dma_start(out=sl4(qrow, q), in_=q_t[:, c, :]).then_inc(krs.h, 16)
        krs.n += 16 * (1 if prefix else 2)
        ld_tgt[c] = krs.n

        @blk.tensor
        def _(t, c=c, PP=PP, wt=sXd[p]):
            if wt is not None:
                t.wait_ge(B.sSq.h, wt)
            for n in range(4):
                last = t.matmul(PP[:, n * 512:(n + 1) * 512],
                         td2sb[:, c * 128:(c + 1) * 128],
                         ubuf[:, n * 512:(n + 1) * 512], start=True, stop=True)
            last.then_inc(B.sT.h, 1)
        B.sT.n += 1
        tv = B.sT.n

        @blk.scalar
        def _(s, p=p, PP=PP, tv=tv, wt=vAd[p]):
            s.wait_ge(B.sT.h, tv)
            if wt is not None:
                s.wait_ge(B.sVq.h, wt)
            s.activation(scrF[:, p * T:(p + 1) * T], PP[:, :], AF.Exp)
        sXd[p] = B.sSq.n
        sx = B.sSq.n

        @blk.vector
        def _(v, c=c, p=p, q=q, sx=sx, ws=sS_done[p], fsps=fsps, wg=fsp_done[q]):
            v.wait_ge(B.sSq.h, sx)
            if ws is not None:
                v.wait_ge(B.sSq.h, ws)
            if wg is not None:
                v.wait_ge(fsps.h, wg)
            v.tensor_scalar(sl(evf, p), scrF[:, p * T:(p + 1) * T], -1.0, -5.0,
                            ALU.mult, ALU.max)
            v.tensor_tensor_scan(sl(bfr, p), B.cmask[:, :], sl(evf, p), 0.0,
                                 ALU.mult, ALU.add)
            v.tensor_copy(btr[:, q * 128:(q + 1) * 128].rearrange(
                              "p (a b) -> p a b", b=1),
                          sl(bfr, p).rearrange("p (a b) -> p a b", b=16)[:, :, 15:16])
            b4 = sl(bfr, p).rearrange("p (k a b) -> p k a b", a=8, b=16)
            if prefix:
                bt8 = btr[:, q * 128:(q + 1) * 128] \
                    .rearrange("p (k a) -> p k a", a=8)[:, :, 7:8] \
                    .unsqueeze(-1).broadcast_to((128, NT, 8, 16))
                v.tensor_tensor(b4, bt8, b4, ALU.subtract)
            else:
                bt3 = btr[:, q * 128:(q + 1) * 128].rearrange("p (k a) -> p k a", a=8)
                bp3 = btrp[:, :].rearrange("p (k a) -> p k a", a=8)
                v.tensor_copy(bp3[:, :, 1:8], bt3[:, :, 0:7])
                bp4 = btrp[:, :] \
                    .rearrange("p (k a) -> p k a", a=8) \
                    .unsqueeze(-1).broadcast_to((128, NT, 8, 16))
                v.tensor_tensor(sl(fqs, p).rearrange("p (k a b) -> p k a b",
                                                     a=8, b=16),
                                b4, bp4, ALU.subtract)
                bt4 = btr[:, q * 128:(q + 1) * 128] \
                    .rearrange("p (k a) -> p k a", a=8) \
                    .unsqueeze(-1).broadcast_to((128, NT, 8, 16))
                v.tensor_tensor(b4, bt4, b4, ALU.subtract)
        vAd[p] = B.sVq.n
        va_c[c] = B.sVq.n

        if c > 0:
            emit_tail(c - 1)
    emit_tail(NT - 1)

    B.barrier(blk)
    names = ["ubuf", "td2sb", "scrF", "evf", "bfr", "few", "fek", "fonem",
             "btr", "krow"]
    if not prefix:
        names += ["fqs", "feq", "btrp", "qrow"]
    P.freeb(*names)


def emit_scan(B, blk, prefix, chunk_cb=None):
    P = B.pool
    ko_t = (B.h_kst if prefix else B.h_k16).rearrange("(t p) n -> p t n", p=128)
    q16_t = B.h_q16.rearrange("(t p) n -> p t n", p=128)
    y_t = B.h_y.rearrange("(t p) n -> p t n", p=128)

    strk = P.alloc("strk", 2 * T, BF16)
    strv = P.alloc("strv", 2 * T, BF16)
    btres = P.alloc("btres", NT * 128, F32)
    khT = P.alloc("khT", 2 * 256, BF16)
    drowp = [P.alloc(f"drow{p}", NT, F32) for p in range(2)]
    strkv = [strk[:, 0:T], strk[:, T:2 * T]]
    if not prefix:
        strq = P.alloc("strq", 2 * T, BF16)
        bt9 = P.alloc("bt9", 2 * NT * 9, F32)
        tabin = P.alloc("tabin", 2 * 640, F32)
        tabex = P.alloc("tabex", 2 * 640, BF16)
        fkst = [P.alloc(f"fkst{p}", T, BF16) for p in range(2)]
        fqh = [P.alloc(f"fqh{p}", T, BF16) for p in range(2)]
        fkd = [P.alloc(f"fkd{p}", T, BF16) for p in range(2)]
        fq32 = [P.alloc(f"fq32{p}", NT * 64, BF16) for p in range(2)]
        fk32 = [P.alloc(f"fk32{p}", NT * 64, BF16) for p in range(2)]
        fq64 = [P.alloc(f"fq64{p}", NT * 64, BF16) for p in range(2)]
        fk64 = [P.alloc(f"fk64{p}", NT * 64, BF16) for p in range(2)]
        Asb = P.alloc("Asb", 2 * 512, BF16)
        ybc = P.alloc("ybc", 2 * T, BF16)
        strqv = [strq[:, 0:T], strq[:, T:2 * T]]
        bt9r = bt9.rearrange("p (q t c) -> p q t c", q=2, c=9)
        tabinv = [tabin[:, 0:640], tabin[:, 640:1280]]
        tabexv = [tabex[:, 0:640], tabex[:, 640:1280]]
    btres_r = btres.rearrange("p (m c) -> p m c", c=128)

    pA = None
    pO = [B.PA[:, 1024:1280], B.PA[:, 1536:1792]]
    pST = B.PB[:, 0:1024]
    pTRb = B.PB[:, 1024:2048].bitcast(BF16)
    pTR = [pTRb[:, 0:256], pTRb[:, 1024:1280]]

    war_t = [None, None]
    sup_v = [None]
    exps_done = [None, None]
    fact_v = {}
    fact_k = {}
    fact_s = {}
    ld_tgt = {}
    ysp = B.dsem("ysp")
    bts = B.dsem("bts")

    @blk.gpsimd
    def _(g, bts=bts, pv=bts.n):
        g.wait_ge(bts.h, pv)
        g.dma_start(out=btres[:, :],
                    in_=B.h_btab.rearrange("m p c -> p m c")).then_inc(bts.h, 16)
    bts.n += 16
    bt_t = bts.n

    if not prefix:
        @blk.vector
        def _(v):
            v.memset(bt9r[:, :, :, 0:1], 0.0)

    def loads(g, ci):
        p = ci % 2
        scs = B.dsem(f"scn{p}")
        g.wait_ge(scs.h, scs.n)
        if war_t[p] is not None:
            g.wait_ge(B.sT.h, war_t[p])
        if not prefix:
            g.dma_start(out=strqv[p], in_=q16_t[:, :, ci * CH:(ci + 1) * CH]) \
                .then_inc(scs.h, 16)
            scs.n += 16
        g.dma_start(out=strkv[p], in_=ko_t[:, :, ci * CH:(ci + 1) * CH]) \
            .then_inc(scs.h, 16)
        g.dma_start(out=strv[:, p * T:(p + 1) * T],
                    in_=B.h_v[ci * CH:(ci + 1) * CH, :]).then_inc(scs.h, 16)
        scs.n += 32

    def factors_p1(ci):
        p = ci % 2
        tabv = None
        if prefix:
            fact_k[ci] = None
            fact_v[ci] = None
        else:
            @blk.vector
            def _(v, ci=ci, p=p, bt_t=bt_t, bts=bts, ws=exps_done[p]):
                v.wait_ge(bts.h, bt_t)
                if ws is not None:
                    v.wait_ge(B.sSq.h, ws)
                bt = bt9r[:, p]
                v.tensor_copy(bt[:, :, 1:9], btres_r[:, :, ci * 8:(ci + 1) * 8])
                tbi = tabinv[p]
                v.tensor_tensor(tbi[:, 0:128].rearrange("p (t c) -> p t c", c=8),
                                bt[:, :, 8:9].broadcast_to((128, NT, 8)),
                                bt[:, :, 1:9], ALU.subtract)
                v.tensor_tensor(tbi[:, 128:192].rearrange("p (t c) -> p t c", c=4),
                                bt[:, :, 4:8],
                                bt[:, :, 4:5].broadcast_to((128, NT, 4)),
                                ALU.subtract)
                v.tensor_tensor(tbi[:, 192:256].rearrange("p (t c) -> p t c", c=4),
                                bt[:, :, 4:5].broadcast_to((128, NT, 4)),
                                bt[:, :, 1:5], ALU.subtract)
                v.tensor_tensor(tbi[:, 256:384].rearrange("p (t c) -> p t c", c=8),
                                bt[:, :, 0:8], bt[:, :, 1:9], ALU.subtract)
                v.tensor_tensor(tbi[:, 384:512].rearrange("p (t c) -> p t c", c=8),
                                bt[:, :, 1:9], bt[:, :, 0:8], ALU.subtract)
                v.tensor_copy(tbi[:, 512:640].rearrange("p (t c) -> p t c", c=8),
                              bt[:, :, 0:8])
            tabv = B.sVq.n

        @blk.scalar
        def _(s, ci=ci, p=p, tabv=tabv, wv=sup_v[0], bts=bts, bt_t=bt_t):
            s.wait_ge(bts.h, bt_t)
            if tabv is not None:
                s.wait_ge(B.sVq.h, tabv)
            if wv is not None:
                s.wait_ge(B.sVq.h, wv)
            if not prefix:
                s.activation(tabexv[p], tabinv[p], AF.Exp)
            s.activation(drowp[p][:, :],
                         btres_r[:, :, ci * 8 + 7:ci * 8 + 8]
                         .rearrange("p m c -> p (m c)"), AF.Exp)
        exps_done[p] = B.sSq.n
        fact_s[ci] = B.sSq.n

    def factors_p2a(ci):
        p = ci % 2
        scs = B.dsem(f"scn{p}")
        c0 = B.sVq.n

        @blk.vector
        def _(v, ci=ci, p=p, scs=scs, dx_t=ld_tgt[ci], ws=exps_done[p],
              wt=war_t[p]):
            v.wait_ge(scs.h, dx_t)
            v.wait_ge(B.sSq.h, ws)
            if wt is not None:
                v.wait_ge(B.sT.h, wt)
            te = tabexv[p]
            k4 = strkv[p].rearrange("p (t a b) -> p t a b", a=8, b=16)
            q4 = strqv[p].rearrange("p (t a b) -> p t a b", a=8, b=16)
            est_b = te[:, 0:128].rearrange("p (t c) -> p t c", c=8) \
                .unsqueeze(-1).broadcast_to((128, NT, 8, 16))
            v.tensor_tensor(fkst[p][:, :].rearrange("p (t a b) -> p t a b",
                                                    a=8, b=16),
                            k4, est_b, ALU.mult)
            ein_b = te[:, 512:640].rearrange("p (t c) -> p t c", c=8) \
                .unsqueeze(-1).broadcast_to((128, NT, 8, 16))
            v.tensor_tensor(fqh[p][:, :].rearrange("p (t a b) -> p t a b",
                                                   a=8, b=16),
                            q4, ein_b, ALU.mult)
            rd_b = te[:, 256:384].rearrange("p (t c) -> p t c", c=8) \
                .unsqueeze(-1).broadcast_to((128, NT, 8, 16))
            v.tensor_tensor(fkd[p][:, :].rearrange("p (t a b) -> p t a b",
                                                   a=8, b=16),
                            k4, rd_b, ALU.mult)
        fact_k[ci] = c0 + 1

    def factors_p2b(ci):
        p = ci % 2
        scs = B.dsem(f"scn{p}")

        @blk.vector
        def _(v, ci=ci, p=p, scs=scs, dx_t=ld_tgt[ci], ws=exps_done[p],
              wt=war_t[p]):
            v.wait_ge(scs.h, dx_t)
            v.wait_ge(B.sSq.h, ws)
            if wt is not None:
                v.wait_ge(B.sT.h, wt)
            te = tabexv[p]
            k4 = strkv[p].rearrange("p (t a b) -> p t a b", a=8, b=16)
            q4 = strqv[p].rearrange("p (t a b) -> p t a b", a=8, b=16)
            e64q_b = te[:, 128:192].rearrange("p (t c) -> p t c", c=4) \
                .unsqueeze(-1).broadcast_to((128, NT, 4, 16))
            v.tensor_tensor(fq64[p][:, :].rearrange("p (t a b) -> p t a b",
                                                    a=4, b=16),
                            strqv[p].rearrange("p (t s) -> p t s", s=128)
                            [:, :, 64:128].rearrange("p t (a b) -> p t a b", b=16),
                            e64q_b, ALU.mult)
            e64k_b = te[:, 192:256].rearrange("p (t c) -> p t c", c=4) \
                .unsqueeze(-1).broadcast_to((128, NT, 4, 16))
            v.tensor_tensor(fk64[p][:, :].rearrange("p (t a b) -> p t a b",
                                                    a=4, b=16),
                            strkv[p].rearrange("p (t s) -> p t s", s=128)
                            [:, :, 0:64].rearrange("p t (a b) -> p t a b", b=16),
                            e64k_b, ALU.mult)
            q32_4 = fq32[p][:, :].rearrange("p (t a b) -> p t a b", a=4, b=16)
            k32_4 = fk32[p][:, :].rearrange("p (t a b) -> p t a b", a=4, b=16)
            dd = te[:, 384:512].rearrange("p (t c) -> p t c", c=8)
            v.tensor_copy(q32_4[:, :, 0::2, :], q4[:, :, 2::4, :])
            v.tensor_tensor(q32_4[:, :, 1::2, :], q4[:, :, 3::4, :],
                            dd[:, :, 2::4].unsqueeze(-1)
                            .broadcast_to((128, NT, 2, 16)), ALU.mult)
            v.tensor_copy(k32_4[:, :, 1::2, :], k4[:, :, 1::4, :])
            v.tensor_tensor(k32_4[:, :, 0::2, :], k4[:, :, 0::4, :],
                            dd[:, :, 1::4].unsqueeze(-1)
                            .broadcast_to((128, NT, 2, 16)), ALU.mult)
        fact_v[ci] = B.sVq.n

    def supd_emit_a(ci, tS):
        p = ci % 2

        @blk.vector
        def _(v, p=p, tS=tS, ws=fact_s[ci]):
            v.wait_ge(B.sT.h, tS)
            v.wait_ge(B.sSq.h, ws)
            sp = B.spar
            drow_b = drowp[p][:, :].unsqueeze(-1).broadcast_to((128, NT, 64))
            v.tensor_tensor(B.S[:, 2048:3072], B.S[:, sp * 1024:(sp + 1) * 1024],
                            drow_b, ALU.mult)
            v.tensor_tensor(B.S[:, (1 - sp) * 1024:(2 - sp) * 1024],
                            B.S[:, 2048:3072], pST, ALU.add)
        B.spar = 1 - B.spar
        if prefix:
            sup_v[0] = B.sVq.n

    def supd_emit_b(ci, tOmm):
        @blk.vector
        def _(v, tOmm=tOmm):
            v.wait_ge(B.sT.h, tOmm)
            sp = B.spar
            v.tensor_copy(B.Sbf[:, :], B.S[:, sp * 1024:(sp + 1) * 1024])
        sup_v[0] = B.sVq.n

    @blk.gpsimd
    def _(g):
        loads(g, 0)
    ld_tgt[0] = B.dsem("scn0").n
    factors_p1(0)
    if not prefix:
        factors_p2a(0)
        factors_p2b(0)

    pre_done = False
    for ci in range(NCH):
        p = ci % 2
        if ci + 1 < NCH:
            @blk.gpsimd
            def _(g, ci=ci):
                loads(g, ci + 1)
            ld_tgt[ci + 1] = B.dsem(f"scn{(ci + 1) % 2}").n
        hooks = {}
        if ci + 1 < NCH:
            hooks[1] = [lambda ci=ci: factors_p1(ci + 1)]
            if not prefix:
                hooks[3] = [lambda ci=ci: factors_p2a(ci + 1)]
                hooks[4] = [lambda ci=ci: factors_p2b(ci + 1)]
        nxt = None
        if ci + 1 < NCH:
            nxt = dict(p=1 - p, scs=B.dsem(f"scn{(ci + 1) % 2}"),
                       gK=(lambda ci=ci: fact_k[ci + 1]),
                       ld=(lambda ci=ci: ld_tgt[ci + 1]))
        tiles_emit(B, blk, prefix, ci, p, pA, pO, pST, pTR, strv,
                   None if prefix else fkst, strkv, khT,
                   None if prefix else Asb,
                   None if prefix else (strqv, strkv, fkd, fq32, fk32,
                                        fq64, fk64, fqh),
                   None if prefix else ybc,
                   fact_k[ci], fact_v[ci], B.dsem(f"scn{p}"), ld_tgt[ci],
                   sup_v, ysp, hooks,
                   (lambda tS, ci=ci: supd_emit_a(ci, tS),
                    lambda tOmm, ci=ci: supd_emit_b(ci, tOmm)),
                   pre_done, nxt)
        pre_done = ci + 1 < NCH
        war_t[p] = B.sT.n
        if not prefix:
            oev = B.sSq.n

            @blk.gpsimd
            def _(g, ci=ci, p=p, oev=oev):
                g.wait_ge(B.sSq.h, oev)
                g.dma_start(out=y_t[:, :, ci * CH:(ci + 1) * CH],
                            in_=ybc[:, p * T:(p + 1) * T]
                            .rearrange("p (t c) -> p t c", c=128)) \
                    .then_inc(ysp.h, 16)
            ysp.n += 16
        if chunk_cb is not None:
            chunk_cb(ci)

    B.barrier(blk)
    names = ["strk", "strv", "btres", "khT", "drow0", "drow1"]
    if not prefix:
        names += ["strq", "bt9", "tabin", "tabex",
                  "fkst0", "fkst1", "fqh0", "fqh1", "fkd0", "fkd1",
                  "fq320", "fq321", "fk320", "fk321",
                  "fq640", "fq641", "fk640", "fk641", "Asb", "ybc"]
    P.freeb(*names)


def tiles_emit(B, blk, prefix, ci, p, pA, pO, pST, pTR, strv, fkst, strkv, khT,
               Asb, fmain, ybc, gK, gF, scs, ld_t, sup_v, ysp, hooks, supd_cb,
               pre_done, nxt):
    st = getattr(B, "_scn", None)
    if st is None or ci == 0:
        B._scn = {"khv": [None, None], "blk1": [None, None],
                  "aev": [None, None], "omm": [None, None],
                  "oev": [None, None], "pre_khv": None}
        st = B._scn
    if fmain is not None:
        fq16, fk16, fkd, fq32, fk32, fq64, fk64, fqh = fmain
    ksrc = strkv if prefix else fkst
    NP = NT // 2

    def transp(g, pp, kk, sc, ld):
        s = g % 2

        @blk.tensor
        def _(t, g=g, s=s, pp=pp, wk=st["khv"][s], kk=kk, sc=sc, ld=ld):
            if g == 0:
                if prefix:
                    t.wait_ge(sc.h, ld)
                else:
                    t.wait_ge(B.sVq.h, kk)
            if wk is not None:
                t.wait_ge(B.sVq.h, wk)
            for j in (0, 1):
                tile = 2 * g + j
                last = t.transpose(pTR[s][:, j * 128:(j + 1) * 128],
                                   ksrc[pp][:, tile * 128:(tile + 1) * 128],
                                   B.ident[:, :])
            last.then_inc(B.sT.h, 1)
        B.sT.n += 1
        return B.sT.n

    def khcopy(g, trv):
        s = g % 2

        @blk.vector
        def _(v, s=s, trv=trv, wb=st["blk1"][s]):
            v.wait_ge(B.sT.h, trv)
            if wb is not None:
                v.wait_ge(B.sT.h, wb)
            v.tensor_copy(khT[:, s * 256:(s + 1) * 256], pTR[s])
        st["khv"][s] = B.sVq.n
        return B.sVq.n

    if pre_done:
        khv_g = st["pre_khv"]
    else:
        tr0 = transp(0, p, gK, scs, ld_t)
        khv_g = khcopy(0, tr0)

    for g in range(NP):
        s = g % 2

        @blk.tensor
        def _(t, g=g, s=s, khv=khv_g, wa=st["aev"][s], sv=sup_v[0], gF=gF):
            t.wait_ge(B.sVq.h, khv)
            if g == 0:
                if sv is not None:
                    t.wait_ge(B.sVq.h, sv)
                if not prefix:
                    t.wait_ge(B.sVq.h, gF)
            if wa is not None:
                t.wait_ge(B.sVq.h, wa)
            for j in (0, 1):
                tile = 2 * g + j
                last = t.matmul(pST[0:64, tile * 64:(tile + 1) * 64],
                                khT[:, s * 256 + j * 128:s * 256 + j * 128 + 64],
                                strv[:, p * T + tile * 128:p * T + tile * 128 + 64],
                                start=True, stop=True, tile_position=(0, 0))
                last = t.matmul(pST[64:128, tile * 64:(tile + 1) * 64],
                                khT[:, s * 256 + j * 128 + 64:s * 256 + (j + 1) * 128],
                                strv[:, p * T + tile * 128 + 64:p * T + tile * 128 + 128],
                                start=True, stop=True, tile_position=(0, 64))
            if not prefix:
                for j in (0, 1):
                    tile = 2 * g + j
                    ts_ = tile * 128
                    tc_ = tile * 64
                    for hh in (0, 1):
                        hr = 64 * hh
                        base = s * 512 + j * 256 + 128 * hh
                        last = t.matmul(B.PA[0:128, base:base + 128],
                                        fkd[p][hr:hr + 64, ts_:ts_ + 128],
                                        fq16[p][hr:hr + 64, ts_:ts_ + 128],
                                        start=True, stop=True, tile_position=(hr, 0))
                        for J in (0, 2, 4, 6):
                            last = t.matmul(B.PA[J * 16:(J + 1) * 16,
                                                 base + (J + 1) * 16:base + (J + 2) * 16],
                                            fk16[p][hr:hr + 64, ts_ + J * 16:ts_ + (J + 1) * 16],
                                            fq16[p][hr:hr + 64, ts_ + (J + 1) * 16:ts_ + (J + 2) * 16],
                                            start=True, stop=True, tile_position=(hr, J * 16))
                        last = t.matmul(B.PA[0:32, base + 32:base + 64],
                                        fk32[p][hr:hr + 64, tc_:tc_ + 32],
                                        fq32[p][hr:hr + 64, tc_:tc_ + 32],
                                        start=True, stop=True, tile_position=(hr, 0))
                        last = t.matmul(B.PA[64:96, base + 96:base + 128],
                                        fk32[p][hr:hr + 64, tc_ + 32:tc_ + 64],
                                        fq32[p][hr:hr + 64, tc_ + 32:tc_ + 64],
                                        start=True, stop=True, tile_position=(hr, 64))
                        last = t.matmul(B.PA[0:64, base + 64:base + 128],
                                        fk64[p][hr:hr + 64, tc_:tc_ + 64],
                                        fq64[p][hr:hr + 64, tc_:tc_ + 64],
                                        start=True, stop=True, tile_position=(hr, 0))
            last.then_inc(B.sT.h, 1)
        B.sT.n += 1
        st["blk1"][s] = B.sT.n
        tb1 = B.sT.n

        if not prefix:
            @blk.vector
            def _(v, s=s, tb1=tb1, wo=st["omm"][s]):
                v.wait_ge(B.sT.h, tb1)
                if wo is not None:
                    v.wait_ge(B.sT.h, wo)
                utm_b = B.utm[:, :].unsqueeze(1).broadcast_to((128, 4, 128))
                v.tensor_tensor(Asb[:, s * 512:(s + 1) * 512],
                                B.PA[:, s * 512:(s + 1) * 512], utm_b, ALU.mult)
            st["aev"][s] = B.sVq.n
        aev_g = B.sVq.n

        if g == NP - 1:
            supd_cb[0](st["blk1"][s])

        if g + 1 < NP:
            trn = transp(g + 1, p, gK, scs, ld_t)
            khv_g = khcopy(g + 1, trn)

        for cb in hooks.get(g, []):
            cb()

        if prefix:
            continue

        @blk.tensor
        def _(t, g=g, s=s, aev=aev_g, wo=st["oev"][s]):
            t.wait_ge(B.sVq.h, aev)
            if wo is not None:
                t.wait_ge(B.sSq.h, wo)
            for j in (0, 1):
                tile = 2 * g + j
                ts_ = tile * 128
                tc_ = tile * 64
                oc = j * 128
                last = t.matmul(pO[s][0:64, oc:oc + 128], B.Sbf[0:64, tc_:tc_ + 64],
                                fqh[p][0:64, ts_:ts_ + 128], start=True, stop=False,
                                tile_position=(0, 0))
                last = t.matmul(pO[s][0:64, oc:oc + 128],
                                strv[:, p * T + ts_:p * T + ts_ + 64],
                                Asb[:, s * 512 + j * 256:s * 512 + j * 256 + 128],
                                start=False, stop=True, tile_position=(0, 0))
                last = t.matmul(pO[s][64:128, oc:oc + 128], B.Sbf[64:128, tc_:tc_ + 64],
                                fqh[p][64:128, ts_:ts_ + 128], start=True, stop=False,
                                tile_position=(64, 64))
                last = t.matmul(pO[s][64:128, oc:oc + 128],
                                strv[:, p * T + ts_ + 64:p * T + ts_ + 128],
                                Asb[:, s * 512 + j * 256 + 128:s * 512 + (j + 1) * 256],
                                start=False, stop=True, tile_position=(0, 64))
            last.then_inc(B.sT.h, 1)
        B.sT.n += 1
        st["omm"][s] = B.sT.n
        tO = B.sT.n

        @blk.scalar
        def _(sc, g=g, s=s, p=p, tO=tO, wy=ysp.n if (ci >= 2 and g == 0) else None):
            sc.wait_ge(B.sT.h, tO)
            if wy is not None:
                sc.wait_ge(ysp.h, wy)
            sc.copy(ybc[:, p * T + g * 256:p * T + (g + 1) * 256], pO[s])
        st["oev"][s] = B.sSq.n

        if g == NP - 1:
            supd_cb[1](tO)

    if nxt is not None:
        trn = transp(0, nxt["p"], nxt["gK"](), nxt["scs"], nxt["ld"]())
        st["pre_khv"] = khcopy(0, trn)


def emit_sinit(B, blk):
    @blk.vector
    def _(v):
        sp = B.spar
        v.tensor_scalar(B.S[:, (1 - sp) * 1024:(2 - sp) * 1024],
                        B.S[:, sp * 1024:(sp + 1) * 1024],
                        B.sflag[:, 0:1], None, ALU.mult)
        v.tensor_copy(B.Sbf[:, :], B.S[:, (1 - sp) * 1024:(2 - sp) * 1024])
    B.spar = 1 - B.spar
    B.barrier(blk)


def emit_yln(B, blk, pre_cb=None):
    P = B.pool
    y_t = B.h_y.rearrange("(t p) n -> p t n", p=128)
    ytl = P.alloc("ytl", NT * T, BF16)
    sqb = P.alloc("sqb", 2 * T, BF16)
    ylnb = P.alloc("ylnb", NT * T, BF16)
    srow = ylnb[0:1, 0:24576].bitcast(F32)
    abrow = P.alloc("abrow", 2 * T, F32)
    bcast = P.alloc("bcast", 2 * T, F32)
    scrf = P.alloc("scrf", T, F32)
    yls = B.dsem("yls")
    if pre_cb is not None:
        pre_cb()

    sS_done = [None, None]
    sT_done = [None, None]
    for k in range(NT):
        p = k % 2

        @blk.sync
        def _(sy, k=k, p=p, pv=yls.n, wt=sS_done[p], wtt=sT_done[p]):
            sy.wait_ge(yls.h, pv)
            if wtt is not None:
                sy.wait_ge(B.sT.h, wtt)
            sy.dma_start(out=ytl[:, k * T:(k + 1) * T], in_=y_t[:, k, :]) \
                .then_inc(yls.h, 16)
        yls.n += 16
        dx_t = yls.n

        @blk.scalar
        def _(s, k=k, p=p, dx_t=dx_t, wt=sT_done[p]):
            s.wait_ge(yls.h, dx_t)
            if wt is not None:
                s.wait_ge(B.sT.h, wt)
            s.activation(sqb[:, p * T:(p + 1) * T], ytl[:, k * T:(k + 1) * T],
                         AF.Square)
        sS_done[p] = B.sSq.n
        sv = B.sSq.n

        @blk.tensor
        def _(t, k=k, p=p, sv=sv):
            t.wait_ge(B.sSq.h, sv)
            for n in range(4):
                last = t.matmul(B.PA[0:1, n * 512:(n + 1) * 512], B.ones[:, :],
                         ytl[:, k * T + n * 512:k * T + (n + 1) * 512],
                         start=(k == 0), stop=(k == NT - 1))
                last = t.matmul(B.PB[0:1, n * 512:(n + 1) * 512], B.ones[:, :],
                         sqb[:, p * T + n * 512:p * T + (n + 1) * 512],
                         start=(k == 0), stop=(k == NT - 1))
            t.drain()
            t.nop(nofuse=True).then_inc(B.sT.h, 1)
        B.sT.n += 1
        sT_done[p] = B.sT.n

    emit_stats_tail(B, blk, srow, abrow, bcast)
    bc_v = B.sVq.n

    for k in range(NT):
        @blk.vector
        def _(v, k=k, bc_v=bc_v):
            v.wait_ge(B.sVq.h, bc_v)
            v.tensor_tensor(scrf[:, :], ytl[:, k * T:(k + 1) * T], bcast[:, 0:T],
                            ALU.mult)
            v.tensor_tensor(scrf[:, :], scrf[:, :], bcast[:, T:2 * T], ALU.add)
            v.tensor_scalar(ylnb[:, k * T:(k + 1) * T], scrf[:, :],
                            B.vecs[:, 3 * NT + k:3 * NT + k + 1],
                            B.vecs[:, 4 * NT + k:4 * NT + k + 1],
                            ALU.mult, ALU.add)

    B.barrier(blk)
    P.freeb("ytl", "sqb", "abrow", "bcast", "scrf")
    return ylnb


def emit_ffn_prefetch(B, blk):
    P = B.pool
    xnc = P.alloc_top("xnc", NT * 1024, BF16)
    guslab = P.alloc_top("guslab", 2 * NT * 256, BF16)
    xn_t = B.h_xn.rearrange("(t p) n -> p t n", p=128)
    ffx = B.dsem("ffx")

    @blk.sync
    def _(sy, ffx=ffx, pv=ffx.n):
        sy.wait_ge(ffx.h, pv)
        for k in range(NT):
            sy.dma_start(out=xnc[:, k * 1024:(k + 1) * 1024],
                         in_=xn_t[:, k, 1:1025]).then_inc(ffx.h, 16)
    ffx.n += NT * 16
    xq = ffx.n
    gsem = []
    for g in (0, 1):
        ffg = B.dsem(f"ffg{g}")

        @blk.gpsimd
        def _(gg, g=g, ffg=ffg, pv=ffg.n):
            gg.wait_ge(ffg.h, pv)
            gg.dma_start(out=guslab[:, g * NT * 256:(g + 1) * NT * 256],
                         in_=B.i_gut[g].rearrange("k p c -> p k c")).then_inc(ffg.h, 16)
        ffg.n += 16
        gsem.append((ffg, ffg.n))
    return dict(xq=xq, gsem=gsem)


def emit_ffn(B, blk, pre=None):
    P = B.pool
    hbuf = P.alloc("hbuf", 64 * 1024, BF16)
    xn_t = B.h_xn.rearrange("(t p) n -> p t n", p=128)
    s_t = B.h_s.rearrange("(t p) n -> p t n", p=128)
    x_t = B.i_xm.rearrange("(t p) n -> p t n", p=128)
    o_t = B.o_out.rearrange("(t p) n -> p t n", p=128)

    t_pv, v_pv, s_pv = B.sT.n, B.sVq.n, B.sSq.n
    spl_pv = [B.dsem("spl0").n, B.dsem("spl1").n]

    for tc in range(2):
        pre0 = pre is not None and tc == 0
        if pre0:
            xnc = P.view("xnc")
            guslab = P.view("guslab")
        else:
            xnc = P.alloc("xnc", NT * 1024, BF16)
            guslab = P.alloc("guslab", 2 * NT * 256, BF16)
        sgev = P.alloc("sgev", 2 * 1024, BF16)

        ffx = B.dsem("ffx")

        if pre0:
            xq = pre["xq"]
        else:
            @blk.sync
            def _(sy, tc=tc, ffx=ffx, pv=ffx.n, tw=t_pv, vw=v_pv, sw=s_pv,
                  spw=tuple(spl_pv)):
                sy.wait_ge(ffx.h, pv)
                sy.wait_ge(B.sT.h, tw)
                sy.wait_ge(B.sVq.h, vw)
                sy.wait_ge(B.sSq.h, sw)
                sy.wait_ge(B.dsem("spl0").h, spw[0])
                sy.wait_ge(B.dsem("spl1").h, spw[1])
                for k in range(NT):
                    sy.dma_start(out=xnc[:, k * 1024:(k + 1) * 1024],
                                 in_=xn_t[:, k, 1 + tc * 1024:1 + (tc + 1) * 1024]) \
                        .then_inc(ffx.h, 16)
            ffx.n += NT * 16
            xq = ffx.n

        sV_done = [None, None]
        sS_done = [None, None]
        for g in range(64):
            pg = g % 2
            PP = B.PA if pg == 0 else B.PB

            wss = B.dsem(f"wsl{pg}")

            if pre0 and g < 2:
                dws, dw_t = pre["gsem"][g]
            else:
                @blk.gpsimd
                def _(gg, g=g, pg=pg, wss=wss, pv=wss.n, wt=sV_done[pg],
                      tw=t_pv, vw=v_pv):
                    gg.wait_ge(wss.h, pv)
                    if g < 2:
                        gg.wait_ge(B.sT.h, tw)
                        gg.wait_ge(B.sVq.h, vw)
                    if wt is not None:
                        gg.wait_ge(B.sVq.h, wt)
                    gg.dma_start(out=guslab[:, pg * NT * 256:(pg + 1) * NT * 256],
                                 in_=B.i_gut[g].rearrange("k p c -> p k c")).then_inc(wss.h, 16)
                wss.n += 16
                dws, dw_t = wss, wss.n

            @blk.tensor
            def _(t, g=g, pg=pg, PP=PP, dws=dws, ffx=ffx, dw_t=dw_t, xq=xq,
                  wt=sV_done[pg], vw=v_pv, sw=s_pv):
                t.wait_ge(dws.h, dw_t)
                if g < 2:
                    t.wait_ge(ffx.h, xq)
                    t.wait_ge(B.sVq.h, vw)
                    t.wait_ge(B.sSq.h, sw)
                if wt is not None:
                    t.wait_ge(B.sVq.h, wt)
                for kk in range(NT):
                    for n in range(2):
                        last = t.matmul(PP[:, n * 512:(n + 1) * 512],
                                 guslab[:, pg * NT * 256 + kk * 256:pg * NT * 256 + kk * 256 + 128],
                                 xnc[:, kk * 1024 + n * 512:kk * 1024 + (n + 1) * 512],
                                 start=(kk == 0), stop=(kk == NT - 1))
                        last = t.matmul(PP[:, 1024 + n * 512:1024 + (n + 1) * 512],
                                 guslab[:, pg * NT * 256 + kk * 256 + 128:pg * NT * 256 + (kk + 1) * 256],
                                 xnc[:, kk * 1024 + n * 512:kk * 1024 + (n + 1) * 512],
                                 start=(kk == 0), stop=(kk == NT - 1))
                last.then_inc(B.sT.h, 1)
            B.sT.n += 1
            tv = B.sT.n

            @blk.scalar
            def _(s, pg=pg, PP=PP, tv=tv, wt=sS_done[pg]):
                s.wait_ge(B.sT.h, tv)
                if wt is not None:
                    s.wait_ge(B.sVq.h, wt)
                s.activation(sgev[:, pg * 1024:(pg + 1) * 1024], PP[:, 0:1024],
                             AF.Silu)
            sv = B.sSq.n

            @blk.vector
            def _(v, g=g, pg=pg, PP=PP, sv=sv):
                v.wait_ge(B.sSq.h, sv)
                v.tensor_tensor(hbuf[:, g * 1024:(g + 1) * 1024],
                                sgev[:, pg * 1024:(pg + 1) * 1024],
                                PP[:, 1024:2048], ALU.mult)
            sV_done[pg] = B.sVq.n
            sS_done[pg] = B.sVq.n

        t_pv, v_pv, s_pv = B.sT.n, B.sVq.n, B.sSq.n
        P.freeb("xnc", "guslab", "sgev")
        dwslab = P.alloc("dwslab", 2 * 64 * 128, BF16)
        grow = P.alloc("grow", 2 * 1024, BF16)
        srow_b = P.alloc("srow_b", 2 * 1024, BF16)
        xrow_b = P.alloc("xrow_b", 2 * 1024, F32)
        zrB = P.alloc("zrB", 2 * 1024, F32)

        sT_done = [None, None]
        sV_done2 = [None, None]
        out_done = [None, None]
        for m in range(NT):
            pm = m % 2
            PP = B.PA if pm == 0 else B.PB

            wss = B.dsem(f"wsl{pm}")
            ffs = B.dsem("ffs")

            @blk.gpsimd
            def _(gg, m=m, pm=pm, wss=wss, pv=wss.n, wt=sT_done[pm],
                  tw=t_pv, vw=v_pv):
                gg.wait_ge(wss.h, pv)
                if m < 2:
                    gg.wait_ge(B.sT.h, tw)
                    gg.wait_ge(B.sVq.h, vw)
                if wt is not None:
                    gg.wait_ge(B.sT.h, wt)
                gg.dma_start(out=dwslab[:, pm * 64 * 128:(pm + 1) * 64 * 128],
                             in_=B.i_dwt[m].rearrange("k p c -> p k c")).then_inc(wss.h, 16)
            wss.n += 16
            dw_t = wss.n

            @blk.sync
            def _(sy, m=m, tc=tc, pm=pm, ffs=ffs, pv=ffs.n, wt=sV_done2[pm],
                  tw=t_pv, vw=v_pv):
                sy.wait_ge(ffs.h, pv)
                if m < 2:
                    sy.wait_ge(B.sT.h, tw)
                    sy.wait_ge(B.sVq.h, vw)
                if wt is not None:
                    sy.wait_ge(B.sVq.h, wt)
                sy.dma_start(out=srow_b[:, pm * 1024:(pm + 1) * 1024],
                             in_=s_t[:, m, tc * 1024:(tc + 1) * 1024]) \
                    .then_inc(ffs.h, 16)
                sy.dma_start(out=xrow_b[:, pm * 1024:(pm + 1) * 1024],
                             in_=x_t[:, m, tc * 1024:(tc + 1) * 1024]) \
                    .then_inc(ffs.h, 16)
            ffs.n += 32
            ldq = ffs.n

            @blk.tensor
            def _(t, m=m, pm=pm, PP=PP, wss=wss, dw_t=dw_t, wt=sV_done2[pm],
                  vw=v_pv, sw=s_pv):
                t.wait_ge(wss.h, dw_t)
                if m < 2:
                    t.wait_ge(B.sVq.h, vw)
                    t.wait_ge(B.sSq.h, sw)
                if wt is not None:
                    t.wait_ge(B.sVq.h, wt)
                for kk in range(64):
                    for n in range(2):
                        last = t.matmul(PP[:, n * 512:(n + 1) * 512],
                                 dwslab[:, pm * 64 * 128 + kk * 128:pm * 64 * 128 + (kk + 1) * 128],
                                 hbuf[:, kk * 1024 + n * 512:kk * 1024 + (n + 1) * 512],
                                 start=(kk == 0), stop=(kk == 63))
                last.then_inc(B.sT.h, 1)
            B.sT.n += 1
            sT_done[pm] = B.sT.n
            tv = B.sT.n

            @blk.scalar
            def _(s, m=m, pm=pm, PP=PP, tv=tv, wt=sV_done2[pm], vw=v_pv):
                s.wait_ge(B.sT.h, tv)
                if m < 2:
                    s.wait_ge(B.sVq.h, vw)
                if wt is not None:
                    s.wait_ge(B.sVq.h, wt)
                s.activation(grow[:, pm * 1024:(pm + 1) * 1024], PP[:, 0:1024],
                             AF.Silu)
            sv = B.sSq.n

            @blk.vector
            def _(v, m=m, pm=pm, sv=sv, ffs=B.dsem("ffs"), ldq=ldq,
                  sps=B.dsem(f"spl{pm}"), wt=out_done[pm]):
                v.wait_ge(B.sSq.h, sv)
                v.wait_ge(ffs.h, ldq)
                if wt is not None:
                    v.wait_ge(sps.h, wt)
                v.tensor_tensor(zrB[:, pm * 1024:(pm + 1) * 1024],
                                grow[:, pm * 1024:(pm + 1) * 1024],
                                srow_b[:, pm * 1024:(pm + 1) * 1024], ALU.mult)
                v.tensor_tensor(zrB[:, pm * 1024:(pm + 1) * 1024],
                                zrB[:, pm * 1024:(pm + 1) * 1024],
                                xrow_b[:, pm * 1024:(pm + 1) * 1024], ALU.add)
            sV_done2[pm] = B.sVq.n
            vv = B.sVq.n

            sps = B.dsem(f"spl{pm}")

            @blk.scalar
            def _(s, m=m, tc=tc, pm=pm, sps=sps, vv=vv):
                s.wait_ge(B.sVq.h, vv)
                s.dma_start(out=o_t[:, m, tc * 1024:(tc + 1) * 1024],
                            in_=zrB[:, pm * 1024:(pm + 1) * 1024]).then_inc(sps.h, 16)
            sps.n += 16
            out_done[pm] = sps.n

        t_pv, v_pv, s_pv = B.sT.n, B.sVq.n, B.sSq.n
        spl_pv = [B.dsem("spl0").n, B.dsem("spl1").n]
        P.freeb("dwslab", "grow", "srow_b", "xrow_b", "zrB")

    P.freeb("hbuf")



def emit_tap(B, blk, name, src):
    if name not in B.tapt:
        return
    B.barrier(blk)

    tps = B.dsem("tap")

    @blk.sync
    def _(sy, tps=tps):
        sy.dma_start(out=B.tapt[name], in_=src).then_inc(tps.h, 16)
    tps.n += 16
    B.barrier(blk)


def build_kernel(taps=(), phases=99):
    B = KB(taps, phases)
    B.declare()
    B.allocate()
    with B.nc.Block() as raw_blk:
        blk = BlkProxy(raw_blk, B)
        emit_p0(B, blk)
        for prefix in (True, False):
            w_dma = emit_w12_prefetch(B, blk)
            emit_ln(B, blk, B.i_xp if prefix else B.i_xm, prefix,
                    p1_done=not prefix)
            td_dma = emit_mm(B, blk, w_dma)
            emit_mixer_r(B, blk, 3)
            B.pool.freeb("mmr3")
            emit_wpath_head(B, blk, td_dma)
            if not prefix:
                emit_mixer_r(B, blk, 0)
                B.pool.freeb("mmr0")
                emit_bigmm_cm(B, blk, B.i_qw, B.pool.view("xrb"), B.h_q,
                              bias_col=0)
            emit_mixer_r(B, blk, 1)
            B.pool.freeb("mmr1")
            emit_bigmm_cm(B, blk, B.i_kw, B.pool.view("xrb"), B.h_k, bias_col=1)
            emit_mixer_r(B, blk, 2)
            fr = ["mmr2", "w2sb", "xnres"]
            if prefix:
                fr.append("mmr0")
            B.pool.freeb(*fr)
            emit_vmm_tm(B, blk, B.pool.view("xrb"))
            B.pool.freeb("xrb")
            emit_wpath_tail(B, blk, prefix)
            if prefix:
                ln_tile = emit_ln_p1(B, blk, B.i_xm, alloc=True)
                emit_scan(B, blk, True, chunk_cb=ln_tile)
                emit_sinit(B, blk)
            else:
                emit_scan(B, blk, False)
        box = {}
        ylnb = emit_yln(B, blk)
        emit_bigmm_cm(B, blk, B.i_ow, ylnb, B.h_s, bias_col=None,
                      tail_cb=lambda: box.__setitem__(
                          "ffn", emit_ffn_prefetch(B, blk)))
        B.pool.freeb("ylnb")
        emit_ffn(B, blk, pre=box["ffn"])
        B.barrier(blk)
    B.close()
    return B


_BUILD_CACHE = {}


def stage_weights(i):
    f32 = np.float32

    def cc(a):
        return np.ascontiguousarray(a, dtype=f32)

    qw = cc(i["L0_attn_q_w"]).reshape(16, 128, 16, 128).transpose(2, 0, 1, 3)
    kw = cc(i["L0_attn_k_w"]).reshape(16, 128, 16, 128).transpose(2, 0, 1, 3)
    ow = cc(i["L0_attn_o_w"]).reshape(16, 128, 16, 128).transpose(2, 0, 1, 3)
    vw = cc(i["L0_attn_v_w"]).reshape(16, 128, 4, 512).transpose(2, 0, 1, 3)
    w1 = cc(i["L0_attn_maa_w1"])
    w2 = np.stack([np.concatenate([cc(i["L0_attn_maa_w2"])[r],
                                   cc(i["L0_attn_" + n])[None, :]], axis=0)
                   for r, n in enumerate(["maa_r", "maa_k", "maa_v", "maa_w"])])
    td1 = cc(i["L0_attn_td_w1"])
    td2 = np.concatenate([cc(i["L0_attn_td_w2"]),
                          cc(i["L0_attn_time_decay"])[None, :]], axis=0)
    wgT = cc(i["L0_ffn_w_gate_up"]).T
    gate = np.ascontiguousarray(wgT[:, :8192]).reshape(16, 128, 64, 128)
    up = np.ascontiguousarray(wgT[:, 8192:]).reshape(16, 128, 64, 128)
    gut = np.ascontiguousarray(np.concatenate([gate, up], axis=3).transpose(2, 0, 1, 3))
    wdT = cc(i["L0_ffn_w_down"]).T
    dwt = np.ascontiguousarray(wdT.reshape(64, 128, 16, 128).transpose(2, 0, 1, 3))

    def col(name):
        return cc(i[name]).reshape(16, 128).T

    vecs = np.ascontiguousarray(np.concatenate(
        [col("L0_attn_q_b"), col("L0_attn_k_b"), col("L0_attn_v_b"),
         col("L0_attn_ln_g"), col("L0_attn_ln_b"), col("L0_attn_maa_x")], axis=1))
    vbrow = np.ascontiguousarray(np.broadcast_to(cc(i["L0_attn_v_b"])[None, :],
                                                 (128, 2048)))
    utm = (np.arange(128)[:, None] <= np.arange(128)[None, :]).astype(f32)
    ident = np.eye(128, dtype=f32)
    cmask = np.ones((128, T), f32)
    cmask[:, 0::CH] = 0.0
    return dict(qw_t=np.ascontiguousarray(qw), kw_t=np.ascontiguousarray(kw),
                ow_t=np.ascontiguousarray(ow), vw_t=np.ascontiguousarray(vw),
                w1=w1, w2t=w2, td1=td1, td2t=td2, gut=gut, dwt=dwt,
                vecs=vecs, vbrow=vbrow, utm=utm, ident=ident, cmask=cmask)


def run_cores(inputs, taps=(), phases=99, cores=8):
    key = (tuple(sorted(taps)), phases)
    if key not in _BUILD_CACHE:
        _BUILD_CACHE[key] = build_kernel(taps, phases)
    B = _BUILD_CACHE[key]
    shared = stage_weights(inputs)
    x = np.asarray(inputs["L0_attn_x"], dtype=np.float32)
    in_maps = []
    for core in range(cores):
        b, half = core // 2, core % 2
        xT = np.ascontiguousarray(x[b].T)
        m = dict(shared)
        m["xm"] = np.ascontiguousarray(xT[:, half * T:(half + 1) * T])
        m["xp"] = np.ascontiguousarray(xT[:, 0:T])
        m["sflag"] = np.full((128, 1), float(half), np.float32)
        in_maps.append(m)
    res = run_bass_kernel_spmd(B.nc, in_maps, core_ids=list(range(cores)))
    return B, res


def kernel(**inputs):
    _, res = run_cores(inputs, (), 99, 8)
    x = np.asarray(inputs["L0_attn_x"])
    out = np.empty((4, 4096, 2048), np.float32)
    for core in range(8):
        b, half = core // 2, core % 2
        out[b, half * T:(half + 1) * T, :] = res.results[core]["out"].T
    return out
